# version 01
# baseline (speedup 1.0000x reference)
import contextlib
import math
from collections import defaultdict

import numpy as np
import ml_dtypes
import concourse.bass as bass
import concourse.mybir as mybir
from concourse.bass_utils import run_bass_kernel_spmd

F32 = mybir.dt.float32
BF16 = mybir.dt.bfloat16
AF = mybir.ActivationFunctionType
ALU = mybir.AluOpType
AX = mybir.AxisListType

PE_LIMIT = 110
ENG_LIMIT = 200
D = 1024
T = 2048
NC = D // 128
NT = T // 512


class Tile:
    __slots__ = ("base", "off", "cols")

    def __init__(self, base, off, cols):
        self.base = base
        self.off = off
        self.cols = cols

    def __getitem__(self, idx):
        ps, cs = idx
        a = cs.start if cs.start is not None else 0
        b = cs.stop if cs.stop is not None else self.cols
        assert 0 <= a <= b <= self.cols, (a, b, self.cols)
        return self.base[ps, self.off + a:self.off + b]


class Arena:
    ALIGN = 128

    def __init__(self, nbytes):
        self.free_list = [(0, nbytes)]
        self.named = {}
        self.hwm = 0

    def alloc(self, name, nbytes, top=False):
        nbytes = -(-nbytes // self.ALIGN) * self.ALIGN
        cands = [i for i, (off, ln) in enumerate(self.free_list) if ln >= nbytes]
        if top:
            cands.sort(key=lambda i: -(self.free_list[i][0] + self.free_list[i][1]))
        else:
            cands.sort(key=lambda i: (self.free_list[i][1], i))
        for i in cands[:1]:
            off, ln = self.free_list[i]
            if True:
                if top:
                    self.named[name] = (off + ln - nbytes, nbytes)
                    self.free_list[i] = (off, ln - nbytes)
                    if ln == nbytes:
                        self.free_list.pop(i)
                    return off + ln - nbytes
                self.named[name] = (off, nbytes)
                if ln == nbytes:
                    self.free_list.pop(i)
                else:
                    self.free_list[i] = (off + nbytes, ln - nbytes)
                self.hwm = max(self.hwm, off + nbytes)
                return off
        raise MemoryError(f"arena OOM allocating {name} ({nbytes}B); "
                          f"free={self.free_list} named={list(self.named)}")

    def free(self, name):
        off, nbytes = self.named.pop(name)
        self.free_list.append((off, nbytes))
        self.free_list.sort()
        out = []
        for off, ln in self.free_list:
            if out and out[-1][0] + out[-1][1] == off:
                out[-1] = (out[-1][0], out[-1][1] + ln)
            else:
                out.append((off, ln))
        self.free_list = out


class ChunkedX:

    def __init__(self, tiles):
        self.tiles = tiles

    def __getitem__(self, idx):
        ps, cs = idx
        a, b = cs.start or 0, cs.stop
        c = a // T
        assert b <= (c + 1) * T, (a, b)
        return self.tiles[c][ps, a - c * T: b - c * T]


class Prog:
    def __init__(self):
        self.nc = bass.Bass("TRN2", target_bir_lowering=False)
        self._blk = None
        self._cnt = defaultdict(int)
        self._sems = {}
        self.semv = defaultdict(int)
        self._sb = {}
        self._ps = {}
        self.dram = {}
        self.inmap = {}
        self.env = {}
        self._consts = {}

    def constf(self, val, dtype=F32):
        key = (dtype, float(val))
        if key not in self._consts:
            nm = "cstv%d" % len(self._consts)
            h = self.sbuf(nm, 1, dtype, top=True)
            self.v(lambda v: v.memset(h[:, :], val))
            self.cut()
            self.nc.const_aps.register(dtype, float(val), h[:, 0:1])
            self._consts[key] = h
        return self._consts[key]

    def din(self, name, arr):
        if name not in self.dram:
            dt = {np.dtype(np.float32): F32, np.dtype(ml_dtypes.bfloat16): BF16}[arr.dtype]
            t = self.nc.dram_tensor(name, list(arr.shape), dt, kind="ExternalInput")
            self.dram[name] = t.ap()
            self.inmap[name] = arr
        return self.dram[name]

    def dout(self, name, shape, dtype=F32):
        t = self.nc.dram_tensor(name, list(shape), dtype, kind="ExternalOutput")
        return t.ap()

    ARENA_BYTES = 204 * 1024

    def _arena(self):
        if not hasattr(self, "_arh"):
            h = self.nc.sbuf_tensor("arena", [128, self.ARENA_BYTES // 4], F32).__enter__()
            self._arh = {F32: h, BF16: h.bitcast(BF16)}
            self._aralloc = Arena(self.ARENA_BYTES)
        return self._aralloc

    def sbuf(self, name, cols, dtype, parts=128, top=False):
        assert name not in self._sb, name
        if parts != 128:
            cm = self.nc.sbuf_tensor(name, [parts, cols], dtype)
            h = cm.__enter__()
            self._sb[name] = "raw"
            return h
        isz = 4 if dtype == F32 else 2
        off = self._arena().alloc(name, cols * isz, top=top)
        self._sb[name] = "arena"
        return Tile(self._arh[dtype], off // isz, cols)

    def sfree(self, name):
        kind = self._sb.pop(name)
        if kind == "arena":
            self._aralloc.free(name)

    def psum(self, name, cols, dtype=F32, parts=128):
        cm = self.nc.psum_tensor(name, [parts, cols], dtype)
        h = cm.__enter__()
        self._ps[name] = (h, cm)
        return h

    def sem(self, name):
        if name not in self._sems:
            cm = self.nc.semaphore(name)
            self._sems[name] = (cm.__enter__(), cm)
        return self._sems[name][0]

    def bump(self, name, k):
        self.semv[name] += k
        return self.semv[name]

    def _ensure(self):
        if self._blk is None:
            self._blk = self.nc.Block()
            self._blk.__enter__()

    def cut(self):
        if self._blk is not None:
            self._blk.__exit__(None, None, None)
            self._blk = None
            self._cnt.clear()

    def _emit(self, eng, f, n):
        self._ensure()
        getattr(self._blk, eng)(f)
        self._cnt[eng] += n
        lim = PE_LIMIT if eng == "tensor" else ENG_LIMIT
        if self._cnt[eng] >= lim:
            self.cut()

    def t(self, f, n=1):
        self._emit("tensor", f, n)

    def v(self, f, n=1):
        self._emit("vector", f, n)

    def s(self, f, n=1):
        self._emit("scalar", f, n)

    def g(self, f, n=1):
        self._emit("gpsimd", f, n)

    def y(self, f, n=1):
        self._emit("sync", f, n)

    def dma(self, q, out, in_, semname, wait=None):
        s = self.sem(semname)
        c = self.bump(semname, 16)
        if wait is not None:
            wsem, wval = wait
            ws = self.sem(wsem)
            self._emit(q, lambda e: e.wait_ge(ws, wval), 1)
        self._emit(q, lambda e: e.dma_start(out=out, in_=in_).then_inc(s, 16), 1)
        return c

    def wait(self, eng, semname, val):
        s = self.sem(semname)
        self._emit(eng, lambda e: e.wait_ge(s, val), 1)

    def all_wait(self, semname, val):
        for eng in ("tensor", "vector", "scalar", "gpsimd", "sync"):
            self.wait(eng, semname, val)
        self.cut()

    def finish(self):
        self.cut()

    def dbg_dump(self, name, ap_f, cols, parts=128, dtype=F32):
        self.cut()
        o = self.dout("dbg_" + name, [parts, cols], dtype)
        self.dma("sync", o[:, :], ap_f, "dbgs")
        self.wait("sync", "dbgs", self.semv["dbgs"])
        self.cut()



def _fm_vec(v):
    v = np.asarray(v, np.float32)
    return np.ascontiguousarray(v.reshape(-1, 128).T)


def _io(W):
    return np.ascontiguousarray(np.asarray(W, np.float32))


def _oi(W):
    return np.ascontiguousarray(np.asarray(W, np.float32).T)


def make_consts():
    c = {}
    c["ident_f32"] = np.eye(128, dtype=np.float32)
    c["tril_f32"] = np.tril(np.ones((128, 128), dtype=np.float32)).T.copy()
    c["ones_col_f32"] = np.ones((128, 1), dtype=np.float32)
    c["ones_col_b16"] = np.ones((128, 1), dtype=ml_dtypes.bfloat16)
    c["ones_row_f32"] = np.ones((1, 128), dtype=np.float32)
    sel = np.zeros((16, 16 * 128), dtype=np.float32)
    for e in range(16):
        sel[e, e * 128:(e + 1) * 128] = 1.0
    c["sel16_f32"] = sel
    blk = np.zeros((128, 4), dtype=ml_dtypes.bfloat16)
    for g in range(4):
        blk[g * 32:(g + 1) * 32, g] = 1.0
    c["blk32_b16"] = blk
    i = np.arange(16, dtype=np.float32)
    inv_freq = 1.0 / 10000.0 ** (i / 256.0)
    t = np.arange(2048, dtype=np.float32)
    freqs = np.outer(t, inv_freq)
    cos = np.cos(freqs).astype(ml_dtypes.bfloat16)
    sin = np.sin(freqs).astype(ml_dtypes.bfloat16)
    p_idx = np.arange(128)
    j = p_idx % 32
    d_idx = j % 16
    sgn = np.where(j < 16, -1.0, 1.0).astype(np.float32)
    CK = cos.astype(np.float32).T[d_idx]
    SK = sin.astype(np.float32).T[d_idx] * sgn[:, None]
    c["rope"] = np.concatenate(
        [(CK * 0.125), (SK * 0.125), CK, SK], axis=1).astype(ml_dtypes.bfloat16)
    return c


def pack_all(inputs):
    A = {k: np.asarray(v) for k, v in inputs.items()}
    p = {}
    p.update({"c_" + k: v for k, v in make_consts().items()})
    p["L2w_wq"] = _io(A["L2_attn_Wq"])
    p["L2w_wk"] = _io(A["L2_attn_Wk"])
    p["L2w_wv"] = _io(A["L2_attn_Wv"])
    p["L2w_wo"] = _io(A["L2_attn_Wo"])
    p["L2w_ffw"] = _oi(A["L2_ffn_weight"])
    p["L2w_ffb"] = _fm_vec(A["L2_ffn_bias"])
    p["L2w_alpha"] = np.asarray(A["L2_ffn_alpha"], np.float32).reshape(1, 1)
    p["L1w_w1"] = _oi(A["L1_ffn_w1_w"])
    p["L1w_w2"] = _oi(A["L1_ffn_w2_w"])
    p["L1w_h1"] = _oi(A["L1_ffn_h1_w"])
    p["L1w_h2"] = _oi(A["L1_ffn_h2_w"])
    p["L1w_w3"] = _oi(A["L1_ffn_w3_w"])
    p["L1w_s1"] = _fm_vec(A["L1_ffn_w1_s"])
    p["L1w_s2"] = _fm_vec(A["L1_ffn_w2_s"])
    p["L1w_sh1"] = _fm_vec(A["L1_ffn_h1_s"])
    p["L1w_sh2"] = _fm_vec(A["L1_ffn_h2_s"])
    p["L1w_s3"] = _fm_vec(A["L1_ffn_w3_s"])
    p["L3w_rw"] = _io(A["L3_ffn_router_w"])
    p["L3w_rb"] = np.asarray(A["L3_ffn_router_b"], np.float32).reshape(1, 16)
    for ex in range(16):
        p[f"L3w_w1_{ex}"] = _io(A["L3_ffn_w1"][ex])
        p[f"L3w_w2_{ex}"] = _io(A["L3_ffn_w2"][ex])
    p["L3w_b1"] = np.concatenate([_fm_vec(A["L3_ffn_b1"][ex]) for ex in range(16)], axis=1)
    p["L3w_b2"] = np.ascontiguousarray(np.asarray(A["L3_ffn_b2"], np.float32))
    p["L1a_wq"] = _io(A["L1_attn_Wq"])
    p["L1a_wl"] = _io(A["L1_attn_Wl"])
    p["L1a_wk"] = _io(A["L1_attn_Wk"])
    p["L1a_wv"] = _io(A["L1_attn_Wv"])
    p["L1a_wo"] = _io(A["L1_attn_Wo"])
    p["L1a_bias"] = np.concatenate(
        [_fm_vec(A["L1_attn_bq"]), _fm_vec(A["L1_attn_bl"]), _fm_vec(A["L1_attn_bk"]),
         _fm_vec(A["L1_attn_bo"]), _fm_vec(A["L1_attn_bv"])], axis=1)
    p["L0f_sg"] = _oi(A["L0_ffn_sg"])
    p["L0f_su"] = _oi(A["L0_ffn_su"])
    p["L0f_sd"] = _oi(A["L0_ffn_sd"])
    for ex in range(7):
        p[f"L0f_rg_{ex}"] = _oi(A["L0_ffn_rg"][ex])
        p[f"L0f_ru_{ex}"] = _oi(A["L0_ffn_ru"][ex])
        p[f"L0f_rd_{ex}"] = _oi(A["L0_ffn_rd"][ex])
    p["L0f_wr"] = _oi(A["L0_ffn_wr"])
    p["L0f_rb"] = np.asarray(A["L0_ffn_rb"], np.float32).reshape(1, 7)
    p["L0a_wq"] = _io(np.asarray(A["L0_attn_Wq"], np.float32).T)
    p["L0a_wk"] = _io(np.asarray(A["L0_attn_Wk"], np.float32).T)
    p["L0a_wv"] = _oi(A["L0_attn_Wv"])
    wpj = _oi(A["L0_attn_Wproj"])
    il = np.empty_like(wpj)
    for k in range(8):
        il[:, (2 * k) * 128:(2 * k + 1) * 128] = wpj[:, k * 128:(k + 1) * 128]
        il[:, (2 * k + 1) * 128:(2 * k + 2) * 128] = wpj[:, (8 + k) * 128:(9 + k) * 128]
    p["L0a_wproj"] = np.ascontiguousarray(il)
    p["L0a_lam"] = np.concatenate(
        [np.asarray(A[f"L0_attn_lambda_{n}"], np.float32) for n in ("q1", "q2", "k1", "k2")]).reshape(1, 256)
    p["L0a_alpha"] = np.asarray(A["L0_attn_alpha"], np.float32).reshape(1, 1)
    wqkv = _oi(A["L3_attn_w_qkv"])
    p["L3a_wqk"] = np.ascontiguousarray(wqkv[:, 0:2048])
    p["L3a_wv"] = np.ascontiguousarray(wqkv[:, 2048:3072])
    p["L3a_wo"] = _oi(A["L3_attn_w_o"])
    bqkv = np.asarray(A["L3_attn_b_qkv"], np.float32)
    p["L3a_bias"] = np.concatenate(
        [_fm_vec(bqkv[0:1024]), _fm_vec(bqkv[1024:2048]), _fm_vec(bqkv[2048:3072]),
         _fm_vec(A["L3_attn_b_o"])], axis=1)
    return p



def setup_env(P, packed):
    kr = P.nc._kernel_sem_range
    base = kr.start + 2
    if P.nc._bir_kernel_barrier_sem is not None:
        base += 1
    base += len(P.nc._monotonic_sems)
    rng = range(base, kr.stop)
    P.g(lambda g: g.dma_reset(rng))
    P.g(lambda g: g.sem_clear(rng))
    P.cut()
    e = P.env
    e["X"] = ChunkedX([P.sbuf(f"X{c}", T, F32) for c in range(NC)])
    e["XH"] = P.sbuf("XH", 16384, BF16)
    e["rows"] = P.sbuf("rows", 4096, F32)
    e["bc0"] = P.sbuf("bc0", 2048, F32)
    e["bc1"] = P.sbuf("bc1", 2048, F32)
    e["nscr"] = P.sbuf("nscr", 2048, F32)
    e["psA"] = P.psum("psA", 2048, F32)
    e["psB"] = P.psum("psB", 2048, F32)
    e["ident"] = P.sbuf("ident", 128, F32)
    e["tril"] = P.sbuf("tril", 128, F32)
    e["ones_c32"] = P.sbuf("ones_c32", 1, F32)
    e["ones_cb"] = P.sbuf("ones_cb", 1, BF16)
    e["ones_r32"] = P.sbuf("ones_r32", 128, F32, parts=1)
    P.dma("sync", e["ident"][:, :], P.din("c_ident_f32", packed["c_ident_f32"])[:, :], "cst")
    P.dma("sync", e["tril"][:, :], P.din("c_tril_f32", packed["c_tril_f32"])[:, :], "cst")
    P.dma("sync", e["ones_c32"][:, :], P.din("c_ones_col_f32", packed["c_ones_col_f32"])[:, :], "cst")
    P.dma("sync", e["ones_cb"][:, :], P.din("c_ones_col_b16", packed["c_ones_col_b16"])[:, :], "cst")
    P.dma("sync", e["ones_r32"][:, :], P.din("c_ones_row_f32", packed["c_ones_row_f32"])[:, :], "cst")
    xin = P.dout("xout", [128, 16384], F32)
    e["xout"] = xin
    xdram = P.din("xin", packed["xin0"])
    for c in range(NC):
        P.dma("sync", e["X"][:, c * T:(c + 1) * T], xdram[:, c * T:(c + 1) * T], "cst")
    P.all_wait("cst", P.semv["cst"])


def store_x(P):
    e = P.env
    P.cut()
    for c in range(NC):
        P.dma("sync", e["xout"][:, c * T:(c + 1) * T], P.env["X"][:, c * T:(c + 1) * T], "xst")
    P.wait("sync", "xst", P.semv["xst"])
    P.cut()


def park_x(P):
    e = P.env
    if "xpark" not in e:
        e["xpark"] = P.nc.dram_tensor("xpark", [128, 16384], F32).ap()
    P.cut()
    for c in range(NC):
        P.dma("sync", e["xpark"][:, c * T:(c + 1) * T], e["X"][:, c * T:(c + 1) * T], "xpk")
    P.all_wait("xpk", P.semv["xpk"])
    for c in range(NC):
        P.sfree(f"X{c}")
    e["X"] = None


def unpark_x(P):
    e = P.env
    e["X"] = ChunkedX([P.sbuf(f"X{c}", T, F32) for c in range(NC)])
    P.cut()
    for c in range(NC):
        P.dma("sync", e["X"][:, c * T:(c + 1) * T], e["xpark"][:, c * T:(c + 1) * T], "xup")
    P.all_wait("xup", P.semv["xup"])


def free_xh(P):
    P.sfree("XH")
    P.env["XH"] = None


def emit_ln_v2(P):
    e = P.env
    if e.get("XH") is None:
        e["XH"] = P.sbuf("XH", 16384, BF16)
    X, XH = e["X"], e["XH"]
    rows, psA, psB = e["rows"], e["psA"], e["psB"]
    XB = P.sbuf("XB", 16384, BF16)
    for c in range(NC):
        P.v(lambda v, c=c: v.tensor_copy(XB[:, c * T:(c + 1) * T], X[:, c * T:(c + 1) * T]))
    P.cut()
    for c in range(NC):
        for j in range(NT):
            P.t(lambda t, c=c, j=j: t.matmul(
                psA[0:1, j * 512:(j + 1) * 512],
                e["ones_cb"][:, :],
                XB[:, c * T + j * 512: c * T + (j + 1) * 512],
                start=(c == 0), stop=(c == NC - 1)))
    P.cut()
    for c in range(NC):
        P.s(lambda s, c=c: s.activation(XB[:, c * T:(c + 1) * T], XB[:, c * T:(c + 1) * T], AF.Square))
    P.cut()
    for c in range(NC):
        for j in range(NT):
            P.t(lambda t, c=c, j=j: t.matmul(
                psB[0:1, j * 512:(j + 1) * 512],
                e["ones_cb"][:, :],
                XB[:, c * T + j * 512: c * T + (j + 1) * 512],
                start=(c == 0), stop=(c == NC - 1)))
    P.cut()
    mu = rows[0:1, 0:T]
    w = rows[0:1, T:2 * T]
    P.s(lambda s: s.activation(mu, psA[0:1, 0:T], AF.Copy, scale=1.0 / 1024.0))
    P.cut()
    P.v(lambda v: v.scalar_tensor_tensor(w, mu, -1.0, mu, ALU.mult, ALU.mult))
    P.cut()
    P.v(lambda v: v.scalar_tensor_tensor(w, psB[0:1, 0:T], 1.0 / 1024.0, w, ALU.mult, ALU.add))
    P.cut()
    P.constf(1e-5)
    P.s(lambda s: s.activation(w, w, AF.Sqrt, bias=1e-5))
    P.cut()
    P.v(lambda v: v.reciprocal(w, w))
    P.cut()
    P.v(lambda v: v.scalar_tensor_tensor(mu, mu, -1.0, w, ALU.mult, ALU.mult))
    P.cut()
    for j in range(NT):
        P.t(lambda t, j=j: t.matmul(psA[:, j * 512:(j + 1) * 512], e["ones_r32"][:, :],
                                    w[:, j * 512:(j + 1) * 512], start=True, stop=True))
        P.t(lambda t, j=j: t.matmul(psB[:, j * 512:(j + 1) * 512], e["ones_r32"][:, :],
                                    mu[:, j * 512:(j + 1) * 512], start=True, stop=True))
    P.cut()
    P.v(lambda v: v.tensor_copy(e["bc0"][:, :], psA[:, :]))
    P.v(lambda v: v.tensor_copy(e["bc1"][:, :], psB[:, :]))
    P.cut()
    for c in range(NC):
        P.v(lambda v, c=c: v.tensor_tensor(e["nscr"][:, :], X[:, c * T:(c + 1) * T], e["bc0"][:, :], ALU.mult))
        P.v(lambda v, c=c: v.tensor_tensor(XH[:, c * T:(c + 1) * T], e["nscr"][:, :], e["bc1"][:, :], ALU.add))
    P.cut()
    P.sfree("XB")


class WFull:

    def __init__(self, P, name, arr, sem=None):
        self.P = P
        self.I, self.O = arr.shape
        self.name = name
        self.sem = sem or ("w_" + name)
        self.dram = P.din("W_" + name, arr)
        self.sb = P.sbuf("wsb_" + name, (self.I // 128) * self.O, BF16)
        for ic in range(self.I // 128):
            P.dma("gpsimd", self.sb[:, ic * self.O:(ic + 1) * self.O],
                  self.dram[ic * 128:(ic + 1) * 128, :], self.sem)
        self.ready = P.semv[self.sem]

    def lhsT(self, ic, o0, owidth=128):
        return self.sb[:, ic * self.O + o0: ic * self.O + o0 + owidth]

    def free(self):
        self.P.sfree("wsb_" + self.name)


def mm_fm(P, W, rhs_sb, rhs_base, n_i, n_o, evac, tcols=T, wait_sem=None, wait_val=None):
    e = P.env
    nt = tcols // 512
    for oc in range(n_o + 1):
        if oc < n_o:
            ps = e["psA"] if oc % 2 == 0 else e["psB"]
            if oc == 0 and wait_sem is not None:
                P.wait("tensor", wait_sem, wait_val)
            for ic in range(n_i):
                for j in range(nt):
                    P.t(lambda t, oc=oc, ic=ic, j=j, ps=ps: t.matmul(
                        ps[:, j * 512:(j + 1) * 512],
                        W.lhsT(ic, oc * 128),
                        rhs_sb[:, rhs_base + ic * tcols + j * 512: rhs_base + ic * tcols + (j + 1) * 512],
                        start=(ic == 0), stop=(ic == n_i - 1)))
        if oc > 0:
            psp = e["psA"] if (oc - 1) % 2 == 0 else e["psB"]
            evac(oc - 1, psp)
        P.cut()


def mm_stream(P, name, arr, rhs_sb, rhs_off, rhs_stride, tcols, evac, PW=512,
              psum_cols=None):
    e = P.env
    I, O = arr.shape
    assert O % PW == 0, (O, PW)
    nic, nop, opp = I // 128, O // 128, PW // 128
    npan = O // PW
    bname = ["wp_0", "wp_1"]
    bufs = [P.sbuf(bname[s], nic * PW, BF16) for s in (0, 1)]
    sems = ["ws_0", "ws_1"]
    dram = P.din("W_" + name, arr)
    panel_sem = {}
    pe_done = {}

    def issue(p):
        s = p % 2
        if p >= 2:
            P.wait("gpsimd", "pep", pe_done[p - 2])
        for ic in range(nic):
            panel_sem[p] = P.dma("gpsimd", bufs[s][:, ic * PW:(ic + 1) * PW],
                                 dram[ic * 128:(ic + 1) * 128, p * PW:(p + 1) * PW],
                                 sems[s])
    issue(0)
    if npan > 1:
        issue(1)
    nt = tcols // 512
    pcols = psum_cols if psum_cols is not None else tcols
    pep = P.sem("pep")
    for oc in range(nop + 1):
        if oc < nop:
            p, s, o0 = oc // opp, (oc // opp) % 2, (oc % opp) * 128
            ps = e["psA"] if oc % 2 == 0 else e["psB"]
            if oc % opp == 0:
                P.wait("tensor", sems[s], panel_sem[p])
            for ic in range(nic):
                for j in range(nt):
                    last = (ic == nic - 1 and j == nt - 1 and
                            (oc % opp == opp - 1 or oc == nop - 1))

                    def mk(oc=oc, ic=ic, j=j, ps=ps, s=s, o0=o0, last=last):
                        def f(t):
                            if callable(rhs_sb):
                                rhs = rhs_sb(ic, j)
                            else:
                                rhs = rhs_sb[:, rhs_off + ic * rhs_stride + j * 512:
                                             rhs_off + ic * rhs_stride + min((j + 1) * 512, tcols)]
                            ins = t.matmul(
                                ps[:, j * 512:min((j + 1) * 512, pcols)],
                                bufs[s][:, ic * PW + o0: ic * PW + o0 + 128],
                                rhs,
                                start=(ic == 0), stop=(ic == nic - 1))
                            if last:
                                ins.then_inc(pep, 1)
                        return f
                    P.t(mk())
            if oc % opp == opp - 1 or oc == nop - 1:
                pe_done[p] = P.bump("pep", 1)
                if p + 2 < npan:
                    issue(p + 2)
        if oc > 0:
            psp = e["psA"] if (oc - 1) % 2 == 0 else e["psB"]
            evac(oc - 1, psp)
        P.cut()
    P.sfree(bname[0])
    P.sfree(bname[1])


def emit_l1_ffn(P, packed):
    e = P.env
    emit_ln_v2(P)
    park_x(P)
    XH = e["XH"]
    if "l1scales" not in P._sb:
        sc = P.sbuf("l1scales", 104, F32, top=True)
        sh2 = P.sbuf("l1sh2", 32, F32, top=True)
        e["l1scales"], e["l1sh2"] = sc, sh2
        P.dma("sync", sc[:, 0:32], P.din("L1w_s1", packed["L1w_s1"])[:, :], "l1sc")
        P.dma("sync", sc[:, 32:64], P.din("L1w_s2", packed["L1w_s2"])[:, :], "l1sc")
        P.dma("sync", sc[:, 64:96], P.din("L1w_sh1", packed["L1w_sh1"])[:, :], "l1sc")
        P.dma("sync", sc[:, 96:104], P.din("L1w_s3", packed["L1w_s3"])[:, :], "l1sc")
        P.dma("sync", sh2[:, :], P.din("L1w_sh2", packed["L1w_sh2"])[:, :], "l1sc")
        P.wait("vector", "l1sc", P.semv["l1sc"])
        P.wait("scalar", "l1sc", P.semv["l1sc"])
        P.cut()
    sc, sh2 = e["l1scales"], e["l1sh2"]
    t1 = P.sbuf("l1t1", 16384, BF16)
    t2 = P.sbuf("l1t2", 16384, BF16)
    yv = P.sbuf("l1y", 4096, F32)
    TQ = 512
    for qq in range(4):
        def ev1(oc, ps):
            P.s(lambda s, oc=oc: s.activation(t1[:, oc * TQ:(oc + 1) * TQ], ps[:, 0:TQ],
                                              AF.Silu, scale=sc[:, oc:oc + 1]))
        mm_stream(P, "l1w1", packed["L1w_w1"], XH, qq * TQ, T, TQ, ev1, PW=512)
        def ev2(oc, ps):
            P.v(lambda v, oc=oc: v.tensor_scalar(e["nscr"][:, 0:TQ], ps[:, 0:TQ],
                                                 sc[:, 32 + oc:33 + oc], None, ALU.mult))
            P.v(lambda v, oc=oc: v.tensor_tensor(t1[:, oc * TQ:(oc + 1) * TQ],
                                                 e["nscr"][:, 0:TQ],
                                                 t1[:, oc * TQ:(oc + 1) * TQ], ALU.mult))
        mm_stream(P, "l1w2", packed["L1w_w2"], XH, qq * TQ, T, TQ, ev2, PW=512)
        def ev3(oc, ps):
            P.s(lambda s, oc=oc: s.activation(t2[:, oc * TQ:(oc + 1) * TQ], ps[:, 0:TQ],
                                              AF.Gelu, scale=sc[:, 64 + oc:65 + oc]))
        mm_stream(P, "l1h1", packed["L1w_h1"], t1, 0, TQ, TQ, ev3, PW=128)
        def ev4(oc, ps):
            P.s(lambda s, oc=oc: s.activation(t1[:, oc * TQ:(oc + 1) * TQ], ps[:, 0:TQ],
                                              AF.Gelu, scale=sh2[:, oc:oc + 1]))
        mm_stream(P, "l1h2", packed["L1w_h2"], t2, 0, TQ, TQ, ev4, PW=128)
        def ev5(oc, ps):
            P.v(lambda v, oc=oc: v.tensor_scalar(yv[:, oc * TQ:(oc + 1) * TQ], ps[:, 0:TQ],
                                                 sc[:, 96 + oc:97 + oc], None, ALU.mult))
        mm_stream(P, "l1w3", packed["L1w_w3"], t1, 0, TQ, TQ, ev5, PW=128)
        P.cut()
        for c in range(NC):
            sl = e["nscr"][:, (c % 2) * 512:(c % 2) * 512 + 512]
            v1 = P.dma("sync", sl, e["xpark"][:, c * T + qq * TQ: c * T + qq * TQ + TQ], "xq")
            P.wait("vector", "xq", v1)
            P.v(lambda v, c=c, sl=sl: v.scalar_tensor_tensor(
                sl, yv[:, c * TQ:(c + 1) * TQ], 0.5, sl, ALU.mult, ALU.add))
            P.cut()
            P.dma("sync", e["xpark"][:, c * T + qq * TQ: c * T + qq * TQ + TQ], sl, "xqs")
        P.wait("sync", "xqs", P.semv["xqs"])
        P.cut()
    P.sfree("l1t1")
    P.sfree("l1t2")
    P.sfree("l1y")
    unpark_x(P)


def emit_l3_ffn(P, packed):
    e = P.env
    psA, psB, rows = P.env["psA"], P.env["psB"], P.env["rows"]
    emit_ln_v2(P)
    park_x(P)
    XH = e["XH"]
    if "l3rw" not in P._sb:
        rw = P.sbuf("l3rw", 128, BF16, top=True)
        rbr = P.sbuf("l3rb", 16, F32, top=True)
        b1c = P.sbuf("l3b1", 256, F32, top=True)
        b2c = P.sbuf("l3b2", 1024, BF16, top=True)
        e["l3rw"], e["l3rb"], e["l3b1"], e["l3b2"] = rw, rbr, b1c, b2c
        P.dma("gpsimd", rw[:, :], P.din("L3w_rw", packed["L3w_rw"]).rearrange("(c p) e -> p c e", p=128), "l3c")
        P.dma("sync", rbr[0:1, :], P.din("L3w_rb", packed["L3w_rb"])[:, :], "l3c")
        P.dma("sync", b1c[:, :], P.din("L3w_b1", packed["L3w_b1"])[:, :], "l3c")
        P.dma("gpsimd", b2c[0:16, :], P.din("L3w_b2", packed["L3w_b2"])[:, :], "l3c")
        P.all_wait("l3c", P.semv["l3c"])
    rw, rbr, b1c, b2c = e["l3rw"], e["l3rb"], e["l3b1"], e["l3b2"]
    sel = ensure_sel(P, packed)

    logit = P.sbuf("l3lg", 256, F32)
    P.t(lambda t: t.matmul(psB[:, 32:48], e["ones_r32"][:, :], rbr[0:1, :], start=True, stop=True))
    P.cut()
    P.v(lambda v: v.tensor_copy(e["bc1"][:, 0:16], psB[:, 32:48]))
    P.cut()
    for tt in range(17):
        if tt < 16:
            ps = psA if tt % 2 == 0 else psB
            for ic in range(NC):
                P.t(lambda t, tt=tt, ic=ic, ps=ps: t.matmul(
                    ps[:, 0:16],
                    XH[:, ic * T + tt * 128: ic * T + tt * 128 + 128],
                    rw[:, ic * 16:(ic + 1) * 16],
                    start=(ic == 0), stop=(ic == NC - 1)))
        if tt > 0:
            psp = psA if (tt - 1) % 2 == 0 else psB
            P.v(lambda v, tt=tt, psp=psp: v.tensor_tensor(
                logit[:, (tt - 1) * 16:tt * 16], psp[:, 0:16], e["bc1"][:, 0:16], ALU.add))
        P.cut()

    pbuf = P.sbuf("l3p", 256, F32)
    mcol = P.sbuf("l3m", 16, F32)
    dsum = P.sbuf("l3d", 16, F32)
    keep = P.sbuf("l3keep", 256, F32)
    for tt in range(16):
        P.v(lambda v, tt=tt: v.tensor_reduce(mcol[:, tt:tt + 1], logit[:, tt * 16:(tt + 1) * 16],
                                             AX.X, ALU.max, negate=True))
    P.cut()
    for tt in range(16):
        P.s(lambda s, tt=tt: s.activation(pbuf[:, tt * 16:(tt + 1) * 16],
                                          logit[:, tt * 16:(tt + 1) * 16],
                                          AF.Exp, bias=mcol[:, tt:tt + 1]))
    P.cut()
    P.v(lambda v: v.tensor_copy(logit[:, :], pbuf[:, :]))
    P.cut()
    for step in range(4):
        for tt in range(16):
            P.v(lambda v, tt=tt: v.tensor_reduce(mcol[:, tt:tt + 1], logit[:, tt * 16:(tt + 1) * 16],
                                                 AX.X, ALU.max))
        P.cut()
        if step == 0:
            P.v(lambda v: v.tensor_copy(dsum[:, :], mcol[:, :]))
        else:
            P.v(lambda v: v.tensor_tensor(dsum[:, :], dsum[:, :], mcol[:, :], ALU.add))
        if step < 3:
            for tt in range(16):
                P.v(lambda v, tt=tt: v.tensor_scalar(keep[:, tt * 16:(tt + 1) * 16],
                                                     logit[:, tt * 16:(tt + 1) * 16],
                                                     mcol[:, tt:tt + 1], None, ALU.is_lt))
            for tt in range(16):
                P.v(lambda v, tt=tt: v.tensor_tensor(logit[:, tt * 16:(tt + 1) * 16],
                                                     logit[:, tt * 16:(tt + 1) * 16],
                                                     keep[:, tt * 16:(tt + 1) * 16], ALU.mult))
        P.cut()
    for tt in range(16):
        P.v(lambda v, tt=tt: v.tensor_scalar(keep[:, tt * 16:(tt + 1) * 16],
                                             pbuf[:, tt * 16:(tt + 1) * 16],
                                             mcol[:, tt:tt + 1], None, ALU.is_ge))
    P.v(lambda v: v.reciprocal(dsum[:, :], dsum[:, :]))
    P.cut()
    for tt in range(16):
        P.v(lambda v, tt=tt: v.tensor_scalar(pbuf[:, tt * 16:(tt + 1) * 16],
                                             pbuf[:, tt * 16:(tt + 1) * 16],
                                             dsum[:, tt:tt + 1], None, ALU.mult))
    P.cut()
    for tt in range(16):
        P.v(lambda v, tt=tt: v.tensor_tensor(pbuf[:, tt * 16:(tt + 1) * 16],
                                             pbuf[:, tt * 16:(tt + 1) * 16],
                                             keep[:, tt * 16:(tt + 1) * 16], ALU.mult))
    P.cut()
    wrow = P.sbuf("l3wr", 2048, F32)
    wrowb = P.sbuf("l3wrb", 2048, BF16)
    for tt in range(16):
        P.t(lambda t, tt=tt: t.transpose(psA[0:16, tt * 128:(tt + 1) * 128],
                                         pbuf[:, tt * 16:(tt + 1) * 16], e["ident"][:, :]))
    P.cut()
    P.v(lambda v: v.tensor_copy(wrow[0:16, :], psA[0:16, 0:2048]))
    P.v(lambda v: v.tensor_copy(wrowb[0:16, :], psA[0:16, 0:2048]))
    P.cut()
    P.sfree("l3lg")
    P.sfree("l3p")
    P.sfree("l3m")
    P.sfree("l3d")
    P.sfree("l3keep")

    He = P.sbuf("l3he", 16384, BF16)
    ACC = P.sbuf("l3acc", 8192, F32)
    TH = 1024
    for half in range(2):
        q0 = half * TH
        for ex in range(16):
            P.t(lambda t, ex=ex: t.matmul(psB[:, 1024:1536], sel[0:16, ex * 128:(ex + 1) * 128],
                                          wrow[0:16, q0:q0 + 512], start=True, stop=True))
            P.t(lambda t, ex=ex: t.matmul(psB[:, 1536:2048], sel[0:16, ex * 128:(ex + 1) * 128],
                                          wrow[0:16, q0 + 512:q0 + TH], start=True, stop=True))
            P.cut()
            P.v(lambda v: v.tensor_copy(e["bc0"][:, 0:TH], psB[:, 1024:2048]))
            P.cut()

            def ev1(oc, ps, ex=ex):
                P.s(lambda s, oc=oc, ex=ex, ps=ps: s.activation(
                    He[:, oc * TH:(oc + 1) * TH], ps[:, 0:TH], AF.Gelu,
                    bias=b1c[:, ex * 16 + oc:ex * 16 + oc + 1]))
            mm_stream(P, f"l3w1_{ex}", packed[f"L3w_w1_{ex}"], XH, q0, T, TH, ev1, PW=512)

            def ev2(oc, ps, ex=ex):
                P.v(lambda v, oc=oc, ps=ps: v.tensor_tensor(
                    e["nscr"][:, 0:TH], ps[:, 0:TH], e["bc0"][:, 0:TH], ALU.mult))
                if ex == 0:
                    P.v(lambda v, oc=oc: v.tensor_copy(ACC[:, oc * TH:(oc + 1) * TH],
                                                       e["nscr"][:, 0:TH]))
                else:
                    P.v(lambda v, oc=oc: v.tensor_tensor(ACC[:, oc * TH:(oc + 1) * TH],
                                                         ACC[:, oc * TH:(oc + 1) * TH],
                                                         e["nscr"][:, 0:TH], ALU.add))
            mm_stream(P, f"l3w2_{ex}", packed[f"L3w_w2_{ex}"], He, 0, TH, TH, ev2, PW=512)
        for oc in range(NC):
            ps = psA if oc % 2 == 0 else psB
            for j in range(2):
                P.t(lambda t, oc=oc, j=j, ps=ps: t.matmul(
                    ps[:, j * 512:(j + 1) * 512],
                    b2c[0:16, oc * 128:(oc + 1) * 128],
                    wrowb[0:16, q0 + j * 512:q0 + (j + 1) * 512],
                    start=True, stop=True))
            P.cut()
            P.v(lambda v, oc=oc, ps=ps: v.tensor_tensor(
                ACC[:, oc * TH:(oc + 1) * TH], ACC[:, oc * TH:(oc + 1) * TH],
                ps[:, 0:TH], ALU.add))
            P.cut()
        for c in range(NC):
            sl = e["nscr"][:, (c % 2) * 1024:(c % 2) * 1024 + 1024]
            v1 = P.dma("sync", sl, e["xpark"][:, c * T + q0: c * T + q0 + TH], "xq")
            P.wait("vector", "xq", v1)
            P.v(lambda v, c=c, sl=sl: v.scalar_tensor_tensor(
                sl, ACC[:, c * TH:(c + 1) * TH], 0.5, sl, ALU.mult, ALU.add))
            P.cut()
            P.dma("sync", e["xpark"][:, c * T + q0: c * T + q0 + TH], sl, "xqs")
        P.wait("sync", "xqs", P.semv["xqs"])
        P.cut()
    P.sfree("l3he")
    P.sfree("l3acc")
    P.sfree("l3wr")
    P.sfree("l3wrb")
    unpark_x(P)


def attn_core_64(P, Q, Kb, V, Yatt, bv_fn, sqs, s_scr, Pb, cells):
    e = P.env
    psA, psB, rows = e["psA"], e["psB"], e["rows"]
    for h in range(16):
        hc, hr = h // 2, (h % 2) * 64
        P.s(lambda s, hc=hc, hr=hr: s.activation(sqs[0:64, 0:T], Q[hr:hr + 64, hc * T:(hc + 1) * T], AF.Square))
        P.cut()
        P.t(lambda t: t.matmul(psA[0:1, 0:512], e["ones_cb"][0:64, 0:1], sqs[0:64, 0:512], start=True, stop=True))
        P.t(lambda t: t.matmul(psA[0:1, 512:1024], e["ones_cb"][0:64, 0:1], sqs[0:64, 512:1024], start=True, stop=True))
        P.t(lambda t: t.matmul(psA[0:1, 1024:1536], e["ones_cb"][0:64, 0:1], sqs[0:64, 1024:1536], start=True, stop=True))
        P.t(lambda t: t.matmul(psA[0:1, 1536:2048], e["ones_cb"][0:64, 0:1], sqs[0:64, 1536:2048], start=True, stop=True))
        P.cut()
        P.s(lambda s, hc=hc, hr=hr: s.activation(sqs[0:64, 0:T], Kb[hr:hr + 64, hc * T:(hc + 1) * T], AF.Square))
        P.s(lambda s: s.activation(rows[0:1, 0:T], psA[0:1, 0:T], AF.Sqrt))
        P.cut()
        P.t(lambda t: t.matmul(psB[0:1, 0:512], e["ones_cb"][0:64, 0:1], sqs[0:64, 0:512], start=True, stop=True))
        P.t(lambda t: t.matmul(psB[0:1, 512:1024], e["ones_cb"][0:64, 0:1], sqs[0:64, 512:1024], start=True, stop=True))
        P.t(lambda t: t.matmul(psB[0:1, 1024:1536], e["ones_cb"][0:64, 0:1], sqs[0:64, 1024:1536], start=True, stop=True))
        P.t(lambda t: t.matmul(psB[0:1, 1536:2048], e["ones_cb"][0:64, 0:1], sqs[0:64, 1536:2048], start=True, stop=True))
        P.cut()
        P.s(lambda s: s.activation(rows[0:1, T:2 * T], psB[0:1, 0:T], AF.Copy))
        P.cut()
        P.v(lambda v: v.tensor_reduce(cells[0:1, 0:1], rows[0:1, T:2 * T], AX.X, ALU.max))
        P.cut()
        P.s(lambda s: s.activation(cells[0:1, 1:2], cells[0:1, 0:1], AF.Sqrt))
        P.cut()
        P.v(lambda v: v.tensor_scalar(cells[0:1, 2:3], cells[0:1, 1:2], -1.0, None, ALU.mult))
        P.cut()
        P.v(lambda v: v.tensor_scalar(rows[0:1, T:2 * T], rows[0:1, 0:T],
                                      cells[0:1, 2:3], None, ALU.mult))
        P.cut()
        for j in range(NT):
            P.t(lambda t, j=j: t.matmul(psA[:, j * 512:(j + 1) * 512], e["ones_r32"][:, :],
                                        rows[0:1, T + j * 512:T + (j + 1) * 512], start=True, stop=True))
        P.cut()
        P.v(lambda v: v.tensor_copy(e["bc0"][:, :], psA[:, :]))
        P.cut()
        for qh in range(2):
            q0 = qh * 1024
            for j in range(19):
                if j < 16:
                    sps = psA if j % 2 == 0 else psB
                    for n in range(2):
                        P.t(lambda t, j=j, n=n, sps=sps, hc=hc, hr=hr: t.matmul(
                            sps[:, 1024 + n * 512:1024 + (n + 1) * 512],
                            Kb[hr:hr + 64, hc * T + j * 128: hc * T + j * 128 + 128],
                            Q[hr:hr + 64, hc * T + q0 + n * 512: hc * T + q0 + (n + 1) * 512],
                            start=True, stop=True))
                if 0 <= j - 1 < 16:
                    spsp = psA if (j - 1) % 2 == 0 else psB
                    P.v(lambda v, j=j, spsp=spsp: v.tensor_tensor(
                        s_scr[:, ((j - 1) % 2) * 1024:((j - 1) % 2) * 1024 + 1024],
                        spsp[:, 1024:2048], e["bc0"][:, q0:q0 + 1024], ALU.add))
                if 0 <= j - 2 < 16:
                    kc = j - 2
                    P.s(lambda s, kc=kc: s.activation(
                        Pb[:, (kc % 2) * 1024:(kc % 2) * 1024 + 1024],
                        s_scr[:, (kc % 2) * 1024:(kc % 2) * 1024 + 1024], AF.Exp))
                if 0 <= j - 3 < 16:
                    kc = j - 3
                    P.t(lambda t, kc=kc, h=h: t.matmul(
                        psA[0:65, 0:512],
                        V[:, kc * 1040 + h * 65: kc * 1040 + h * 65 + 65],
                        Pb[:, (kc % 2) * 1024:(kc % 2) * 1024 + 512],
                        start=(kc == 0), stop=(kc == 15)))
                    P.t(lambda t, kc=kc, h=h: t.matmul(
                        psA[0:65, 512:1024],
                        V[:, kc * 1040 + h * 65: kc * 1040 + h * 65 + 65],
                        Pb[:, (kc % 2) * 1024 + 512:(kc % 2) * 1024 + 1024],
                        start=(kc == 0), stop=(kc == 15)))
                P.cut()
            P.s(lambda s: s.activation(rows[0:1, 0:1024], psA[64:65, 0:1024], AF.Copy))
            P.cut()
            P.v(lambda v: v.reciprocal(rows[0:1, 0:1024], rows[0:1, 0:1024]))
            P.cut()
            P.t(lambda t: t.matmul(psB[0:64, 0:512], e["ones_r32"][0:1, 0:64],
                                   rows[0:1, 0:512], start=True, stop=True))
            P.t(lambda t: t.matmul(psB[0:64, 512:1024], e["ones_r32"][0:1, 0:64],
                                   rows[0:1, 512:1024], start=True, stop=True))
            P.cut()
            P.v(lambda v: v.tensor_copy(e["bc1"][0:64, 0:1024], psB[0:64, 0:1024]))
            P.cut()
            P.v(lambda v: v.tensor_tensor(s_scr[0:64, 0:1024], psA[0:64, 0:1024],
                                          e["bc1"][0:64, 0:1024], ALU.mult))
            P.v(lambda v, hc=hc, hr=hr, q0=q0: v.tensor_scalar(
                Yatt[hr:hr + 64, hc * T + q0: hc * T + q0 + 1024],
                s_scr[0:64, 0:1024], bv_fn(hr, hc), None, ALU.add))
            P.cut()


def emit_l1_attn(P, packed):
    e = P.env
    psA, psB, rows = e["psA"], e["psB"], e["rows"]
    emit_ln_v2(P)
    park_x(P)
    XH = e["XH"]
    bias = P.sbuf("l1abias", 48, F32, top=True)
    P.dma("sync", bias[:, 0:34], P.din("L1a_bias", packed["L1a_bias"])[:, :], "l1ab")
    P.wait("vector", "l1ab", P.semv["l1ab"])
    P.cut()
    P.v(lambda v: v.tensor_scalar(bias[:, 40:48], bias[:, 0:8], 0.125, None, ALU.mult))
    P.cut()
    BQ, BL, BK, BO, BV, BQS = 40, 8, 10, 18, 26, 40

    Q = P.sbuf("l1q", 16384, BF16)
    lat = P.sbuf("l1lat", 4096, BF16)
    wq = WFull(P, "l1wq", packed["L1a_wq"])

    def evq(oc, ps):
        P.v(lambda v, oc=oc: v.tensor_scalar(Q[:, oc * T:(oc + 1) * T], ps[:, :],
                                             0.125, bias[:, 40 + oc:41 + oc], ALU.mult, ALU.add))
    mm_fm(P, wq, XH, 0, NC, NC, evq, wait_sem=wq.sem, wait_val=wq.ready)
    wq.free()
    wl = WFull(P, "l1wl", packed["L1a_wl"])

    def evl(oc, ps):
        P.v(lambda v, oc=oc: v.tensor_scalar(lat[:, oc * T:(oc + 1) * T], ps[:, :],
                                             bias[:, 8 + oc:9 + oc], None, ALU.add))
    mm_fm(P, wl, XH, 0, NC, 2, evl, wait_sem=wl.sem, wait_val=wl.ready)
    wl.free()
    free_xh(P)
    Kb = P.sbuf("l1k", 16384, BF16)
    V = P.sbuf("l1v", 16 * 1040, BF16)
    Yatt = P.sbuf("l1yatt", 16384, BF16)
    wk = WFull(P, "l1wk", packed["L1a_wk"])

    def evk(oc, ps):
        P.v(lambda v, oc=oc: v.tensor_scalar(Kb[:, oc * T:(oc + 1) * T], ps[:, :],
                                             bias[:, 10 + oc:11 + oc], None, ALU.add))
    mm_fm(P, wk, lat, 0, 2, NC, evk, wait_sem=wk.sem, wait_val=wk.ready)
    wk.free()
    P.v(lambda v: v.memset(V[:, :], 1.0))
    P.cut()
    wv = WFull(P, "l1wv", packed["L1a_wv"])
    P.wait("tensor", wv.sem, wv.ready)
    for tt in range(17):
        if tt < 16:
            ps = psA if tt % 2 == 0 else psB
            for ic in range(2):
                for j in range(2):
                    P.t(lambda t, tt=tt, ic=ic, j=j, ps=ps: t.matmul(
                        ps[:, j * 512:(j + 1) * 512],
                        lat[:, ic * T + tt * 128: ic * T + tt * 128 + 128],
                        wv.sb[:, ic * 1024 + j * 512: ic * 1024 + (j + 1) * 512],
                        start=(ic == 0), stop=(ic == 1)))
        if tt > 0:
            psp = psA if (tt - 1) % 2 == 0 else psB
            dst = V[:, (tt - 1) * 1040:tt * 1040].rearrange("p (h d) -> p h d", d=65)[:, :, 0:64]
            P.v(lambda v, psp=psp, dst=dst: v.tensor_copy(dst, psp[:, 0:1024]))
        P.cut()
    wv.free()
    P.sfree("l1lat")
    sqs = P.sbuf("l1sqs", 2048, BF16)
    s_scr = P.sbuf("l1ss", 2048, F32)
    Pb = P.sbuf("l1pb", 2048, BF16)
    cells = P.sbuf("l1cells", 8, F32)

    attn_core_64(P, Q, Kb, V, Yatt, lambda hr, hc: bias[hr:hr + 64, 26 + hc:27 + hc],
                 sqs, s_scr, Pb, cells)
    P.sfree("l1q")
    P.sfree("l1k")
    P.sfree("l1v")
    P.sfree("l1sqs")
    P.sfree("l1ss")
    P.sfree("l1pb")
    P.sfree("l1cells")
    unpark_x(P)
    X = e["X"]
    wo = WFull(P, "l1wo", packed["L1a_wo"])

    def evo(oc, ps):
        P.v(lambda v, oc=oc: v.scalar_tensor_tensor(
            X[:, oc * T:(oc + 1) * T], ps[:, :], bias[:, 18 + oc:19 + oc],
            X[:, oc * T:(oc + 1) * T], ALU.add, ALU.add))
    mm_fm(P, wo, Yatt, 0, NC, NC, evo, wait_sem=wo.sem, wait_val=wo.ready)
    wo.free()
    P.sfree("l1yatt")


def emit_l3_attn(P, packed):
    e = P.env
    psA, psB = e["psA"], e["psB"]
    emit_ln_v2(P)
    park_x(P)
    XH = e["XH"]
    bias = P.sbuf("l3abias", 40, F32, top=True)
    P.dma("sync", bias[:, 0:32], P.din("L3a_bias", packed["L3a_bias"])[:, :], "l3ab")
    P.wait("vector", "l3ab", P.semv["l3ab"])
    P.cut()
    P.v(lambda v: v.tensor_scalar(bias[:, 32:40], bias[:, 0:8], 0.03125, None, ALU.mult))
    P.cut()
    Q = P.sbuf("l3q", 16384, BF16)
    Kb = P.sbuf("l3k", 16384, BF16)
    V = P.sbuf("l3v", 16 * 1040, BF16)

    def evqk(oc, ps):
        if oc < 8:
            P.v(lambda v, oc=oc: v.tensor_scalar(Q[:, oc * T:(oc + 1) * T], ps[:, :],
                                                 0.03125, bias[:, 32 + oc:33 + oc],
                                                 ALU.mult, ALU.add))
        else:
            c = oc - 8
            P.v(lambda v, c=c: v.tensor_scalar(Kb[:, c * T:(c + 1) * T], ps[:, :],
                                               bias[:, 8 + c:9 + c], None, ALU.add))
    mm_stream(P, "l3a_wqk", packed["L3a_wqk"], XH, 0, T, T, evqk, PW=512)
    P.v(lambda v: v.memset(V[:, :], 1.0))
    P.cut()
    wv = WFull(P, "l3awv", packed["L3a_wv"])
    P.wait("tensor", wv.sem, wv.ready)
    for tt in range(17):
        if tt < 16:
            ps = psA if tt % 2 == 0 else psB
            for ic in range(NC):
                for j in range(2):
                    P.t(lambda t, tt=tt, ic=ic, j=j, ps=ps: t.matmul(
                        ps[:, j * 512:(j + 1) * 512],
                        XH[:, ic * T + tt * 128: ic * T + tt * 128 + 128],
                        wv.sb[:, ic * 1024 + j * 512: ic * 1024 + (j + 1) * 512],
                        start=(ic == 0), stop=(ic == NC - 1)))
        if tt > 0:
            psp = psA if (tt - 1) % 2 == 0 else psB
            dst = V[:, (tt - 1) * 1040:tt * 1040].rearrange("p (h d) -> p h d", d=65)[:, :, 0:64]
            P.v(lambda v, psp=psp, dst=dst: v.tensor_copy(dst, psp[:, 0:1024]))
        P.cut()
    wv.free()
    sqs = P.sbuf("l3sqs", 2048, BF16)
    s_scr = P.sbuf("l3ss", 2048, F32)
    Pb = P.sbuf("l3pb", 2048, BF16)
    cells = P.sbuf("l3cells", 8, F32)
    attn_core_64(P, Q, Kb, V, Q, lambda hr, hc: bias[hr:hr + 64, 16 + hc:17 + hc],
                 sqs, s_scr, Pb, cells)
    P.sfree("l3k")
    P.sfree("l3v")
    P.sfree("l3sqs")
    P.sfree("l3pb")
    P.sfree("l3cells")
    unpark_x(P)
    X = e["X"]
    wo = WFull(P, "l3awo", packed["L3a_wo"])

    def evo(oc, ps):
        P.v(lambda v, oc=oc: v.scalar_tensor_tensor(
            s_scr[:, 0:T] if False else e["nscr"][:, :], ps[:, :], bias[:, 24 + oc:25 + oc],
            XH[:, oc * T:(oc + 1) * T], ALU.add, ALU.add))
        P.v(lambda v, oc=oc: v.tensor_tensor(X[:, oc * T:(oc + 1) * T], e["nscr"][:, :],
                                             X[:, oc * T:(oc + 1) * T], ALU.add))
    mm_fm(P, wo, Q, 0, NC, NC, evo, wait_sem=wo.sem, wait_val=wo.ready)
    wo.free()
    P.sfree("l3ss")
    P.sfree("l3q")


def emit_l0_attn(P, packed):
    e = P.env
    psA, psB, rows = e["psA"], e["psB"], e["rows"]
    emit_ln_v2(P)
    park_x(P)
    XH = e["XH"]
    lamr = P.sbuf("l0lam", 256, F32, top=True)
    alr = P.sbuf("l0al", 2, F32, top=True)
    cl = P.sbuf("l0cl", 16, F32, top=True)
    c1col = P.sbuf("l0c1c", 2, F32, top=True)
    blk = P.sbuf("l0blk", 4, BF16, top=True)
    P.dma("sync", lamr[0:1, :], P.din("L0a_lam", packed["L0a_lam"])[:, :], "l0ac")
    P.dma("sync", alr[0:1, 0:1], P.din("L0a_alpha", packed["L0a_alpha"])[:, :], "l0ac")
    P.dma("sync", blk[:, :], P.din("c_blk32_b16", packed["c_blk32_b16"])[:, :], "l0ac")
    P.all_wait("l0ac", P.semv["l0ac"])
    P.v(lambda v: v.tensor_tensor(rows[0:1, 0:64], lamr[0:1, 0:64], lamr[0:1, 128:192], ALU.mult))
    P.v(lambda v: v.tensor_tensor(rows[0:1, 64:128], lamr[0:1, 64:128], lamr[0:1, 192:256], ALU.mult))
    P.cut()
    P.v(lambda v: v.tensor_reduce(cl[0:1, 0:1], rows[0:1, 0:64], AX.X, ALU.add))
    P.v(lambda v: v.tensor_reduce(cl[0:1, 1:2], rows[0:1, 64:128], AX.X, ALU.add))
    P.cut()
    P.s(lambda s: s.activation(cl[0:1, 4:5], cl[0:1, 0:1], AF.Exp))
    P.s(lambda s: s.activation(cl[0:1, 5:6], cl[0:1, 1:2], AF.Exp))
    P.cut()
    P.v(lambda v: v.tensor_tensor(cl[0:1, 2:3], cl[0:1, 4:5], cl[0:1, 5:6], ALU.subtract))
    P.cut()
    P.v(lambda v: v.tensor_scalar(cl[0:1, 2:3], cl[0:1, 2:3], 0.2, None, ALU.add))
    P.v(lambda v: v.tensor_scalar(cl[0:1, 3:4], alr[0:1, 0:1], 2.0 / math.pi,
                                  1.0 / math.pi, ALU.mult, ALU.add))
    P.cut()
    P.t(lambda t: t.matmul(psA[:, 0:1], e["ones_r32"][:, :], cl[0:1, 3:4], start=True, stop=True))
    P.cut()
    P.v(lambda v: v.tensor_copy(c1col[:, 0:1], psA[:, 0:1]))
    P.cut()

    if "vpark" not in e:
        e["vpark"] = P.nc.dram_tensor("vpark", [128, 32768], BF16).ap()
    V = P.sbuf("l0v", 32768, BF16)
    wv = WFull(P, "l0awv", packed["L0a_wv"])
    P.wait("tensor", wv.sem, wv.ready)
    for tt in range(17):
        if tt < 16:
            ps = psA if tt % 2 == 0 else psB
            for ic in range(NC):
                for jj in range(4):
                    P.t(lambda t, tt=tt, ic=ic, jj=jj, ps=ps: t.matmul(
                        ps[:, jj * 512:(jj + 1) * 512],
                        XH[:, ic * T + tt * 128: ic * T + tt * 128 + 128],
                        wv.sb[:, ic * 2048 + jj * 512: ic * 2048 + (jj + 1) * 512],
                        start=(ic == 0), stop=(ic == NC - 1)))
        if tt > 0:
            psp = psA if (tt - 1) % 2 == 0 else psB
            P.v(lambda v, tt=tt, psp=psp: v.tensor_copy(V[:, (tt - 1) * 2048:tt * 2048], psp[:, :]))
        P.cut()
    wv.free()
    for c in range(16):
        P.dma("sync", e["vpark"][:, c * 2048:(c + 1) * 2048], V[:, c * 2048:(c + 1) * 2048], "vpk")
    P.all_wait("vpk", P.semv["vpk"])
    P.sfree("l0v")

    Q = P.sbuf("l0q", 16384, BF16)
    wq = WFull(P, "l0awq", packed["L0a_wq"])

    def evq(oc, ps):
        P.v(lambda v, oc=oc: v.tensor_copy(Q[:, oc * T:(oc + 1) * T], ps[:, :]))
    mm_fm(P, wq, XH, 0, NC, NC, evq, wait_sem=wq.sem, wait_val=wq.ready)
    wq.free()
    Kb = P.sbuf("l0k", 16384, BF16)
    wk = WFull(P, "l0awk", packed["L0a_wk"])

    def evk(oc, ps):
        P.v(lambda v, oc=oc: v.tensor_copy(Kb[:, oc * T:(oc + 1) * T], ps[:, :]))
    mm_fm(P, wk, XH, 0, NC, NC, evk, wait_sem=wk.sem, wait_val=wk.ready)
    wk.free()
    free_xh(P)

    rope = P.sbuf("l0rope", 8192, BF16)
    P.dma("sync", rope[:, :], P.din("c_rope", packed["c_rope"])[:, :], "l0rp")
    P.wait("vector", "l0rp", P.semv["l0rp"])
    P.cut()
    zsw = P.sbuf("l0zsw", 2048, BF16)
    rs = P.sbuf("l0rs", 2048, F32)
    SHUF = list(range(16, 32)) + list(range(16))
    for buf, co, so in ((Q, 0, 2048), (Kb, 4096, 6144)):
        for c in range(NC):
            P.v(lambda v, buf=buf, c=c: v.stream_shuffle(zsw[:, :], buf[:, c * T:(c + 1) * T], SHUF))
            P.v(lambda v, buf=buf, c=c, co=co: v.tensor_tensor(
                e["nscr"][:, :], buf[:, c * T:(c + 1) * T], rope[:, co:co + 2048], ALU.mult))
            P.v(lambda v, so=so: v.tensor_tensor(rs[:, :], zsw[:, :], rope[:, so:so + 2048], ALU.mult))
            P.v(lambda v, buf=buf, c=c: v.tensor_tensor(
                buf[:, c * T:(c + 1) * T], e["nscr"][:, :], rs[:, :], ALU.add))
    P.cut()
    P.sfree("l0rope")
    P.sfree("l0zsw")
    P.sfree("l0rs")

    V = P.sbuf("l0v", 32768, BF16)
    P.cut()
    for c in range(16):
        P.dma("sync", V[:, c * 2048:(c + 1) * 2048], e["vpark"][:, c * 2048:(c + 1) * 2048], "vup")
    P.all_wait("vup", P.semv["vup"])

    Yh = P.sbuf("l0yh", 2048, F32)
    rs = P.sbuf("l0rs2", 512, F32)
    Pb = P.sbuf("l0pb", 2048, BF16)
    s_scr = P.sbuf("l0ss", 2048, F32)
    sqs = P.sbuf("l0sq", 2048, BF16)
    ib = P.sbuf("l0ib", 1024, F32)
    gcol = P.sbuf("l0gn", 4, F32)
    negb2 = e["bc1"]
    for h in range(16):
        hc4, hr = h // 4, (h % 4) * 32
        for m in (0, 1):
            qch = 4 * m + hc4
            P.s(lambda s, qch=qch: s.activation(sqs[:, :], Q[:, qch * T:(qch + 1) * T], AF.Square))
            P.cut()
            for jj in range(NT):
                P.t(lambda t, jj=jj: t.matmul(psA[0:1, jj * 512:(jj + 1) * 512],
                                              blk[:, h % 4:h % 4 + 1],
                                              sqs[:, jj * 512:(jj + 1) * 512], start=True, stop=True))
            P.cut()
            P.s(lambda s, qch=qch: s.activation(sqs[:, :], Kb[:, qch * T:(qch + 1) * T], AF.Square))
            P.s(lambda s: s.activation(rows[0:1, 0:T], psA[0:1, 0:T], AF.Sqrt))
            P.cut()
            for jj in range(NT):
                P.t(lambda t, jj=jj: t.matmul(psB[0:1, jj * 512:(jj + 1) * 512],
                                              blk[:, h % 4:h % 4 + 1],
                                              sqs[:, jj * 512:(jj + 1) * 512], start=True, stop=True))
            P.cut()
            P.s(lambda s: s.activation(rows[0:1, T:2 * T], psB[0:1, 0:T], AF.Copy))
            P.cut()
            P.v(lambda v: v.tensor_reduce(cl[0:1, 8:9], rows[0:1, T:2 * T], AX.X, ALU.max))
            P.cut()
            P.s(lambda s: s.activation(cl[0:1, 9:10], cl[0:1, 8:9], AF.Sqrt))
            P.cut()
            P.v(lambda v: v.tensor_scalar(cl[0:1, 10:11], cl[0:1, 9:10], -1.0, None, ALU.mult))
            P.cut()
            P.v(lambda v: v.tensor_scalar(rows[0:1, T:2 * T], rows[0:1, 0:T],
                                          cl[0:1, 10:11], None, ALU.mult))
            P.cut()
            for jj in range(NT):
                P.t(lambda t, jj=jj: t.matmul(psA[:, jj * 512:(jj + 1) * 512], e["ones_r32"][:, :],
                                              rows[0:1, T + jj * 512:T + (jj + 1) * 512],
                                              start=True, stop=True))
            P.cut()
            dstbc = e["bc0"] if m == 0 else e["bc1"]
            P.v(lambda v, dstbc=dstbc: v.tensor_copy(dstbc[:, :], psA[:, :]))
            P.cut()
        for qq in range(4):
            q0 = qq * 512
            for j in range(19):
                if j < 16:
                    for m in (0, 1):
                        qch = 4 * m + hc4
                        sp = psA if m == 0 else psB
                        P.t(lambda t, j=j, m=m, qch=qch, sp=sp: t.matmul(
                            sp[:, 1024 + (j % 2) * 512:1024 + (j % 2) * 512 + 512],
                            Kb[hr:hr + 32, qch * T + j * 128: qch * T + j * 128 + 128],
                            Q[hr:hr + 32, qch * T + q0: qch * T + q0 + 512],
                            start=True, stop=True, tile_position=(hr, 0)))
                if 0 <= j - 1 < 16:
                    for m in (0, 1):
                        sp = psA if m == 0 else psB
                        bcm = e["bc0"] if m == 0 else e["bc1"]
                        P.v(lambda v, j=j, m=m, sp=sp, bcm=bcm: v.tensor_tensor(
                            s_scr[:, (m * 2 + ((j - 1) % 2)) * 512:(m * 2 + ((j - 1) % 2)) * 512 + 512],
                            sp[:, 1024 + ((j - 1) % 2) * 512:1024 + ((j - 1) % 2) * 512 + 512],
                            bcm[:, q0:q0 + 512], ALU.add))
                if 0 <= j - 2 < 16:
                    kc = j - 2
                    for m in (0, 1):
                        P.s(lambda s, kc=kc, m=m: s.activation(
                            Pb[:, (m * 2 + (kc % 2)) * 512:(m * 2 + (kc % 2)) * 512 + 512],
                            s_scr[:, (m * 2 + (kc % 2)) * 512:(m * 2 + (kc % 2)) * 512 + 512],
                            AF.Exp))
                if 0 <= j - 3 < 16:
                    kc = j - 3
                    for m in (0, 1):
                        sp = psA if m == 0 else psB
                        P.t(lambda t, kc=kc, m=m, h=h, sp=sp: t.matmul(
                            sp[:, 0:512],
                            V[:, kc * 2048 + h * 128: kc * 2048 + h * 128 + 128],
                            Pb[:, (m * 2 + (kc % 2)) * 512:(m * 2 + (kc % 2)) * 512 + 512],
                            start=(kc == 0), stop=(kc == 15)))
                        P.t(lambda t, kc=kc, m=m, sp=sp: t.matmul(
                            sp[0:1, 512:1024],
                            e["ones_cb"][:, :],
                            Pb[:, (m * 2 + (kc % 2)) * 512:(m * 2 + (kc % 2)) * 512 + 512],
                            start=(kc == 0), stop=(kc == 15)))
                P.cut()
            P.s(lambda s: s.activation(rows[0:1, 0:512], psA[0:1, 512:1024], AF.Copy))
            P.s(lambda s: s.activation(rows[0:1, 512:1024], psB[0:1, 512:1024], AF.Copy))
            P.cut()
            P.v(lambda v: v.reciprocal(rows[0:1, 0:512], rows[0:1, 0:512]))
            P.v(lambda v: v.reciprocal(rows[0:1, 512:1024], rows[0:1, 512:1024]))
            P.cut()
            P.v(lambda v: v.tensor_scalar(rows[0:1, 512:1024], rows[0:1, 512:1024],
                                          cl[0:1, 2:3], None, ALU.mult))
            P.cut()
            P.t(lambda t: t.matmul(psA[:, 1024:1536], e["ones_r32"][:, :],
                                   rows[0:1, 0:512], start=True, stop=True))
            P.t(lambda t: t.matmul(psB[:, 1024:1536], e["ones_r32"][:, :],
                                   rows[0:1, 512:1024], start=True, stop=True))
            P.cut()
            P.v(lambda v: v.tensor_copy(ib[:, 0:512], psA[:, 1024:1536]))
            P.v(lambda v: v.tensor_copy(ib[:, 512:1024], psB[:, 1024:1536]))
            P.cut()
            P.v(lambda v: v.tensor_tensor(e["nscr"][:, 0:512], psA[:, 0:512], ib[:, 0:512], ALU.mult))
            P.v(lambda v: v.tensor_tensor(rs[:, 0:512], psB[:, 0:512], ib[:, 512:1024], ALU.mult))
            P.v(lambda v, q0=q0: v.tensor_tensor(Yh[:, q0:q0 + 512], e["nscr"][:, 0:512],
                                                 rs[:, 0:512], ALU.subtract))
            P.cut()
        P.v(lambda v: v.tensor_copy(sqs[:, :], Yh[:, :]))
        P.cut()
        for jj in range(NT):
            P.t(lambda t, jj=jj: t.matmul(psA[0:1, jj * 512:(jj + 1) * 512], e["ones_cb"][:, :],
                                          sqs[:, jj * 512:(jj + 1) * 512], start=True, stop=True))
        P.cut()
        P.s(lambda s: s.activation(sqs[:, :], sqs[:, :], AF.Square))
        P.cut()
        for jj in range(NT):
            P.t(lambda t, jj=jj: t.matmul(psB[0:1, jj * 512:(jj + 1) * 512], e["ones_cb"][:, :],
                                          sqs[:, jj * 512:(jj + 1) * 512], start=True, stop=True))
        P.cut()
        P.s(lambda s: s.activation(rows[0:1, 0:T], psA[0:1, 0:T], AF.Copy))
        P.s(lambda s: s.activation(rows[0:1, T:2 * T], psB[0:1, 0:T], AF.Copy))
        P.cut()
        P.v(lambda v: v.tensor_reduce(cl[0:1, 8:9], rows[0:1, 0:T], AX.X, ALU.add))
        P.v(lambda v: v.tensor_reduce(cl[0:1, 9:10], rows[0:1, T:2 * T], AX.X, ALU.add))
        P.cut()
        NE = 128.0 * 2048.0
        P.v(lambda v: v.tensor_scalar(cl[0:1, 10:11], cl[0:1, 8:9], 1.0 / NE, None, ALU.mult))
        P.cut()
        P.v(lambda v: v.scalar_tensor_tensor(cl[0:1, 11:12], cl[0:1, 10:11], -1.0,
                                             cl[0:1, 10:11], ALU.mult, ALU.mult))
        P.cut()
        P.v(lambda v: v.scalar_tensor_tensor(cl[0:1, 11:12], cl[0:1, 9:10], 1.0 / NE,
                                             cl[0:1, 11:12], ALU.mult, ALU.add))
        P.cut()
        P.constf(1e-5)
        P.s(lambda s: s.activation(cl[0:1, 11:12], cl[0:1, 11:12], AF.Sqrt, bias=1e-5))
        P.cut()
        P.v(lambda v: v.reciprocal(cl[0:1, 11:12], cl[0:1, 11:12]))
        P.cut()
        P.v(lambda v: v.tensor_scalar(cl[0:1, 11:12], cl[0:1, 11:12], 0.8, None, ALU.mult))
        P.cut()
        P.v(lambda v: v.scalar_tensor_tensor(cl[0:1, 12:13], cl[0:1, 10:11], -1.0,
                                             cl[0:1, 11:12], ALU.mult, ALU.mult))
        P.cut()
        P.t(lambda t: t.matmul(psA[:, 1024:1026], e["ones_r32"][:, :], cl[0:1, 11:13],
                               start=True, stop=True))
        P.cut()
        P.v(lambda v: v.tensor_copy(gcol[:, 0:2], psA[:, 1024:1026]))
        P.cut()
        Yre = Yh[:, 0:2048].rearrange("p (c a) -> p c a", a=16)
        for g in range(4):
            for aa in range(4):
                a = g * 4 + aa
                P.t(lambda t, a=a, aa=aa, Yre=Yre: t.transpose(
                    psB[:, 1024 + aa * 128:1024 + (aa + 1) * 128],
                    Yre[:, :, a:a + 1], e["ident"][:, :]))
            P.cut()
            for aa in range(4):
                a = g * 4 + aa
                P.v(lambda v, a=a, aa=aa, h=h: v.tensor_scalar(
                    V[:, a * 2048 + h * 128: a * 2048 + h * 128 + 128],
                    psB[:, 1024 + aa * 128:1024 + (aa + 1) * 128],
                    gcol[:, 0:1], gcol[:, 1:2], ALU.mult, ALU.add))
            P.cut()
    P.sfree("l0q")
    P.sfree("l0k")
    P.sfree("l0yh")
    P.sfree("l0pb")
    P.sfree("l0ss")
    P.sfree("l0sq")
    P.sfree("l0ib")
    P.sfree("l0rs2")

    unpark_x(P)
    X = e["X"]
    q1 = P.sbuf("l0q1", 2048, F32)
    q2 = P.sbuf("l0q2", 2048, F32)
    sg = P.sbuf("l0sg", 2048, BF16)
    gtmp = P.sbuf("l0gt", 2048, BF16)
    Vre = V[:, 0:32768].rearrange("p (a dv) -> p a dv", dv=2048)

    def rhs_fn(ic, j):
        return Vre[:, 4 * j:4 * j + 4, ic * 128:(ic + 1) * 128]

    def evp(oc, ps):
        if oc % 2 == 0:
            P.s(lambda s: s.activation(sg[:, :], ps[:, :], AF.Sign))
            P.s(lambda s: s.activation(q1[:, :], ps[:, :], AF.Abs))
            P.cut()
            P.v(lambda v: v.tensor_scalar(q2[:, :], q1[:, :], 1.0, None, ALU.max))
            P.v(lambda v: v.reciprocal(q2[:, :], q2[:, :]))
            P.v(lambda v: v.tensor_tensor(q2[:, :], q1[:, :], q2[:, :], ALU.min))
            P.cut()
            P.s(lambda s: s.activation(q2[:, :], q2[:, :], AF.Arctan))
            P.cut()
            P.v(lambda v: v.tensor_scalar(e["nscr"][:, :], q2[:, :], -2.0,
                                          math.pi / 2.0, ALU.mult, ALU.add))
            P.v(lambda v: v.tensor_scalar(q1[:, :], q1[:, :], 1.0, None, ALU.is_gt))
            P.v(lambda v: v.tensor_tensor(e["nscr"][:, :], e["nscr"][:, :], q1[:, :], ALU.mult))
            P.v(lambda v: v.tensor_tensor(q2[:, :], q2[:, :], e["nscr"][:, :], ALU.add))
            P.v(lambda v: v.tensor_tensor(q2[:, :], q2[:, :], sg[:, :], ALU.mult))
            P.v(lambda v: v.tensor_scalar(gtmp[:, :], q2[:, :], c1col[:, 0:1], 0.5,
                                          ALU.mult, ALU.add))
        else:
            c = oc // 2
            P.v(lambda v, c=c: v.tensor_tensor(e["nscr"][:, :], ps[:, :], gtmp[:, :], ALU.mult))
            P.v(lambda v, c=c: v.tensor_tensor(X[:, c * T:(c + 1) * T], e["nscr"][:, :],
                                               X[:, c * T:(c + 1) * T], ALU.add))
    mm_stream(P, "l0a_wproj", packed["L0a_wproj"], rhs_fn, 0, 0, T, evp, PW=128)
    P.sfree("l0v")
    P.sfree("l0q1")
    P.sfree("l0q2")
    P.sfree("l0sg")
    P.sfree("l0gt")


def ensure_sel(P, packed):
    if "selw" not in P._sb:
        sel = P.sbuf("selw", 2048, F32, top=True)
        P.env["selw"] = sel
        P.dma("sync", sel[0:16, :], P.din("c_sel16_f32", packed["c_sel16_f32"])[:, :], "selc")
        P.all_wait("selc", P.semv["selc"])
    return P.env["selw"]


def emit_l0_ffn(P, packed):
    e = P.env
    psA, psB, rows = e["psA"], e["psB"], e["rows"]
    emit_ln_v2(P)
    park_x(P)
    XH = e["XH"]
    NH = 22
    if "l0rw" not in P._sb:
        rw = P.sbuf("l0rw", 64, BF16, top=True)
        rbr = P.sbuf("l0rb", 8, F32, top=True)
        e["l0rw"], e["l0rb"] = rw, rbr
        P.dma("gpsimd", rw[:, 0:56], P.din("L0f_wr", packed["L0f_wr"]).rearrange("(c p) e -> p c e", p=128), "l0c")
        P.dma("sync", rbr[0:1, 0:7], P.din("L0f_rb", packed["L0f_rb"])[:, :], "l0c")
        P.all_wait("l0c", P.semv["l0c"])
    rw, rbr = e["l0rw"], e["l0rb"]
    sel = ensure_sel(P, packed)

    logit = P.sbuf("l0lg", 128, F32)
    P.t(lambda t: t.matmul(psB[:, 32:39], e["ones_r32"][:, :], rbr[0:1, 0:7], start=True, stop=True))
    P.cut()
    P.v(lambda v: v.tensor_copy(e["bc1"][:, 0:7], psB[:, 32:39]))
    P.cut()
    for tt in range(17):
        if tt < 16:
            ps = psA if tt % 2 == 0 else psB
            for ic in range(NC):
                P.t(lambda t, tt=tt, ic=ic, ps=ps: t.matmul(
                    ps[:, 0:7],
                    XH[:, ic * T + tt * 128: ic * T + tt * 128 + 128],
                    rw[:, ic * 7:(ic + 1) * 7],
                    start=(ic == 0), stop=(ic == NC - 1)))
        if tt > 0:
            psp = psA if (tt - 1) % 2 == 0 else psB
            P.v(lambda v, tt=tt, psp=psp: v.tensor_tensor(
                logit[:, (tt - 1) * 7:tt * 7], psp[:, 0:7], e["bc1"][:, 0:7], ALU.add))
        P.cut()

    pbuf = P.sbuf("l0p", 128, F32)
    mcol = P.sbuf("l0m", 16, F32)
    dsum = P.sbuf("l0d", 16, F32)
    keep = P.sbuf("l0keep", 128, F32)
    for tt in range(16):
        P.v(lambda v, tt=tt: v.tensor_reduce(mcol[:, tt:tt + 1], logit[:, tt * 7:(tt + 1) * 7],
                                             AX.X, ALU.max, negate=True))
    P.cut()
    for tt in range(16):
        P.s(lambda s, tt=tt: s.activation(pbuf[:, tt * 7:(tt + 1) * 7],
                                          logit[:, tt * 7:(tt + 1) * 7],
                                          AF.Exp, bias=mcol[:, tt:tt + 1]))
    P.cut()
    P.v(lambda v: v.tensor_copy(logit[:, :], pbuf[:, :]))
    P.cut()
    for step in range(2):
        for tt in range(16):
            P.v(lambda v, tt=tt: v.tensor_reduce(mcol[:, tt:tt + 1], logit[:, tt * 7:(tt + 1) * 7],
                                                 AX.X, ALU.max))
        P.cut()
        if step == 0:
            P.v(lambda v: v.tensor_copy(dsum[:, :], mcol[:, :]))
            for tt in range(16):
                P.v(lambda v, tt=tt: v.tensor_scalar(keep[:, tt * 7:(tt + 1) * 7],
                                                     logit[:, tt * 7:(tt + 1) * 7],
                                                     mcol[:, tt:tt + 1], None, ALU.is_lt))
            for tt in range(16):
                P.v(lambda v, tt=tt: v.tensor_tensor(logit[:, tt * 7:(tt + 1) * 7],
                                                     logit[:, tt * 7:(tt + 1) * 7],
                                                     keep[:, tt * 7:(tt + 1) * 7], ALU.mult))
        else:
            P.v(lambda v: v.tensor_tensor(dsum[:, :], dsum[:, :], mcol[:, :], ALU.add))
        P.cut()
    for tt in range(16):
        P.v(lambda v, tt=tt: v.tensor_scalar(keep[:, tt * 7:(tt + 1) * 7],
                                             pbuf[:, tt * 7:(tt + 1) * 7],
                                             mcol[:, tt:tt + 1], None, ALU.is_ge))
    P.v(lambda v: v.reciprocal(dsum[:, :], dsum[:, :]))
    P.cut()
    for tt in range(16):
        P.v(lambda v, tt=tt: v.tensor_scalar(pbuf[:, tt * 7:(tt + 1) * 7],
                                             pbuf[:, tt * 7:(tt + 1) * 7],
                                             dsum[:, tt:tt + 1], None, ALU.mult))
    P.cut()
    for tt in range(16):
        P.v(lambda v, tt=tt: v.tensor_tensor(pbuf[:, tt * 7:(tt + 1) * 7],
                                             pbuf[:, tt * 7:(tt + 1) * 7],
                                             keep[:, tt * 7:(tt + 1) * 7], ALU.mult))
    P.cut()
    wrow = P.sbuf("l0wr", 2048, F32)
    for tt in range(16):
        P.t(lambda t, tt=tt: t.transpose(psA[0:7, tt * 128:(tt + 1) * 128],
                                         pbuf[:, tt * 7:(tt + 1) * 7], e["ident"][:, :]))
    P.cut()
    P.v(lambda v: v.tensor_copy(wrow[0:7, :], psA[0:7, 0:2048]))
    P.cut()
    P.sfree("l0lg")
    P.sfree("l0p")
    P.sfree("l0m")
    P.sfree("l0d")
    P.sfree("l0keep")

    He = P.sbuf("l0he", NH * 1024, BF16)
    ACC = P.sbuf("l0acc", 8192, F32)
    TH = 1024
    for half in range(2):
        q0 = half * TH
        for ei in range(8):
            routed = ei > 0
            ex = ei - 1
            if routed:
                P.t(lambda t, ex=ex: t.matmul(psB[:, 1024:1536], sel[0:7, ex * 128:(ex + 1) * 128],
                                              wrow[0:7, q0:q0 + 512], start=True, stop=True))
                P.t(lambda t, ex=ex: t.matmul(psB[:, 1536:2048], sel[0:7, ex * 128:(ex + 1) * 128],
                                              wrow[0:7, q0 + 512:q0 + TH], start=True, stop=True))
                P.cut()
                P.v(lambda v: v.tensor_copy(e["bc0"][:, 0:TH], psB[:, 1024:2048]))
                P.cut()
            gname = "L0f_sg" if not routed else f"L0f_rg_{ex}"
            uname = "L0f_su" if not routed else f"L0f_ru_{ex}"
            dname = "L0f_sd" if not routed else f"L0f_rd_{ex}"

            def evg(oc, ps):
                P.s(lambda s, oc=oc, ps=ps: s.activation(
                    He[:, oc * TH:(oc + 1) * TH], ps[:, 0:TH], AF.Silu))
            mm_stream(P, gname, packed[gname], XH, q0, T, TH, evg, PW=256)

            def evu(oc, ps):
                P.v(lambda v, oc=oc, ps=ps: v.tensor_tensor(
                    He[:, oc * TH:(oc + 1) * TH], ps[:, 0:TH],
                    He[:, oc * TH:(oc + 1) * TH], ALU.mult))
            mm_stream(P, uname, packed[uname], XH, q0, T, TH, evu, PW=256)

            def evd(oc, ps, routed=routed, ei=ei):
                if not routed:
                    P.v(lambda v, oc=oc, ps=ps: v.tensor_copy(
                        ACC[:, oc * TH:(oc + 1) * TH], ps[:, 0:TH]))
                else:
                    P.v(lambda v, oc=oc, ps=ps: v.tensor_tensor(
                        e["nscr"][:, 0:TH], ps[:, 0:TH], e["bc0"][:, 0:TH], ALU.mult))
                    P.v(lambda v, oc=oc: v.tensor_tensor(
                        ACC[:, oc * TH:(oc + 1) * TH], ACC[:, oc * TH:(oc + 1) * TH],
                        e["nscr"][:, 0:TH], ALU.add))
            mm_stream(P, dname, packed[dname], He, 0, TH, TH, evd, PW=256)
        for c in range(NC):
            sl = e["nscr"][:, (c % 2) * 1024:(c % 2) * 1024 + 1024]
            v1 = P.dma("sync", sl, e["xpark"][:, c * T + q0: c * T + q0 + TH], "xq")
            P.wait("vector", "xq", v1)
            P.v(lambda v, c=c, sl=sl: v.scalar_tensor_tensor(
                sl, ACC[:, c * TH:(c + 1) * TH], 0.5, sl, ALU.mult, ALU.add))
            P.cut()
            P.dma("sync", e["xpark"][:, c * T + q0: c * T + q0 + TH], sl, "xqs")
        P.wait("sync", "xqs", P.semv["xqs"])
        P.cut()
    P.sfree("l0he")
    P.sfree("l0acc")
    P.sfree("l0wr")
    unpark_x(P)


def emit_l2_quant(P, packed):
    e = P.env
    wf = P.sbuf("l2wf", NC * 1024, F32)
    wd = P.din("L2w_ffw", packed["L2w_ffw"])
    for ic in range(NC):
        P.dma("sync", wf[:, ic * 1024:(ic + 1) * 1024], wd[ic * 128:(ic + 1) * 128, :], "l2w")
    e["l2bias"] = P.sbuf("l2bias", NC, F32, top=True)
    P.dma("sync", e["l2bias"][:, :], P.din("L2w_ffb", packed["L2w_ffb"])[:, :], "l2w")
    arow = P.sbuf("l2arow", 1, F32, parts=1)
    P.dma("sync", arow[:, :], P.din("L2w_alpha", packed["L2w_alpha"])[:, :], "l2w")
    P.wait("vector", "l2w", P.semv["l2w"])
    P.wait("scalar", "l2w", P.semv["l2w"])
    P.wait("tensor", "l2w", P.semv["l2w"])
    P.cut()
    if P.env.get("_dbg"):
        P.dbg_dump("wf", wf[:, 0:8192], 8192)
    rows = e["rows"]
    acc = P.sbuf("l2acc", 8, F32)
    for ic in range(NC):
        P.v(lambda v, ic=ic: v.tensor_reduce(acc[:, ic:ic + 1], wf[:, ic * 1024:(ic + 1) * 1024],
                                             AX.X, ALU.add, apply_absolute_value=True))
    P.cut()
    if P.env.get("_dbg"):
        P.dbg_dump("acc", acc[:, 0:8], 8)
    P.t(lambda t: t.matmul(e["psA"][0:1, 0:8], e["ones_c32"][:, :], acc[:, 0:8], start=True, stop=True))
    P.cut()
    drow = rows[0:1, 0:2]
    P.v(lambda v: v.tensor_reduce(rows[0:1, 4:5], e["psA"][0:1, 0:8], AX.X, ALU.add))
    P.cut()
    P.s(lambda s: s.activation(drow[:, 0:1], rows[0:1, 4:5], AF.Copy, scale=0.7 / (1024.0 * 1024.0)))
    P.cut()
    P.v(lambda v: v.tensor_scalar(drow[:, 1:2], drow[:, 0:1], -1.0, None, ALU.mult))
    P.cut()
    P.t(lambda t: t.matmul(e["psB"][:, 0:2], e["ones_r32"][:, :], drow[:, 0:2], start=True, stop=True))
    P.t(lambda t: t.matmul(e["psB"][:, 2:3], e["ones_r32"][:, :], arow[:, 0:1], start=True, stop=True))
    P.cut()
    dcols = P.sbuf("l2dcols", 4, F32, top=True)
    e["l2alpha"] = dcols
    P.v(lambda v: v.tensor_copy(dcols[:, 0:3], e["psB"][:, 0:3]))
    P.cut()
    tern = P.sbuf("l2tern", NC * 1024, BF16, top=True)
    e["l2tern"] = tern
    t0 = P.sbuf("l2t0", 1024, BF16)
    t1 = P.sbuf("l2t1", 1024, BF16)
    for ic in range(NC):
        P.v(lambda v, ic=ic: v.tensor_scalar(t0[:, :], wf[:, ic * 1024:(ic + 1) * 1024],
                                             dcols[:, 0:1], None, ALU.is_gt))
        P.v(lambda v, ic=ic: v.tensor_scalar(t1[:, :], wf[:, ic * 1024:(ic + 1) * 1024],
                                             dcols[:, 1:2], None, ALU.is_lt))
        P.cut()
        P.v(lambda v, ic=ic: v.tensor_tensor(tern[:, ic * 1024:(ic + 1) * 1024], t0[:, :], t1[:, :],
                                             ALU.subtract))
        P.cut()
    P.sfree("l2t0")
    P.sfree("l2t1")
    P.sfree("l2acc")
    P.sfree("l2wf")


def emit_l2_ffn(P, packed, first):
    e = P.env
    emit_ln_v2(P)
    if P.env.get("_dbg"):
        P.dbg_dump("XH", e["XH"][:, 0:16384], 16384, dtype=BF16)
    if first:
        emit_l2_quant(P, packed)
    if P.env.get("_dbg"):
        P.dbg_dump("tern", e["l2tern"][:, 0:8192], 8192, dtype=BF16)
        P.dbg_dump("dcols", e["l2alpha"][:, 0:4], 4)
    tern, dcols, bias = e["l2tern"], e["l2alpha"], e["l2bias"]
    X, XH = e["X"], e["XH"]

    def evac(oc, ps):
        P.v(lambda v: v.tensor_scalar(e["nscr"][:, :], ps[:, :],
                                      dcols[:, 2:3], bias[:, oc:oc + 1], ALU.mult, ALU.add))
        P.v(lambda v: v.scalar_tensor_tensor(X[:, oc * T:(oc + 1) * T], e["nscr"][:, :], 0.5,
                                             X[:, oc * T:(oc + 1) * T], ALU.mult, ALU.add))

    class TW:
        def lhsT(self, ic, o0, ow=128):
            return tern[:, ic * 1024 + o0: ic * 1024 + o0 + ow]

    mm_fm(P, TW(), XH, 0, NC, NC, evac)



def emit_l2_attn(P, packed):
    e = P.env
    psA, psB, rows = e["psA"], e["psB"], e["rows"]
    emit_ln_v2(P)
    park_x(P)
    XH = e["XH"]
    Q = P.sbuf("l2q", 16384, BF16)
    Kb = P.sbuf("l2k", 16384, BF16)
    V = P.sbuf("l2v", 16384, BF16)

    wq = WFull(P, "l2wq", packed["L2w_wq"])

    def evq(oc, ps):
        P.s(lambda s: s.activation(Q[:, oc * T:(oc + 1) * T], ps[:, :], AF.Copy, scale=0.03125))
    mm_fm(P, wq, XH, 0, NC, NC, evq, wait_sem=wq.sem, wait_val=wq.ready)
    wq.free()
    wk = WFull(P, "l2wk", packed["L2w_wk"])

    def evk(oc, ps):
        P.v(lambda v: v.tensor_copy(Kb[:, oc * T:(oc + 1) * T], ps[:, :]))
    mm_fm(P, wk, XH, 0, NC, NC, evk, wait_sem=wk.sem, wait_val=wk.ready)
    wk.free()

    wv = WFull(P, "l2wv", packed["L2w_wv"])
    P.wait("tensor", wv.sem, wv.ready)
    for tt in range(17):
        if tt < 16:
            ps = psA if tt % 2 == 0 else psB
            for ic in range(NC):
                for j in range(2):
                    P.t(lambda t, tt=tt, ic=ic, j=j, ps=ps: t.matmul(
                        ps[:, j * 512:(j + 1) * 512],
                        XH[:, ic * T + tt * 128: ic * T + tt * 128 + 128],
                        wv.sb[:, ic * 1024 + j * 512: ic * 1024 + (j + 1) * 512],
                        start=(ic == 0), stop=(ic == NC - 1)))
        if tt > 0:
            psp = psA if (tt - 1) % 2 == 0 else psB
            P.v(lambda v, tt=tt, psp=psp: v.tensor_copy(V[:, (tt - 1) * 1024:tt * 1024], psp[:, 0:1024]))
        P.cut()
    wv.free()
    free_xh(P)
    Yatt = P.sbuf("l2yatt", 16384, BF16)
    cells = P.sbuf("l2cells", 8, F32)

    sq0 = P.sbuf("l2sq0", 2048, BF16)
    sq1 = P.sbuf("l2sq1", 2048, BF16)
    for src, psrow in ((Q, psA), (Kb, psB)):
        for c in range(NC + 1):
            if c < NC:
                dst = sq0 if c % 2 == 0 else sq1
                P.s(lambda s, c=c, src=src, dst=dst: s.activation(
                    dst[:, :], src[:, c * T:(c + 1) * T], AF.Square))
            if c > 0:
                sb = sq0 if (c - 1) % 2 == 0 else sq1
                for j in range(NT):
                    P.t(lambda t, j=j, sb=sb, psrow=psrow, c=c: t.matmul(
                        psrow[0:1, j * 512:(j + 1) * 512], e["ones_cb"][:, :],
                        sb[:, j * 512:(j + 1) * 512],
                        start=(c - 1 == 0), stop=(c - 1 == NC - 1)))
            P.cut()
    P.sfree("l2sq0")
    P.sfree("l2sq1")
    qn = rows[0:1, 0:T]
    kn2 = rows[0:1, T:2 * T]
    P.s(lambda s: s.activation(qn, psA[0:1, 0:T], AF.Sqrt))
    P.s(lambda s: s.activation(kn2, psB[0:1, 0:T], AF.Copy))
    P.cut()
    P.v(lambda v: v.tensor_reduce(cells[0:1, 0:1], kn2, AX.X, ALU.max))
    P.cut()
    P.s(lambda s: s.activation(cells[0:1, 1:2], cells[0:1, 0:1], AF.Sqrt))
    P.cut()
    P.v(lambda v: v.tensor_scalar(cells[0:1, 2:3], cells[0:1, 1:2], -1.0, None, ALU.mult))
    P.cut()
    P.v(lambda v: v.tensor_scalar(kn2, qn, cells[0:1, 2:3], None, ALU.mult))
    P.cut()
    for j in range(NT):
        P.t(lambda t, j=j: t.matmul(psA[:, j * 512:(j + 1) * 512], e["ones_r32"][:, :],
                                    kn2[:, j * 512:(j + 1) * 512], start=True, stop=True))
    P.cut()
    P.v(lambda v: v.tensor_copy(e["bc0"][:, :], psA[:, :]))
    P.cut()

    P_all = P.sbuf("l2pall", 16 * 256, BF16)
    s_scr = P.sbuf("l2sscr", 512, F32)
    for qt in range(8):
        nkc = 2 * qt + 2
        q0 = qt * 256
        for j in range(nkc + 3):
            if j < nkc:
                sps = psA if j % 2 == 0 else psB
                for ic in range(NC):
                    P.t(lambda t, j=j, ic=ic, sps=sps: t.matmul(
                        sps[:, 512:768],
                        Kb[:, ic * T + j * 128: ic * T + j * 128 + 128],
                        Q[:, ic * T + q0: ic * T + q0 + 256],
                        start=(ic == 0), stop=(ic == NC - 1)))
            if 0 <= j - 1 < nkc:
                spsp = psA if (j - 1) % 2 == 0 else psB
                P.v(lambda v, j=j, spsp=spsp: v.tensor_tensor(
                    s_scr[:, ((j - 1) % 2) * 256:((j - 1) % 2) * 256 + 256],
                    spsp[:, 512:768], e["bc0"][:, q0:q0 + 256], ALU.add))
            if 0 <= j - 2 < nkc:
                kc = j - 2
                P.s(lambda s, kc=kc: s.activation(
                    P_all[:, kc * 256:(kc + 1) * 256],
                    s_scr[:, (kc % 2) * 256:(kc % 2) * 256 + 256], AF.Exp))
            if 0 <= j - 3 < nkc:
                kc = j - 3
                if kc == 2 * qt:
                    P.v(lambda v, kc=kc: v.tensor_tensor(
                        P_all[:, kc * 256:kc * 256 + 128],
                        P_all[:, kc * 256:kc * 256 + 128], e["tril"][:, :], ALU.mult))
                elif kc == 2 * qt + 1:
                    P.v(lambda v, kc=kc: v.memset(P_all[:, kc * 256:kc * 256 + 128], 0.0))
                    P.v(lambda v, kc=kc: v.tensor_tensor(
                        P_all[:, kc * 256 + 128:kc * 256 + 256],
                        P_all[:, kc * 256 + 128:kc * 256 + 256], e["tril"][:, :], ALU.mult))
            P.cut()
        for kc in range(nkc):
            P.t(lambda t, kc=kc: t.matmul(psA[0:1, 1536:1792], e["ones_cb"][:, :],
                                          P_all[:, kc * 256:(kc + 1) * 256],
                                          start=(kc == 0), stop=(kc == nkc - 1)))
        P.cut()
        lrow = rows[0:1, 2048:2304]
        P.s(lambda s: s.activation(lrow, psA[0:1, 1536:1792], AF.Copy))
        P.cut()
        P.v(lambda v: v.reciprocal(lrow, lrow))
        P.cut()
        P.t(lambda t: t.matmul(psB[:, 768:1024], e["ones_r32"][:, :], lrow, start=True, stop=True))
        P.cut()
        P.v(lambda v: v.tensor_copy(e["bc1"][:, 0:256], psB[:, 768:1024]))
        P.cut()
        for z in range(9):
            if z < 8:
                ps = psA if z % 2 == 0 else psB
                for kc in range(nkc):
                    P.t(lambda t, z=z, kc=kc, ps=ps: t.matmul(
                        ps[:, 0:256],
                        V[:, kc * 1024 + z * 128: kc * 1024 + z * 128 + 128],
                        P_all[:, kc * 256:(kc + 1) * 256],
                        start=(kc == 0), stop=(kc == nkc - 1)))
            if z > 0:
                psp = psA if (z - 1) % 2 == 0 else psB
                P.v(lambda v, z=z, psp=psp: v.tensor_tensor(
                    Yatt[:, (z - 1) * T + q0:(z - 1) * T + q0 + 256],
                    psp[:, 0:256], e["bc1"][:, 0:256], ALU.mult))
            P.cut()
    P.sfree("l2pall")
    P.sfree("l2sscr")
    P.sfree("l2q")
    P.sfree("l2k")
    P.sfree("l2v")
    P.sfree("l2cells")

    unpark_x(P)
    X = e["X"]
    wo = WFull(P, "l2wo", packed["L2w_wo"])

    def evo(oc, ps):
        P.v(lambda v, oc=oc: v.tensor_tensor(X[:, oc * T:(oc + 1) * T], ps[:, :],
                                             X[:, oc * T:(oc + 1) * T], ALU.add))
    mm_fm(P, wo, Yatt, 0, NC, NC, evo, wait_sem=wo.sem, wait_val=wo.ready)
    wo.free()
    P.sfree("l2yatt")


SUBLAYER_SET = ["l0_ffn_a", "l0_attn", "l0_ffn_b", "l1_ffn_a", "l1_attn", "l1_ffn_b",
                "l2_ffn_a", "l2_attn", "l2_ffn_b", "l3_ffn_a", "l3_attn", "l3_ffn_b"]


def emit_sublayer(P, name, packed):
    if name == "l2_ffn_a":
        emit_l2_ffn(P, packed, first=True)
    elif name == "l2_ffn_b":
        emit_l2_ffn(P, packed, first="l2_ffn_a" not in P.env.get("_done", ()))
    elif name == "l2_attn":
        emit_l2_attn(P, packed)
    elif name in ("l1_ffn_a", "l1_ffn_b"):
        emit_l1_ffn(P, packed)
    elif name in ("l3_ffn_a", "l3_ffn_b"):
        emit_l3_ffn(P, packed)
    elif name == "l1_attn":
        emit_l1_attn(P, packed)
    elif name == "l3_attn":
        emit_l3_attn(P, packed)
    elif name in ("l0_ffn_a", "l0_ffn_b"):
        emit_l0_ffn(P, packed)
    elif name == "l0_attn":
        emit_l0_attn(P, packed)
    else:
        raise NotImplementedError(name)


def build_program(names, packed, dbg=False):
    P = Prog()
    P.env["_dbg"] = dbg
    setup_env(P, packed)
    done = []
    for nm in names:
        P.env["_done"] = tuple(done)
        emit_sublayer(P, nm, packed)
        done.append(nm)
    store_x(P)
    P.finish()
    return P


def run_launch(P, xs):
    in_maps = []
    for b in range(8):
        m = dict(P.inmap)
        m["xin"] = xs[b]
        in_maps.append(m)
    res = run_bass_kernel_spmd(P.nc, in_maps, core_ids=list(range(8)))
    return [r["xout"] for r in res.results]


LAUNCH_GROUPS = [[nm] for nm in SUBLAYER_SET]


def kernel(**inputs):
    packed = pack_all(inputs)
    x = np.asarray(inputs["L0_attn_x"])
    xs = []
    for b in range(8):
        xf = np.ascontiguousarray(x[b].T.astype(np.float32))
        xs.append(np.ascontiguousarray(xf.reshape(NC, 128, T).transpose(1, 0, 2).reshape(128, NC * T)))
    for grp in LAUNCH_GROUPS:
        packed["xin0"] = xs[0]
        P = build_program(grp, packed)
        xs = run_launch(P, xs)
    outs = []
    for b in range(8):
        o = xs[b].reshape(128, NC, T).transpose(2, 1, 0).reshape(T, D)
        outs.append(o)
    return np.stack(outs).astype(np.float32)


# version 03
# speedup vs baseline: 1.2206x; 1.2206x over previous
import contextlib
import math
from collections import defaultdict

import numpy as np
import ml_dtypes
import concourse.bass as bass
import concourse.mybir as mybir
from concourse.bass_utils import run_bass_kernel_spmd

F32 = mybir.dt.float32
BF16 = mybir.dt.bfloat16
AF = mybir.ActivationFunctionType
ALU = mybir.AluOpType
AX = mybir.AxisListType

PE_LIMIT = 110
ENG_LIMIT = 200
D = 1024
T = 2048
NC = D // 128
NT = T // 512


class Tile:
    __slots__ = ("base", "off", "cols")

    def __init__(self, base, off, cols):
        self.base = base
        self.off = off
        self.cols = cols

    def __getitem__(self, idx):
        ps, cs = idx
        a = cs.start if cs.start is not None else 0
        b = cs.stop if cs.stop is not None else self.cols
        assert 0 <= a <= b <= self.cols, (a, b, self.cols)
        return self.base[ps, self.off + a:self.off + b]


class Arena:
    ALIGN = 128

    def __init__(self, nbytes):
        self.free_list = [(0, nbytes)]
        self.named = {}
        self.hwm = 0

    def alloc(self, name, nbytes, top=False):
        nbytes = -(-nbytes // self.ALIGN) * self.ALIGN
        cands = [i for i, (off, ln) in enumerate(self.free_list) if ln >= nbytes]
        if top:
            cands.sort(key=lambda i: -(self.free_list[i][0] + self.free_list[i][1]))
        else:
            cands.sort(key=lambda i: (self.free_list[i][1], i))
        for i in cands[:1]:
            off, ln = self.free_list[i]
            if True:
                if top:
                    self.named[name] = (off + ln - nbytes, nbytes)
                    self.free_list[i] = (off, ln - nbytes)
                    if ln == nbytes:
                        self.free_list.pop(i)
                    return off + ln - nbytes
                self.named[name] = (off, nbytes)
                if ln == nbytes:
                    self.free_list.pop(i)
                else:
                    self.free_list[i] = (off + nbytes, ln - nbytes)
                self.hwm = max(self.hwm, off + nbytes)
                return off
        raise MemoryError(f"arena OOM allocating {name} ({nbytes}B); "
                          f"free={self.free_list} named={list(self.named)}")

    def free(self, name):
        off, nbytes = self.named.pop(name)
        self.free_list.append((off, nbytes))
        self.free_list.sort()
        out = []
        for off, ln in self.free_list:
            if out and out[-1][0] + out[-1][1] == off:
                out[-1] = (out[-1][0], out[-1][1] + ln)
            else:
                out.append((off, ln))
        self.free_list = out


class ChunkedX:

    def __init__(self, tiles):
        self.tiles = tiles

    def __getitem__(self, idx):
        ps, cs = idx
        a, b = cs.start or 0, cs.stop
        c = a // T
        assert b <= (c + 1) * T, (a, b)
        return self.tiles[c][ps, a - c * T: b - c * T]


class Prog:
    def __init__(self):
        self.nc = bass.Bass("TRN2", target_bir_lowering=False)
        self._blk = None
        self._cnt = defaultdict(int)
        self._sems = {}
        self.semv = defaultdict(int)
        self._sb = {}
        self._ps = {}
        self.dram = {}
        self.inmap = {}
        self.env = {}
        self._consts = {}

    def constf(self, val, dtype=F32):
        key = (dtype, float(val))
        if key not in self._consts:
            nm = "cstv%d" % len(self._consts)
            h = self.sbuf(nm, 1, dtype, top=True)
            self.v(lambda v: v.memset(h[:, :], val))
            self.cut()
            self.nc.const_aps.register(dtype, float(val), h[:, 0:1])
            self._consts[key] = h
        return self._consts[key]

    def din(self, name, arr):
        if name not in self.dram:
            dt = {np.dtype(np.float32): F32, np.dtype(ml_dtypes.bfloat16): BF16}[arr.dtype]
            t = self.nc.dram_tensor(name, list(arr.shape), dt, kind="ExternalInput")
            self.dram[name] = t.ap()
            self.inmap[name] = arr
        return self.dram[name]

    def dout(self, name, shape, dtype=F32):
        t = self.nc.dram_tensor(name, list(shape), dtype, kind="ExternalOutput")
        return t.ap()

    ARENA_BYTES = 206 * 1024

    def _arena(self):
        if not hasattr(self, "_arh"):
            h = self.nc.sbuf_tensor("arena", [128, self.ARENA_BYTES // 4], F32).__enter__()
            self._arh = {F32: h, BF16: h.bitcast(BF16)}
            self._aralloc = Arena(self.ARENA_BYTES)
        return self._aralloc

    def sbuf(self, name, cols, dtype, parts=128, top=False):
        assert name not in self._sb, name
        if parts != 128:
            cm = self.nc.sbuf_tensor(name, [parts, cols], dtype)
            h = cm.__enter__()
            self._sb[name] = "raw"
            return h
        isz = 4 if dtype == F32 else 2
        off = self._arena().alloc(name, cols * isz, top=top)
        self._sb[name] = "arena"
        return Tile(self._arh[dtype], off // isz, cols)

    def sfree(self, name):
        kind = self._sb.pop(name)
        if kind == "arena":
            self._aralloc.free(name)

    def psum(self, name, cols, dtype=F32, parts=128):
        cm = self.nc.psum_tensor(name, [parts, cols], dtype)
        h = cm.__enter__()
        self._ps[name] = (h, cm)
        return h

    def sem(self, name):
        if name not in self._sems:
            cm = self.nc.semaphore(name)
            self._sems[name] = (cm.__enter__(), cm)
        return self._sems[name][0]

    def bump(self, name, k):
        self.semv[name] += k
        return self.semv[name]

    def _ensure(self):
        if self._blk is None:
            self._blk = self.nc.Block()
            self._blk.__enter__()
            self._q = {k: [] for k in ("tensor", "vector", "scalar", "gpsimd", "sync")}

    def cut(self):
        if self._blk is not None:
            for eng, fs in self._q.items():
                if fs:
                    getattr(self._blk, eng)(lambda e, fs=fs: [f(e) for f in fs])
            self._blk.__exit__(None, None, None)
            self._blk = None
            self._cnt.clear()

    def _emit(self, eng, f, n):
        self._ensure()
        self._q[eng].append(f)
        self._cnt[eng] += n
        lim = PE_LIMIT if eng == "tensor" else ENG_LIMIT
        if self._cnt[eng] >= lim:
            self.cut()

    def t(self, f, n=1):
        self._emit("tensor", f, n)

    def v(self, f, n=1):
        self._emit("vector", f, n)

    def s(self, f, n=1):
        self._emit("scalar", f, n)

    def g(self, f, n=1):
        self._emit("gpsimd", f, n)

    def y(self, f, n=1):
        self._emit("sync", f, n)

    def dma(self, q, out, in_, semname, wait=None):
        s = self.sem(semname)
        c = self.bump(semname, 16)
        if wait is not None:
            wsem, wval = wait
            ws = self.sem(wsem)
            self._emit(q, lambda e: e.wait_ge(ws, wval), 1)
        self._emit(q, lambda e: e.dma_start(out=out, in_=in_).then_inc(s, 16), 1)
        return c

    def wait(self, eng, semname, val):
        s = self.sem(semname)
        self._emit(eng, lambda e: e.wait_ge(s, val), 1)

    def all_wait(self, semname, val):
        for eng in ("tensor", "vector", "scalar", "gpsimd", "sync"):
            self.wait(eng, semname, val)
        self.cut()

    def finish(self):
        self.cut()

    def dbg_dump(self, name, ap_f, cols, parts=128, dtype=F32):
        self.cut()
        o = self.dout("dbg_" + name, [parts, cols], dtype)
        self.dma("sync", o[:, :], ap_f, "dbgs")
        self.wait("sync", "dbgs", self.semv["dbgs"])
        self.cut()



def _fm_vec(v):
    v = np.asarray(v, np.float32)
    return np.ascontiguousarray(v.reshape(-1, 128).T)


def _io(W):
    return np.ascontiguousarray(np.asarray(W, np.float32))


def _oi(W):
    return np.ascontiguousarray(np.asarray(W, np.float32).T)


def make_consts():
    c = {}
    c["ident_f32"] = np.eye(128, dtype=np.float32)
    c["tril_f32"] = np.tril(np.ones((128, 128), dtype=np.float32)).T.copy()
    c["ones_col_f32"] = np.ones((128, 1), dtype=np.float32)
    c["ones_col_b16"] = np.ones((128, 1), dtype=ml_dtypes.bfloat16)
    c["ones_row_f32"] = np.ones((1, 128), dtype=np.float32)
    sel = np.zeros((16, 16 * 128), dtype=np.float32)
    for e in range(16):
        sel[e, e * 128:(e + 1) * 128] = 1.0
    c["sel16_f32"] = sel
    blk = np.zeros((128, 4), dtype=ml_dtypes.bfloat16)
    for g in range(4):
        blk[g * 32:(g + 1) * 32, g] = 1.0
    c["blk32_b16"] = blk
    i = np.arange(16, dtype=np.float32)
    inv_freq = 1.0 / 10000.0 ** (i / 256.0)
    t = np.arange(2048, dtype=np.float32)
    freqs = np.outer(t, inv_freq)
    cos = np.cos(freqs).astype(ml_dtypes.bfloat16)
    sin = np.sin(freqs).astype(ml_dtypes.bfloat16)
    p_idx = np.arange(128)
    j = p_idx % 32
    d_idx = j % 16
    sgn = np.where(j < 16, -1.0, 1.0).astype(np.float32)
    CK = cos.astype(np.float32).T[d_idx]
    SK = sin.astype(np.float32).T[d_idx] * sgn[:, None]
    c["rope"] = np.concatenate(
        [(CK * 0.125), (SK * 0.125), CK, SK], axis=1).astype(ml_dtypes.bfloat16)
    return c


def pack_all(inputs):
    A = {k: np.asarray(v) for k, v in inputs.items()}
    p = {}
    p.update({"c_" + k: v for k, v in make_consts().items()})
    p["L2w_wq"] = _io(A["L2_attn_Wq"])
    p["L2w_wk"] = _io(A["L2_attn_Wk"])
    p["L2w_wv"] = _io(A["L2_attn_Wv"])
    p["L2w_wo"] = _io(A["L2_attn_Wo"])
    p["L2w_ffw"] = _oi(A["L2_ffn_weight"])
    p["L2w_ffb"] = _fm_vec(A["L2_ffn_bias"])
    p["L2w_alpha"] = np.asarray(A["L2_ffn_alpha"], np.float32).reshape(1, 1)
    p["L1w_w1"] = _oi(A["L1_ffn_w1_w"])
    p["L1w_w2"] = _oi(A["L1_ffn_w2_w"])
    p["L1w_h1"] = _oi(A["L1_ffn_h1_w"])
    p["L1w_h2"] = _oi(A["L1_ffn_h2_w"])
    p["L1w_w3"] = _oi(A["L1_ffn_w3_w"])
    p["L1w_s1"] = _fm_vec(A["L1_ffn_w1_s"])
    p["L1w_s2"] = _fm_vec(A["L1_ffn_w2_s"])
    p["L1w_sh1"] = _fm_vec(A["L1_ffn_h1_s"])
    p["L1w_sh2"] = _fm_vec(A["L1_ffn_h2_s"])
    p["L1w_s3"] = _fm_vec(A["L1_ffn_w3_s"])
    p["L3w_rw"] = _io(A["L3_ffn_router_w"])
    p["L3w_rb"] = np.asarray(A["L3_ffn_router_b"], np.float32).reshape(1, 16)
    for ex in range(16):
        p[f"L3w_w1_{ex}"] = _io(A["L3_ffn_w1"][ex])
        p[f"L3w_w2_{ex}"] = _io(A["L3_ffn_w2"][ex])
    p["L3w_b1"] = np.concatenate([_fm_vec(A["L3_ffn_b1"][ex]) for ex in range(16)], axis=1)
    p["L3w_b2"] = np.ascontiguousarray(np.asarray(A["L3_ffn_b2"], np.float32))
    p["L1a_wq"] = _io(A["L1_attn_Wq"])
    p["L1a_wl"] = _io(A["L1_attn_Wl"])
    p["L1a_wk"] = _io(A["L1_attn_Wk"])
    p["L1a_wv"] = _io(A["L1_attn_Wv"])
    p["L1a_wo"] = _io(A["L1_attn_Wo"])
    p["L1a_bias"] = np.concatenate(
        [_fm_vec(A["L1_attn_bq"]), _fm_vec(A["L1_attn_bl"]), _fm_vec(A["L1_attn_bk"]),
         _fm_vec(A["L1_attn_bo"]), _fm_vec(A["L1_attn_bv"])], axis=1)
    p["L0f_sg"] = _oi(A["L0_ffn_sg"])
    p["L0f_su"] = _oi(A["L0_ffn_su"])
    p["L0f_sd"] = _oi(A["L0_ffn_sd"])
    for ex in range(7):
        p[f"L0f_rg_{ex}"] = _oi(A["L0_ffn_rg"][ex])
        p[f"L0f_ru_{ex}"] = _oi(A["L0_ffn_ru"][ex])
        p[f"L0f_rd_{ex}"] = _oi(A["L0_ffn_rd"][ex])
    p["L0f_wr"] = _oi(A["L0_ffn_wr"])
    p["L0f_rb"] = np.asarray(A["L0_ffn_rb"], np.float32).reshape(1, 7)
    p["L0a_wq"] = _io(np.asarray(A["L0_attn_Wq"], np.float32).T)
    p["L0a_wk"] = _io(np.asarray(A["L0_attn_Wk"], np.float32).T)
    p["L0a_wv"] = _oi(A["L0_attn_Wv"])
    wpj = _oi(A["L0_attn_Wproj"])
    il = np.empty_like(wpj)
    for k in range(8):
        il[:, (2 * k) * 128:(2 * k + 1) * 128] = wpj[:, k * 128:(k + 1) * 128]
        il[:, (2 * k + 1) * 128:(2 * k + 2) * 128] = wpj[:, (8 + k) * 128:(9 + k) * 128]
    p["L0a_wproj"] = np.ascontiguousarray(il)
    p["L0a_lam"] = np.concatenate(
        [np.asarray(A[f"L0_attn_lambda_{n}"], np.float32) for n in ("q1", "q2", "k1", "k2")]).reshape(1, 256)
    p["L0a_alpha"] = np.asarray(A["L0_attn_alpha"], np.float32).reshape(1, 1)
    wqkv = _oi(A["L3_attn_w_qkv"])
    p["L3a_wqk"] = np.ascontiguousarray(wqkv[:, 0:2048])
    p["L3a_wv"] = np.ascontiguousarray(wqkv[:, 2048:3072])
    p["L3a_wo"] = _oi(A["L3_attn_w_o"])
    bqkv = np.asarray(A["L3_attn_b_qkv"], np.float32)
    p["L3a_bias"] = np.concatenate(
        [_fm_vec(bqkv[0:1024]), _fm_vec(bqkv[1024:2048]), _fm_vec(bqkv[2048:3072]),
         _fm_vec(A["L3_attn_b_o"])], axis=1)
    return p



def setup_env(P, packed):
    kr = P.nc._kernel_sem_range
    base = kr.start + 2
    if P.nc._bir_kernel_barrier_sem is not None:
        base += 1
    base += len(P.nc._monotonic_sems)
    rng = range(base, kr.stop)
    P.g(lambda g: g.dma_reset(rng))
    P.g(lambda g: g.sem_clear(rng))
    P.cut()
    e = P.env
    e["X"] = ChunkedX([P.sbuf(f"X{c}", T, F32) for c in range(NC)])
    e["XH"] = P.sbuf("XH", 16384, BF16)
    e["rows"] = P.sbuf("rows", 4096, F32)
    e["bc0"] = P.sbuf("bc0", 2048, F32)
    e["bc1"] = P.sbuf("bc1", 2048, F32)
    e["nscr"] = P.sbuf("nscr", 2048, F32)
    e["psA"] = P.psum("psA", 2048, F32)
    e["psB"] = P.psum("psB", 2048, F32)
    e["ident"] = P.sbuf("ident", 128, F32)
    e["tril"] = P.sbuf("tril", 128, F32)
    e["ones_c32"] = P.sbuf("ones_c32", 1, F32)
    e["ones_cb"] = P.sbuf("ones_cb", 1, BF16)
    e["ones_r32"] = P.sbuf("ones_r32", 128, F32, parts=1)
    P.dma("sync", e["ident"][:, :], P.din("c_ident_f32", packed["c_ident_f32"])[:, :], "cst")
    P.dma("sync", e["tril"][:, :], P.din("c_tril_f32", packed["c_tril_f32"])[:, :], "cst")
    P.dma("sync", e["ones_c32"][:, :], P.din("c_ones_col_f32", packed["c_ones_col_f32"])[:, :], "cst")
    P.dma("sync", e["ones_cb"][:, :], P.din("c_ones_col_b16", packed["c_ones_col_b16"])[:, :], "cst")
    P.dma("sync", e["ones_r32"][:, :], P.din("c_ones_row_f32", packed["c_ones_row_f32"])[:, :], "cst")
    xin = P.dout("xout", [128, 16384], F32)
    e["xout"] = xin
    xdram = P.din("xin", packed["xin0"])
    for c in range(NC):
        P.dma("sync", e["X"][:, c * T:(c + 1) * T], xdram[:, c * T:(c + 1) * T], "cst")
    P.all_wait("cst", P.semv["cst"])


def store_x(P):
    e = P.env
    P.cut()
    for c in range(NC):
        P.dma("sync", e["xout"][:, c * T:(c + 1) * T], P.env["X"][:, c * T:(c + 1) * T], "xst")
    P.wait("sync", "xst", P.semv["xst"])
    P.cut()


def park_x(P):
    e = P.env
    if "xpark" not in e:
        e["xpark"] = P.nc.dram_tensor("xpark", [128, 16384], F32).ap()
    P.cut()
    for c in range(NC):
        P.dma("sync", e["xpark"][:, c * T:(c + 1) * T], e["X"][:, c * T:(c + 1) * T], "xpk")
    P.all_wait("xpk", P.semv["xpk"])
    for c in range(NC):
        P.sfree(f"X{c}")
    e["X"] = None


def unpark_x(P):
    e = P.env
    e["X"] = ChunkedX([P.sbuf(f"X{c}", T, F32) for c in range(NC)])
    P.cut()
    for c in range(NC):
        P.dma("sync", e["X"][:, c * T:(c + 1) * T], e["xpark"][:, c * T:(c + 1) * T], "xup")
    P.all_wait("xup", P.semv["xup"])


def free_xh(P):
    P.sfree("XH")
    P.env["XH"] = None


def emit_ln_v2(P):
    e = P.env
    if e.get("XH") is None:
        e["XH"] = P.sbuf("XH", 16384, BF16)
    X, XH = e["X"], e["XH"]
    rows, psA, psB = e["rows"], e["psA"], e["psB"]
    XB = P.sbuf("XB", 16384, BF16)
    for c in range(NC):
        P.v(lambda v, c=c: v.tensor_copy(XB[:, c * T:(c + 1) * T], X[:, c * T:(c + 1) * T]))
    P.cut()
    for c in range(NC):
        for j in range(NT):
            P.t(lambda t, c=c, j=j: t.matmul(
                psA[0:1, j * 512:(j + 1) * 512],
                e["ones_cb"][:, :],
                XB[:, c * T + j * 512: c * T + (j + 1) * 512],
                start=(c == 0), stop=(c == NC - 1)))
    P.cut()
    for c in range(NC):
        P.s(lambda s, c=c: s.activation(XB[:, c * T:(c + 1) * T], XB[:, c * T:(c + 1) * T], AF.Square))
    P.cut()
    for c in range(NC):
        for j in range(NT):
            P.t(lambda t, c=c, j=j: t.matmul(
                psB[0:1, j * 512:(j + 1) * 512],
                e["ones_cb"][:, :],
                XB[:, c * T + j * 512: c * T + (j + 1) * 512],
                start=(c == 0), stop=(c == NC - 1)))
    P.cut()
    mu = rows[0:1, 0:T]
    w = rows[0:1, T:2 * T]
    P.s(lambda s: s.activation(mu, psA[0:1, 0:T], AF.Copy, scale=1.0 / 1024.0))
    P.cut()
    P.v(lambda v: v.scalar_tensor_tensor(w, mu, -1.0, mu, ALU.mult, ALU.mult))
    P.cut()
    P.v(lambda v: v.scalar_tensor_tensor(w, psB[0:1, 0:T], 1.0 / 1024.0, w, ALU.mult, ALU.add))
    P.cut()
    P.constf(1e-5)
    P.s(lambda s: s.activation(w, w, AF.Sqrt, bias=1e-5))
    P.cut()
    P.v(lambda v: v.reciprocal(w, w))
    P.cut()
    P.v(lambda v: v.scalar_tensor_tensor(mu, mu, -1.0, w, ALU.mult, ALU.mult))
    P.cut()
    for j in range(NT):
        P.t(lambda t, j=j: t.matmul(psA[:, j * 512:(j + 1) * 512], e["ones_r32"][:, :],
                                    w[:, j * 512:(j + 1) * 512], start=True, stop=True))
        P.t(lambda t, j=j: t.matmul(psB[:, j * 512:(j + 1) * 512], e["ones_r32"][:, :],
                                    mu[:, j * 512:(j + 1) * 512], start=True, stop=True))
    P.cut()
    P.v(lambda v: v.tensor_copy(e["bc0"][:, :], psA[:, :]))
    P.v(lambda v: v.tensor_copy(e["bc1"][:, :], psB[:, :]))
    P.cut()
    for c in range(NC):
        P.v(lambda v, c=c: v.tensor_tensor(e["nscr"][:, :], X[:, c * T:(c + 1) * T], e["bc0"][:, :], ALU.mult))
        P.v(lambda v, c=c: v.tensor_tensor(XH[:, c * T:(c + 1) * T], e["nscr"][:, :], e["bc1"][:, :], ALU.add))
    P.cut()
    P.sfree("XB")


class WFull:

    def __init__(self, P, name, arr, sem=None):
        self.P = P
        self.I, self.O = arr.shape
        self.name = name
        self.sem = sem or ("w_" + name)
        self.dram = P.din("W_" + name, arr)
        self.sb = P.sbuf("wsb_" + name, (self.I // 128) * self.O, BF16)
        P.dma("gpsimd", self.sb[:, :],
              self.dram.rearrange("(c p) o -> p c o", p=128), self.sem)
        self.ready = P.semv[self.sem]

    def lhsT(self, ic, o0, owidth=128):
        return self.sb[:, ic * self.O + o0: ic * self.O + o0 + owidth]

    def free(self):
        self.P.sfree("wsb_" + self.name)


def mm_fm(P, W, rhs_sb, rhs_base, n_i, n_o, evac, tcols=T, wait_sem=None, wait_val=None):
    e = P.env
    nt = tcols // 512
    for oc in range(n_o + 1):
        if oc < n_o:
            ps = e["psA"] if oc % 2 == 0 else e["psB"]
            if oc == 0 and wait_sem is not None:
                P.wait("tensor", wait_sem, wait_val)
            for ic in range(n_i):
                for j in range(nt):
                    P.t(lambda t, oc=oc, ic=ic, j=j, ps=ps: t.matmul(
                        ps[:, j * 512:(j + 1) * 512],
                        W.lhsT(ic, oc * 128),
                        rhs_sb[:, rhs_base + ic * tcols + j * 512: rhs_base + ic * tcols + (j + 1) * 512],
                        start=(ic == 0), stop=(ic == n_i - 1)))
        if oc > 0:
            psp = e["psA"] if (oc - 1) % 2 == 0 else e["psB"]
            evac(oc - 1, psp)
        P.cut()


def mm_stream(P, name, arr, rhs_sb, rhs_off, rhs_stride, tcols, evac, PW=512,
              psum_cols=None):
    e = P.env
    I, O = arr.shape
    assert O % PW == 0, (O, PW)
    nic, nop, opp = I // 128, O // 128, PW // 128
    npan = O // PW
    bname = ["wp_0", "wp_1"]
    bufs = [P.sbuf(bname[s], nic * PW, BF16) for s in (0, 1)]
    sems = ["ws_0", "ws_1"]
    dram = P.din("W_" + name, arr)
    panel_sem = {}
    pe_done = {}

    dram3 = dram.rearrange("(c p) o -> p c o", p=128)

    def issue(p):
        s = p % 2
        if p >= 2:
            P.wait("gpsimd", "pep", pe_done[p - 2])
        panel_sem[p] = P.dma("gpsimd", bufs[s][:, :],
                             dram3[:, :, p * PW:(p + 1) * PW], sems[s])
    issue(0)
    if npan > 1:
        issue(1)
    nt = tcols // 512
    pcols = psum_cols if psum_cols is not None else tcols
    pep = P.sem("pep")
    for oc in range(nop + 1):
        if oc < nop:
            p, s, o0 = oc // opp, (oc // opp) % 2, (oc % opp) * 128
            ps = e["psA"] if oc % 2 == 0 else e["psB"]
            if oc % opp == 0:
                P.wait("tensor", sems[s], panel_sem[p])
            for ic in range(nic):
                for j in range(nt):
                    last = (ic == nic - 1 and j == nt - 1 and
                            (oc % opp == opp - 1 or oc == nop - 1))

                    def mk(oc=oc, ic=ic, j=j, ps=ps, s=s, o0=o0, last=last):
                        def f(t):
                            if callable(rhs_sb):
                                rhs = rhs_sb(ic, j)
                            else:
                                rhs = rhs_sb[:, rhs_off + ic * rhs_stride + j * 512:
                                             rhs_off + ic * rhs_stride + min((j + 1) * 512, tcols)]
                            ins = t.matmul(
                                ps[:, j * 512:min((j + 1) * 512, pcols)],
                                bufs[s][:, ic * PW + o0: ic * PW + o0 + 128],
                                rhs,
                                start=(ic == 0), stop=(ic == nic - 1))
                            if last:
                                ins.then_inc(pep, 1)
                        return f
                    P.t(mk())
            if oc % opp == opp - 1 or oc == nop - 1:
                pe_done[p] = P.bump("pep", 1)
                if p + 2 < npan:
                    issue(p + 2)
        if oc > 0:
            psp = e["psA"] if (oc - 1) % 2 == 0 else e["psB"]
            evac(oc - 1, psp)
        P.cut()
    P.sfree(bname[0])
    P.sfree(bname[1])


def emit_l1_ffn(P, packed):
    e = P.env
    emit_ln_v2(P)
    park_x(P)
    XH = e["XH"]
    if "l1scales" not in P._sb:
        sc = P.sbuf("l1scales", 104, F32, top=True)
        sh2 = P.sbuf("l1sh2", 32, F32, top=True)
        e["l1scales"], e["l1sh2"] = sc, sh2
        P.dma("sync", sc[:, 0:32], P.din("L1w_s1", packed["L1w_s1"])[:, :], "l1sc")
        P.dma("sync", sc[:, 32:64], P.din("L1w_s2", packed["L1w_s2"])[:, :], "l1sc")
        P.dma("sync", sc[:, 64:96], P.din("L1w_sh1", packed["L1w_sh1"])[:, :], "l1sc")
        P.dma("sync", sc[:, 96:104], P.din("L1w_s3", packed["L1w_s3"])[:, :], "l1sc")
        P.dma("sync", sh2[:, :], P.din("L1w_sh2", packed["L1w_sh2"])[:, :], "l1sc")
        P.wait("vector", "l1sc", P.semv["l1sc"])
        P.wait("scalar", "l1sc", P.semv["l1sc"])
        P.cut()
    sc, sh2 = e["l1scales"], e["l1sh2"]
    t1 = P.sbuf("l1t1", 16384, BF16)
    t2 = P.sbuf("l1t2", 16384, BF16)
    yv = P.sbuf("l1y", 4096, F32)
    TQ = 512
    for qq in range(4):
        def ev1(oc, ps):
            P.s(lambda s, oc=oc: s.activation(t1[:, oc * TQ:(oc + 1) * TQ], ps[:, 0:TQ],
                                              AF.Silu, scale=sc[:, oc:oc + 1]))
        mm_stream(P, "l1w1", packed["L1w_w1"], XH, qq * TQ, T, TQ, ev1, PW=512)
        def ev2(oc, ps):
            P.v(lambda v, oc=oc: v.tensor_scalar(e["nscr"][:, 0:TQ], ps[:, 0:TQ],
                                                 sc[:, 32 + oc:33 + oc], None, ALU.mult))
            P.v(lambda v, oc=oc: v.tensor_tensor(t1[:, oc * TQ:(oc + 1) * TQ],
                                                 e["nscr"][:, 0:TQ],
                                                 t1[:, oc * TQ:(oc + 1) * TQ], ALU.mult))
        mm_stream(P, "l1w2", packed["L1w_w2"], XH, qq * TQ, T, TQ, ev2, PW=512)
        def ev3(oc, ps):
            P.s(lambda s, oc=oc: s.activation(t2[:, oc * TQ:(oc + 1) * TQ], ps[:, 0:TQ],
                                              AF.Gelu, scale=sc[:, 64 + oc:65 + oc]))
        mm_stream(P, "l1h1", packed["L1w_h1"], t1, 0, TQ, TQ, ev3, PW=128)
        def ev4(oc, ps):
            P.s(lambda s, oc=oc: s.activation(t1[:, oc * TQ:(oc + 1) * TQ], ps[:, 0:TQ],
                                              AF.Gelu, scale=sh2[:, oc:oc + 1]))
        mm_stream(P, "l1h2", packed["L1w_h2"], t2, 0, TQ, TQ, ev4, PW=128)
        def ev5(oc, ps):
            P.v(lambda v, oc=oc: v.tensor_scalar(yv[:, oc * TQ:(oc + 1) * TQ], ps[:, 0:TQ],
                                                 sc[:, 96 + oc:97 + oc], None, ALU.mult))
        mm_stream(P, "l1w3", packed["L1w_w3"], t1, 0, TQ, TQ, ev5, PW=128)
        P.cut()
        for c in range(NC):
            sl = e["nscr"][:, (c % 2) * 512:(c % 2) * 512 + 512]
            v1 = P.dma("sync", sl, e["xpark"][:, c * T + qq * TQ: c * T + qq * TQ + TQ], "xq")
            P.wait("vector", "xq", v1)
            P.v(lambda v, c=c, sl=sl: v.scalar_tensor_tensor(
                sl, yv[:, c * TQ:(c + 1) * TQ], 0.5, sl, ALU.mult, ALU.add))
            P.cut()
            P.dma("sync", e["xpark"][:, c * T + qq * TQ: c * T + qq * TQ + TQ], sl, "xqs")
        P.wait("sync", "xqs", P.semv["xqs"])
        P.cut()
    P.sfree("l1t1")
    P.sfree("l1t2")
    P.sfree("l1y")
    unpark_x(P)


def emit_l3_ffn(P, packed):
    e = P.env
    psA, psB, rows = P.env["psA"], P.env["psB"], P.env["rows"]
    emit_ln_v2(P)
    park_x(P)
    XH = e["XH"]
    if "l3rw" not in P._sb:
        rw = P.sbuf("l3rw", 128, BF16, top=True)
        rbr = P.sbuf("l3rb", 16, F32, top=True)
        b1c = P.sbuf("l3b1", 256, F32, top=True)
        b2c = P.sbuf("l3b2", 1024, BF16, top=True)
        e["l3rw"], e["l3rb"], e["l3b1"], e["l3b2"] = rw, rbr, b1c, b2c
        P.dma("gpsimd", rw[:, :], P.din("L3w_rw", packed["L3w_rw"]).rearrange("(c p) e -> p c e", p=128), "l3c")
        P.dma("sync", rbr[0:1, :], P.din("L3w_rb", packed["L3w_rb"])[:, :], "l3c")
        P.dma("sync", b1c[:, :], P.din("L3w_b1", packed["L3w_b1"])[:, :], "l3c")
        P.dma("gpsimd", b2c[0:16, :], P.din("L3w_b2", packed["L3w_b2"])[:, :], "l3c")
        P.all_wait("l3c", P.semv["l3c"])
    rw, rbr, b1c, b2c = e["l3rw"], e["l3rb"], e["l3b1"], e["l3b2"]
    sel = ensure_sel(P, packed)

    logit = P.sbuf("l3lg", 256, F32)
    P.t(lambda t: t.matmul(psB[:, 32:48], e["ones_r32"][:, :], rbr[0:1, :], start=True, stop=True))
    P.cut()
    P.v(lambda v: v.tensor_copy(e["bc1"][:, 0:16], psB[:, 32:48]))
    P.cut()
    for tt in range(17):
        if tt < 16:
            ps = psA if tt % 2 == 0 else psB
            for ic in range(NC):
                P.t(lambda t, tt=tt, ic=ic, ps=ps: t.matmul(
                    ps[:, 0:16],
                    XH[:, ic * T + tt * 128: ic * T + tt * 128 + 128],
                    rw[:, ic * 16:(ic + 1) * 16],
                    start=(ic == 0), stop=(ic == NC - 1)))
        if tt > 0:
            psp = psA if (tt - 1) % 2 == 0 else psB
            P.v(lambda v, tt=tt, psp=psp: v.tensor_tensor(
                logit[:, (tt - 1) * 16:tt * 16], psp[:, 0:16], e["bc1"][:, 0:16], ALU.add))
        P.cut()

    pbuf = P.sbuf("l3p", 256, F32)
    mcol = P.sbuf("l3m", 16, F32)
    dsum = P.sbuf("l3d", 16, F32)
    keep = P.sbuf("l3keep", 256, F32)
    for tt in range(16):
        P.v(lambda v, tt=tt: v.tensor_reduce(mcol[:, tt:tt + 1], logit[:, tt * 16:(tt + 1) * 16],
                                             AX.X, ALU.max, negate=True))
    P.cut()
    for tt in range(16):
        P.s(lambda s, tt=tt: s.activation(pbuf[:, tt * 16:(tt + 1) * 16],
                                          logit[:, tt * 16:(tt + 1) * 16],
                                          AF.Exp, bias=mcol[:, tt:tt + 1]))
    P.cut()
    P.v(lambda v: v.tensor_copy(logit[:, :], pbuf[:, :]))
    P.cut()
    for step in range(4):
        for tt in range(16):
            P.v(lambda v, tt=tt: v.tensor_reduce(mcol[:, tt:tt + 1], logit[:, tt * 16:(tt + 1) * 16],
                                                 AX.X, ALU.max))
        P.cut()
        if step == 0:
            P.v(lambda v: v.tensor_copy(dsum[:, :], mcol[:, :]))
        else:
            P.v(lambda v: v.tensor_tensor(dsum[:, :], dsum[:, :], mcol[:, :], ALU.add))
        if step < 3:
            for tt in range(16):
                P.v(lambda v, tt=tt: v.tensor_scalar(keep[:, tt * 16:(tt + 1) * 16],
                                                     logit[:, tt * 16:(tt + 1) * 16],
                                                     mcol[:, tt:tt + 1], None, ALU.is_lt))
            for tt in range(16):
                P.v(lambda v, tt=tt: v.tensor_tensor(logit[:, tt * 16:(tt + 1) * 16],
                                                     logit[:, tt * 16:(tt + 1) * 16],
                                                     keep[:, tt * 16:(tt + 1) * 16], ALU.mult))
        P.cut()
    for tt in range(16):
        P.v(lambda v, tt=tt: v.tensor_scalar(keep[:, tt * 16:(tt + 1) * 16],
                                             pbuf[:, tt * 16:(tt + 1) * 16],
                                             mcol[:, tt:tt + 1], None, ALU.is_ge))
    P.v(lambda v: v.reciprocal(dsum[:, :], dsum[:, :]))
    P.cut()
    for tt in range(16):
        P.v(lambda v, tt=tt: v.tensor_scalar(pbuf[:, tt * 16:(tt + 1) * 16],
                                             pbuf[:, tt * 16:(tt + 1) * 16],
                                             dsum[:, tt:tt + 1], None, ALU.mult))
    P.cut()
    for tt in range(16):
        P.v(lambda v, tt=tt: v.tensor_tensor(pbuf[:, tt * 16:(tt + 1) * 16],
                                             pbuf[:, tt * 16:(tt + 1) * 16],
                                             keep[:, tt * 16:(tt + 1) * 16], ALU.mult))
    P.cut()
    wrow = P.sbuf("l3wr", 2048, F32)
    wrowb = P.sbuf("l3wrb", 2048, BF16)
    for tt in range(16):
        P.t(lambda t, tt=tt: t.transpose(psA[0:16, tt * 128:(tt + 1) * 128],
                                         pbuf[:, tt * 16:(tt + 1) * 16], e["ident"][:, :]))
    P.cut()
    P.v(lambda v: v.tensor_copy(wrow[0:16, :], psA[0:16, 0:2048]))
    P.v(lambda v: v.tensor_copy(wrowb[0:16, :], psA[0:16, 0:2048]))
    P.cut()
    P.sfree("l3lg")
    P.sfree("l3p")
    P.sfree("l3m")
    P.sfree("l3d")
    P.sfree("l3keep")

    He = P.sbuf("l3he", 16384, BF16)
    ACC = P.sbuf("l3acc", 8192, F32)
    TH = 1024
    for half in range(2):
        q0 = half * TH
        for ex in range(16):
            P.t(lambda t, ex=ex: t.matmul(psB[:, 1024:1536], sel[0:16, ex * 128:(ex + 1) * 128],
                                          wrow[0:16, q0:q0 + 512], start=True, stop=True))
            P.t(lambda t, ex=ex: t.matmul(psB[:, 1536:2048], sel[0:16, ex * 128:(ex + 1) * 128],
                                          wrow[0:16, q0 + 512:q0 + TH], start=True, stop=True))
            P.cut()
            P.v(lambda v: v.tensor_copy(e["bc0"][:, 0:TH], psB[:, 1024:2048]))
            P.cut()

            def ev1(oc, ps, ex=ex):
                P.s(lambda s, oc=oc, ex=ex, ps=ps: s.activation(
                    He[:, oc * TH:(oc + 1) * TH], ps[:, 0:TH], AF.Gelu,
                    bias=b1c[:, ex * 16 + oc:ex * 16 + oc + 1]))
            mm_stream(P, f"l3w1_{ex}", packed[f"L3w_w1_{ex}"], XH, q0, T, TH, ev1, PW=512)

            def ev2(oc, ps, ex=ex):
                P.v(lambda v, oc=oc, ps=ps: v.tensor_tensor(
                    e["nscr"][:, 0:TH], ps[:, 0:TH], e["bc0"][:, 0:TH], ALU.mult))
                if ex == 0:
                    P.v(lambda v, oc=oc: v.tensor_copy(ACC[:, oc * TH:(oc + 1) * TH],
                                                       e["nscr"][:, 0:TH]))
                else:
                    P.v(lambda v, oc=oc: v.tensor_tensor(ACC[:, oc * TH:(oc + 1) * TH],
                                                         ACC[:, oc * TH:(oc + 1) * TH],
                                                         e["nscr"][:, 0:TH], ALU.add))
            mm_stream(P, f"l3w2_{ex}", packed[f"L3w_w2_{ex}"], He, 0, TH, TH, ev2, PW=512)
        for oc in range(NC):
            ps = psA if oc % 2 == 0 else psB
            for j in range(2):
                P.t(lambda t, oc=oc, j=j, ps=ps: t.matmul(
                    ps[:, j * 512:(j + 1) * 512],
                    b2c[0:16, oc * 128:(oc + 1) * 128],
                    wrowb[0:16, q0 + j * 512:q0 + (j + 1) * 512],
                    start=True, stop=True))
            P.cut()
            P.v(lambda v, oc=oc, ps=ps: v.tensor_tensor(
                ACC[:, oc * TH:(oc + 1) * TH], ACC[:, oc * TH:(oc + 1) * TH],
                ps[:, 0:TH], ALU.add))
            P.cut()
        for c in range(NC):
            sl = e["nscr"][:, (c % 2) * 1024:(c % 2) * 1024 + 1024]
            v1 = P.dma("sync", sl, e["xpark"][:, c * T + q0: c * T + q0 + TH], "xq")
            P.wait("vector", "xq", v1)
            P.v(lambda v, c=c, sl=sl: v.scalar_tensor_tensor(
                sl, ACC[:, c * TH:(c + 1) * TH], 0.5, sl, ALU.mult, ALU.add))
            P.cut()
            P.dma("sync", e["xpark"][:, c * T + q0: c * T + q0 + TH], sl, "xqs")
        P.wait("sync", "xqs", P.semv["xqs"])
        P.cut()
    P.sfree("l3he")
    P.sfree("l3acc")
    P.sfree("l3wr")
    P.sfree("l3wrb")
    unpark_x(P)


def attn_core_64(P, Q, Kb, v_fn, Yatt, bv_fn, sqs, s_scr, Pb, cells):
    e = P.env
    psA, psB, rows = e["psA"], e["psB"], e["rows"]
    for h in range(16):
        hc, hr = h // 2, (h % 2) * 64
        P.s(lambda s, hc=hc, hr=hr: s.activation(sqs[0:64, 0:T], Q[hr:hr + 64, hc * T:(hc + 1) * T], AF.Square))
        P.cut()
        P.t(lambda t: t.matmul(psA[0:1, 0:512], e["ones_cb"][0:64, 0:1], sqs[0:64, 0:512], start=True, stop=True))
        P.t(lambda t: t.matmul(psA[0:1, 512:1024], e["ones_cb"][0:64, 0:1], sqs[0:64, 512:1024], start=True, stop=True))
        P.t(lambda t: t.matmul(psA[0:1, 1024:1536], e["ones_cb"][0:64, 0:1], sqs[0:64, 1024:1536], start=True, stop=True))
        P.t(lambda t: t.matmul(psA[0:1, 1536:2048], e["ones_cb"][0:64, 0:1], sqs[0:64, 1536:2048], start=True, stop=True))
        P.cut()
        P.s(lambda s, hc=hc, hr=hr: s.activation(sqs[0:64, 0:T], Kb[hr:hr + 64, hc * T:(hc + 1) * T], AF.Square))
        P.s(lambda s: s.activation(rows[0:1, 0:T], psA[0:1, 0:T], AF.Sqrt))
        P.cut()
        P.t(lambda t: t.matmul(psB[0:1, 0:512], e["ones_cb"][0:64, 0:1], sqs[0:64, 0:512], start=True, stop=True))
        P.t(lambda t: t.matmul(psB[0:1, 512:1024], e["ones_cb"][0:64, 0:1], sqs[0:64, 512:1024], start=True, stop=True))
        P.t(lambda t: t.matmul(psB[0:1, 1024:1536], e["ones_cb"][0:64, 0:1], sqs[0:64, 1024:1536], start=True, stop=True))
        P.t(lambda t: t.matmul(psB[0:1, 1536:2048], e["ones_cb"][0:64, 0:1], sqs[0:64, 1536:2048], start=True, stop=True))
        P.cut()
        P.s(lambda s: s.activation(rows[0:1, T:2 * T], psB[0:1, 0:T], AF.Copy))
        P.cut()
        P.v(lambda v: v.tensor_reduce(cells[0:1, 0:1], rows[0:1, T:2 * T], AX.X, ALU.max))
        P.cut()
        P.s(lambda s: s.activation(cells[0:1, 1:2], cells[0:1, 0:1], AF.Sqrt))
        P.cut()
        P.v(lambda v: v.tensor_scalar(cells[0:1, 2:3], cells[0:1, 1:2], -1.0, None, ALU.mult))
        P.cut()
        P.v(lambda v: v.tensor_scalar(rows[0:1, T:2 * T], rows[0:1, 0:T],
                                      cells[0:1, 2:3], None, ALU.mult))
        P.cut()
        for j in range(NT):
            P.t(lambda t, j=j: t.matmul(psA[:, j * 512:(j + 1) * 512], e["ones_r32"][:, :],
                                        rows[0:1, T + j * 512:T + (j + 1) * 512], start=True, stop=True))
        P.cut()
        P.v(lambda v: v.tensor_copy(e["bc0"][:, :], psA[:, :]))
        P.cut()
        for qh in range(2):
            q0 = qh * 1024
            for j in range(19):
                if j < 16:
                    sps = psA if j % 2 == 0 else psB
                    for n in range(2):
                        P.t(lambda t, j=j, n=n, sps=sps, hc=hc, hr=hr: t.matmul(
                            sps[:, 1024 + n * 512:1024 + (n + 1) * 512],
                            Kb[hr:hr + 64, hc * T + j * 128: hc * T + j * 128 + 128],
                            Q[hr:hr + 64, hc * T + q0 + n * 512: hc * T + q0 + (n + 1) * 512],
                            start=True, stop=True))
                if 0 <= j - 1 < 16:
                    spsp = psA if (j - 1) % 2 == 0 else psB
                    P.v(lambda v, j=j, spsp=spsp: v.tensor_tensor(
                        s_scr[:, ((j - 1) % 2) * 1024:((j - 1) % 2) * 1024 + 1024],
                        spsp[:, 1024:2048], e["bc0"][:, q0:q0 + 1024], ALU.add))
                if 0 <= j - 2 < 16:
                    kc = j - 2
                    P.s(lambda s, kc=kc: s.activation(
                        Pb[:, (kc % 2) * 1024:(kc % 2) * 1024 + 1024],
                        s_scr[:, (kc % 2) * 1024:(kc % 2) * 1024 + 1024], AF.Exp))
                if 0 <= j - 3 < 16:
                    kc = j - 3
                    P.t(lambda t, kc=kc, h=h: t.matmul(
                        psA[0:65, 0:512],
                        v_fn(kc, h),
                        Pb[:, (kc % 2) * 1024:(kc % 2) * 1024 + 512],
                        start=(kc == 0), stop=(kc == 15)))
                    P.t(lambda t, kc=kc, h=h: t.matmul(
                        psA[0:65, 512:1024],
                        v_fn(kc, h),
                        Pb[:, (kc % 2) * 1024 + 512:(kc % 2) * 1024 + 1024],
                        start=(kc == 0), stop=(kc == 15)))
                P.cut()
            P.s(lambda s: s.activation(rows[0:1, 0:1024], psA[64:65, 0:1024], AF.Copy))
            P.cut()
            P.v(lambda v: v.reciprocal(rows[0:1, 0:1024], rows[0:1, 0:1024]))
            P.cut()
            P.t(lambda t: t.matmul(psB[0:64, 0:512], e["ones_r32"][0:1, 0:64],
                                   rows[0:1, 0:512], start=True, stop=True))
            P.t(lambda t: t.matmul(psB[0:64, 512:1024], e["ones_r32"][0:1, 0:64],
                                   rows[0:1, 512:1024], start=True, stop=True))
            P.cut()
            P.v(lambda v: v.tensor_copy(e["bc1"][0:64, 0:1024], psB[0:64, 0:1024]))
            P.cut()
            P.v(lambda v: v.tensor_tensor(s_scr[0:64, 0:1024], psA[0:64, 0:1024],
                                          e["bc1"][0:64, 0:1024], ALU.mult))
            P.v(lambda v, hc=hc, hr=hr, q0=q0: v.tensor_scalar(
                Yatt[hr:hr + 64, hc * T + q0: hc * T + q0 + 1024],
                s_scr[0:64, 0:1024], bv_fn(hr, hc), None, ALU.add))
            P.cut()


def emit_l1_attn(P, packed):
    e = P.env
    psA, psB, rows = e["psA"], e["psB"], e["rows"]
    emit_ln_v2(P)
    park_x(P)
    XH = e["XH"]
    bias = P.sbuf("l1abias", 48, F32, top=True)
    P.dma("sync", bias[:, 0:34], P.din("L1a_bias", packed["L1a_bias"])[:, :], "l1ab")
    P.wait("vector", "l1ab", P.semv["l1ab"])
    P.cut()
    P.v(lambda v: v.tensor_scalar(bias[:, 40:48], bias[:, 0:8], 0.125, None, ALU.mult))
    P.cut()
    BQ, BL, BK, BO, BV, BQS = 40, 8, 10, 18, 26, 40

    Q = P.sbuf("l1q", 16384, BF16)
    lat = P.sbuf("l1lat", 4096, BF16)
    wq = WFull(P, "l1wq", packed["L1a_wq"])

    def evq(oc, ps):
        P.v(lambda v, oc=oc: v.tensor_scalar(Q[:, oc * T:(oc + 1) * T], ps[:, :],
                                             0.125, bias[:, 40 + oc:41 + oc], ALU.mult, ALU.add))
    mm_fm(P, wq, XH, 0, NC, NC, evq, wait_sem=wq.sem, wait_val=wq.ready)
    wq.free()
    wl = WFull(P, "l1wl", packed["L1a_wl"])

    def evl(oc, ps):
        P.v(lambda v, oc=oc: v.tensor_scalar(lat[:, oc * T:(oc + 1) * T], ps[:, :],
                                             bias[:, 8 + oc:9 + oc], None, ALU.add))
    mm_fm(P, wl, XH, 0, NC, 2, evl, wait_sem=wl.sem, wait_val=wl.ready)
    wl.free()
    free_xh(P)
    Kb = P.sbuf("l1k", 16384, BF16)
    V = P.sbuf("l1v", 16 * 1040, BF16)
    Yatt = Q
    wk = WFull(P, "l1wk", packed["L1a_wk"])

    def evk(oc, ps):
        P.v(lambda v, oc=oc: v.tensor_scalar(Kb[:, oc * T:(oc + 1) * T], ps[:, :],
                                             bias[:, 10 + oc:11 + oc], None, ALU.add))
    mm_fm(P, wk, lat, 0, 2, NC, evk, wait_sem=wk.sem, wait_val=wk.ready)
    wk.free()
    P.v(lambda v: v.memset(V[:, :], 1.0))
    P.cut()
    wv = WFull(P, "l1wv", packed["L1a_wv"])
    P.wait("tensor", wv.sem, wv.ready)
    for tt in range(17):
        if tt < 16:
            ps = psA if tt % 2 == 0 else psB
            for ic in range(2):
                for j in range(2):
                    P.t(lambda t, tt=tt, ic=ic, j=j, ps=ps: t.matmul(
                        ps[:, j * 512:(j + 1) * 512],
                        lat[:, ic * T + tt * 128: ic * T + tt * 128 + 128],
                        wv.sb[:, ic * 1024 + j * 512: ic * 1024 + (j + 1) * 512],
                        start=(ic == 0), stop=(ic == 1)))
        if tt > 0:
            psp = psA if (tt - 1) % 2 == 0 else psB
            dst = V[:, (tt - 1) * 1040:tt * 1040].rearrange("p (h d) -> p h d", d=65)[:, :, 0:64]
            P.v(lambda v, psp=psp, dst=dst: v.tensor_copy(dst, psp[:, 0:1024]))
        P.cut()
    wv.free()
    P.sfree("l1lat")
    sqs = P.sbuf("l1sqs", 2048, BF16)
    s_scr = P.sbuf("l1ss", 2048, F32)
    Pb = P.sbuf("l1pb", 2048, BF16)
    cells = P.sbuf("l1cells", 8, F32)

    attn_core_64(P, Q, Kb, lambda kc, h: V[:, kc * 1040 + h * 65: kc * 1040 + h * 65 + 65],
                 Yatt, lambda hr, hc: bias[hr:hr + 64, 26 + hc:27 + hc],
                 sqs, s_scr, Pb, cells)
    P.sfree("l1k")
    P.sfree("l1v")
    P.sfree("l1sqs")
    P.sfree("l1ss")
    P.sfree("l1pb")
    P.sfree("l1cells")
    unpark_x(P)
    X = e["X"]
    wo = WFull(P, "l1wo", packed["L1a_wo"])

    def evo(oc, ps):
        P.v(lambda v, oc=oc: v.scalar_tensor_tensor(
            X[:, oc * T:(oc + 1) * T], ps[:, :], bias[:, 18 + oc:19 + oc],
            X[:, oc * T:(oc + 1) * T], ALU.add, ALU.add))
    mm_fm(P, wo, Yatt, 0, NC, NC, evo, wait_sem=wo.sem, wait_val=wo.ready)
    wo.free()
    P.sfree("l1q")


def emit_l3_attn(P, packed):
    e = P.env
    psA, psB = e["psA"], e["psB"]
    emit_ln_v2(P)
    park_x(P)
    XH = e["XH"]
    for bn in ("selw", "l3rw", "l3rb", "l3b1", "l3b2"):
        if bn in P._sb:
            P.sfree(bn)
            P.env.pop(bn, None)
    bias = P.sbuf("l3abias", 40, F32, top=True)
    P.dma("sync", bias[:, 0:32], P.din("L3a_bias", packed["L3a_bias"])[:, :], "l3ab")
    P.wait("vector", "l3ab", P.semv["l3ab"])
    P.cut()
    P.v(lambda v: v.tensor_scalar(bias[:, 32:40], bias[:, 0:8], 0.03125, None, ALU.mult))
    P.cut()
    Q = P.sbuf("l3q", 16384, BF16)
    Kb = P.sbuf("l3k", 16384, BF16)
    V0 = P.sbuf("l3v0", 16 * 520, BF16)
    V1 = P.sbuf("l3v1", 16 * 520, BF16)

    def evqk(oc, ps):
        if oc < 8:
            P.v(lambda v, oc=oc: v.tensor_scalar(Q[:, oc * T:(oc + 1) * T], ps[:, :],
                                                 0.03125, bias[:, 32 + oc:33 + oc],
                                                 ALU.mult, ALU.add))
        else:
            c = oc - 8
            P.v(lambda v, c=c: v.tensor_scalar(Kb[:, c * T:(c + 1) * T], ps[:, :],
                                               bias[:, 8 + c:9 + c], None, ALU.add))
    mm_stream(P, "l3a_wqk", packed["L3a_wqk"], XH, 0, T, T, evqk, PW=256)
    P.v(lambda v: v.memset(V0[:, :], 1.0))
    P.v(lambda v: v.memset(V1[:, :], 1.0))
    P.cut()
    wv = WFull(P, "l3awv", packed["L3a_wv"])
    P.wait("tensor", wv.sem, wv.ready)
    for tt in range(17):
        if tt < 16:
            ps = psA if tt % 2 == 0 else psB
            for ic in range(NC):
                for j in range(2):
                    P.t(lambda t, tt=tt, ic=ic, j=j, ps=ps: t.matmul(
                        ps[:, j * 512:(j + 1) * 512],
                        XH[:, ic * T + tt * 128: ic * T + tt * 128 + 128],
                        wv.sb[:, ic * 1024 + j * 512: ic * 1024 + (j + 1) * 512],
                        start=(ic == 0), stop=(ic == NC - 1)))
        if tt > 0:
            psp = psA if (tt - 1) % 2 == 0 else psB
            d0 = V0[:, (tt - 1) * 520:tt * 520].rearrange("p (h d) -> p h d", d=65)[:, :, 0:64]
            d1 = V1[:, (tt - 1) * 520:tt * 520].rearrange("p (h d) -> p h d", d=65)[:, :, 0:64]
            P.v(lambda v, psp=psp, d0=d0: v.tensor_copy(d0, psp[:, 0:512]))
            P.v(lambda v, psp=psp, d1=d1: v.tensor_copy(d1, psp[:, 512:1024]))
        P.cut()
    wv.free()
    sqs = P.sbuf("l3sqs", 2048, BF16)
    s_scr = P.sbuf("l3ss", 2048, F32)
    Pb = P.sbuf("l3pb", 2048, BF16)
    cells = P.sbuf("l3cells", 8, F32)
    def v_fn(kc, h):
        if h < 8:
            return V0[:, kc * 520 + h * 65: kc * 520 + h * 65 + 65]
        return V1[:, kc * 520 + (h - 8) * 65: kc * 520 + (h - 8) * 65 + 65]
    attn_core_64(P, Q, Kb, v_fn, Q, lambda hr, hc: bias[hr:hr + 64, 16 + hc:17 + hc],
                 sqs, s_scr, Pb, cells)
    P.sfree("l3k")
    P.sfree("l3v0")
    P.sfree("l3v1")
    P.sfree("l3sqs")
    P.sfree("l3pb")
    P.sfree("l3cells")
    unpark_x(P)
    X = e["X"]
    wo = WFull(P, "l3awo", packed["L3a_wo"])

    def evo(oc, ps):
        P.v(lambda v, oc=oc: v.scalar_tensor_tensor(
            s_scr[:, 0:T] if False else e["nscr"][:, :], ps[:, :], bias[:, 24 + oc:25 + oc],
            XH[:, oc * T:(oc + 1) * T], ALU.add, ALU.add))
        P.v(lambda v, oc=oc: v.tensor_tensor(X[:, oc * T:(oc + 1) * T], e["nscr"][:, :],
                                             X[:, oc * T:(oc + 1) * T], ALU.add))
    mm_fm(P, wo, Q, 0, NC, NC, evo, wait_sem=wo.sem, wait_val=wo.ready)
    wo.free()
    P.sfree("l3ss")
    P.sfree("l3q")


def emit_l0_attn(P, packed):
    e = P.env
    psA, psB, rows = e["psA"], e["psB"], e["rows"]
    emit_ln_v2(P)
    park_x(P)
    XH = e["XH"]
    if "selw" in P._sb:
        P.sfree("selw")
        P.env.pop("selw", None)
    lamr = P.sbuf("l0lam", 256, F32, top=True)
    alr = P.sbuf("l0al", 2, F32, top=True)
    cl = P.sbuf("l0cl", 16, F32, top=True)
    c1col = P.sbuf("l0c1c", 2, F32, top=True)
    blk = P.sbuf("l0blk", 4, BF16, top=True)
    P.dma("sync", lamr[0:1, :], P.din("L0a_lam", packed["L0a_lam"])[:, :], "l0ac")
    P.dma("sync", alr[0:1, 0:1], P.din("L0a_alpha", packed["L0a_alpha"])[:, :], "l0ac")
    P.dma("sync", blk[:, :], P.din("c_blk32_b16", packed["c_blk32_b16"])[:, :], "l0ac")
    P.all_wait("l0ac", P.semv["l0ac"])
    P.v(lambda v: v.tensor_tensor(rows[0:1, 0:64], lamr[0:1, 0:64], lamr[0:1, 128:192], ALU.mult))
    P.v(lambda v: v.tensor_tensor(rows[0:1, 64:128], lamr[0:1, 64:128], lamr[0:1, 192:256], ALU.mult))
    P.cut()
    P.v(lambda v: v.tensor_reduce(cl[0:1, 0:1], rows[0:1, 0:64], AX.X, ALU.add))
    P.v(lambda v: v.tensor_reduce(cl[0:1, 1:2], rows[0:1, 64:128], AX.X, ALU.add))
    P.cut()
    P.s(lambda s: s.activation(cl[0:1, 4:5], cl[0:1, 0:1], AF.Exp))
    P.s(lambda s: s.activation(cl[0:1, 5:6], cl[0:1, 1:2], AF.Exp))
    P.cut()
    P.v(lambda v: v.tensor_tensor(cl[0:1, 2:3], cl[0:1, 4:5], cl[0:1, 5:6], ALU.subtract))
    P.cut()
    P.v(lambda v: v.tensor_scalar(cl[0:1, 2:3], cl[0:1, 2:3], 0.2, None, ALU.add))
    P.v(lambda v: v.tensor_scalar(cl[0:1, 3:4], alr[0:1, 0:1], 2.0 / math.pi,
                                  1.0 / math.pi, ALU.mult, ALU.add))
    P.cut()
    P.t(lambda t: t.matmul(psA[:, 0:1], e["ones_r32"][:, :], cl[0:1, 3:4], start=True, stop=True))
    P.cut()
    P.v(lambda v: v.tensor_copy(c1col[:, 0:1], psA[:, 0:1]))
    P.cut()

    if "vpark" not in e:
        e["vpark"] = P.nc.dram_tensor("vpark", [128, 32768], BF16).ap()
    V = P.sbuf("l0v", 32768, BF16)
    wv = WFull(P, "l0awv", packed["L0a_wv"])
    P.wait("tensor", wv.sem, wv.ready)
    for tt in range(17):
        if tt < 16:
            ps = psA if tt % 2 == 0 else psB
            for ic in range(NC):
                for jj in range(4):
                    P.t(lambda t, tt=tt, ic=ic, jj=jj, ps=ps: t.matmul(
                        ps[:, jj * 512:(jj + 1) * 512],
                        XH[:, ic * T + tt * 128: ic * T + tt * 128 + 128],
                        wv.sb[:, ic * 2048 + jj * 512: ic * 2048 + (jj + 1) * 512],
                        start=(ic == 0), stop=(ic == NC - 1)))
        if tt > 0:
            psp = psA if (tt - 1) % 2 == 0 else psB
            P.v(lambda v, tt=tt, psp=psp: v.tensor_copy(V[:, (tt - 1) * 2048:tt * 2048], psp[:, :]))
        P.cut()
    wv.free()
    for c in range(16):
        P.dma("sync", e["vpark"][:, c * 2048:(c + 1) * 2048], V[:, c * 2048:(c + 1) * 2048], "vpk")
    P.all_wait("vpk", P.semv["vpk"])
    P.sfree("l0v")

    Q = P.sbuf("l0q", 16384, BF16)
    wq = WFull(P, "l0awq", packed["L0a_wq"])

    def evq(oc, ps):
        P.v(lambda v, oc=oc: v.tensor_copy(Q[:, oc * T:(oc + 1) * T], ps[:, :]))
    mm_fm(P, wq, XH, 0, NC, NC, evq, wait_sem=wq.sem, wait_val=wq.ready)
    wq.free()
    Kb = P.sbuf("l0k", 16384, BF16)
    wk = WFull(P, "l0awk", packed["L0a_wk"])

    def evk(oc, ps):
        P.v(lambda v, oc=oc: v.tensor_copy(Kb[:, oc * T:(oc + 1) * T], ps[:, :]))
    mm_fm(P, wk, XH, 0, NC, NC, evk, wait_sem=wk.sem, wait_val=wk.ready)
    wk.free()
    free_xh(P)

    rope = P.sbuf("l0rope", 8192, BF16)
    P.dma("sync", rope[:, :], P.din("c_rope", packed["c_rope"])[:, :], "l0rp")
    P.wait("vector", "l0rp", P.semv["l0rp"])
    P.cut()
    zsw = P.sbuf("l0zsw", 2048, BF16)
    rs = P.sbuf("l0rs", 2048, F32)
    SHUF = list(range(16, 32)) + list(range(16))
    for buf, co, so in ((Q, 0, 2048), (Kb, 4096, 6144)):
        for c in range(NC):
            P.v(lambda v, buf=buf, c=c: v.stream_shuffle(zsw[:, :], buf[:, c * T:(c + 1) * T], SHUF))
            P.v(lambda v, buf=buf, c=c, co=co: v.tensor_tensor(
                e["nscr"][:, :], buf[:, c * T:(c + 1) * T], rope[:, co:co + 2048], ALU.mult))
            P.v(lambda v, so=so: v.tensor_tensor(rs[:, :], zsw[:, :], rope[:, so:so + 2048], ALU.mult))
            P.v(lambda v, buf=buf, c=c: v.tensor_tensor(
                buf[:, c * T:(c + 1) * T], e["nscr"][:, :], rs[:, :], ALU.add))
    P.cut()
    P.sfree("l0rope")
    P.sfree("l0zsw")
    P.sfree("l0rs")

    V = P.sbuf("l0v", 32768, BF16)
    P.cut()
    for c in range(16):
        P.dma("sync", V[:, c * 2048:(c + 1) * 2048], e["vpark"][:, c * 2048:(c + 1) * 2048], "vup")
    P.all_wait("vup", P.semv["vup"])

    Yh = P.sbuf("l0yh", 2048, F32)
    rs = P.sbuf("l0rs2", 512, F32)
    Pb = P.sbuf("l0pb", 2048, BF16)
    s_scr = P.sbuf("l0ss", 2048, F32)
    sqs = P.sbuf("l0sq", 2048, BF16)
    ib = P.sbuf("l0ib", 1024, F32)
    gcol = P.sbuf("l0gn", 4, F32)
    negb2 = e["bc1"]
    for h in range(16):
        hc4, hr = h // 4, (h % 4) * 32
        for m in (0, 1):
            qch = 4 * m + hc4
            P.s(lambda s, qch=qch: s.activation(sqs[:, :], Q[:, qch * T:(qch + 1) * T], AF.Square))
            P.cut()
            for jj in range(NT):
                P.t(lambda t, jj=jj: t.matmul(psA[0:1, jj * 512:(jj + 1) * 512],
                                              blk[:, h % 4:h % 4 + 1],
                                              sqs[:, jj * 512:(jj + 1) * 512], start=True, stop=True))
            P.cut()
            P.s(lambda s, qch=qch: s.activation(sqs[:, :], Kb[:, qch * T:(qch + 1) * T], AF.Square))
            P.s(lambda s: s.activation(rows[0:1, 0:T], psA[0:1, 0:T], AF.Sqrt))
            P.cut()
            for jj in range(NT):
                P.t(lambda t, jj=jj: t.matmul(psB[0:1, jj * 512:(jj + 1) * 512],
                                              blk[:, h % 4:h % 4 + 1],
                                              sqs[:, jj * 512:(jj + 1) * 512], start=True, stop=True))
            P.cut()
            P.s(lambda s: s.activation(rows[0:1, T:2 * T], psB[0:1, 0:T], AF.Copy))
            P.cut()
            P.v(lambda v: v.tensor_reduce(cl[0:1, 8:9], rows[0:1, T:2 * T], AX.X, ALU.max))
            P.cut()
            P.s(lambda s: s.activation(cl[0:1, 9:10], cl[0:1, 8:9], AF.Sqrt))
            P.cut()
            P.v(lambda v: v.tensor_scalar(cl[0:1, 10:11], cl[0:1, 9:10], -1.0, None, ALU.mult))
            P.cut()
            P.v(lambda v: v.tensor_scalar(rows[0:1, T:2 * T], rows[0:1, 0:T],
                                          cl[0:1, 10:11], None, ALU.mult))
            P.cut()
            for jj in range(NT):
                P.t(lambda t, jj=jj: t.matmul(psA[:, jj * 512:(jj + 1) * 512], e["ones_r32"][:, :],
                                              rows[0:1, T + jj * 512:T + (jj + 1) * 512],
                                              start=True, stop=True))
            P.cut()
            dstbc = e["bc0"] if m == 0 else e["bc1"]
            P.v(lambda v, dstbc=dstbc: v.tensor_copy(dstbc[:, :], psA[:, :]))
            P.cut()
        for qq in range(4):
            q0 = qq * 512
            for j in range(19):
                if j < 16:
                    for m in (0, 1):
                        qch = 4 * m + hc4
                        sp = psA if m == 0 else psB
                        P.t(lambda t, j=j, m=m, qch=qch, sp=sp: t.matmul(
                            sp[:, 1024 + (j % 2) * 512:1024 + (j % 2) * 512 + 512],
                            Kb[hr:hr + 32, qch * T + j * 128: qch * T + j * 128 + 128],
                            Q[hr:hr + 32, qch * T + q0: qch * T + q0 + 512],
                            start=True, stop=True, tile_position=(hr, 0)))
                if 0 <= j - 1 < 16:
                    for m in (0, 1):
                        sp = psA if m == 0 else psB
                        bcm = e["bc0"] if m == 0 else e["bc1"]
                        P.v(lambda v, j=j, m=m, sp=sp, bcm=bcm: v.tensor_tensor(
                            s_scr[:, (m * 2 + ((j - 1) % 2)) * 512:(m * 2 + ((j - 1) % 2)) * 512 + 512],
                            sp[:, 1024 + ((j - 1) % 2) * 512:1024 + ((j - 1) % 2) * 512 + 512],
                            bcm[:, q0:q0 + 512], ALU.add))
                if 0 <= j - 2 < 16:
                    kc = j - 2
                    for m in (0, 1):
                        P.s(lambda s, kc=kc, m=m: s.activation(
                            Pb[:, (m * 2 + (kc % 2)) * 512:(m * 2 + (kc % 2)) * 512 + 512],
                            s_scr[:, (m * 2 + (kc % 2)) * 512:(m * 2 + (kc % 2)) * 512 + 512],
                            AF.Exp))
                if 0 <= j - 3 < 16:
                    kc = j - 3
                    for m in (0, 1):
                        sp = psA if m == 0 else psB
                        P.t(lambda t, kc=kc, m=m, h=h, sp=sp: t.matmul(
                            sp[:, 0:512],
                            V[:, kc * 2048 + h * 128: kc * 2048 + h * 128 + 128],
                            Pb[:, (m * 2 + (kc % 2)) * 512:(m * 2 + (kc % 2)) * 512 + 512],
                            start=(kc == 0), stop=(kc == 15)))
                        P.t(lambda t, kc=kc, m=m, sp=sp: t.matmul(
                            sp[0:1, 512:1024],
                            e["ones_cb"][:, :],
                            Pb[:, (m * 2 + (kc % 2)) * 512:(m * 2 + (kc % 2)) * 512 + 512],
                            start=(kc == 0), stop=(kc == 15)))
                P.cut()
            P.s(lambda s: s.activation(rows[0:1, 0:512], psA[0:1, 512:1024], AF.Copy))
            P.s(lambda s: s.activation(rows[0:1, 512:1024], psB[0:1, 512:1024], AF.Copy))
            P.cut()
            P.v(lambda v: v.reciprocal(rows[0:1, 0:512], rows[0:1, 0:512]))
            P.v(lambda v: v.reciprocal(rows[0:1, 512:1024], rows[0:1, 512:1024]))
            P.cut()
            P.v(lambda v: v.tensor_scalar(rows[0:1, 512:1024], rows[0:1, 512:1024],
                                          cl[0:1, 2:3], None, ALU.mult))
            P.cut()
            P.t(lambda t: t.matmul(psA[:, 1024:1536], e["ones_r32"][:, :],
                                   rows[0:1, 0:512], start=True, stop=True))
            P.t(lambda t: t.matmul(psB[:, 1024:1536], e["ones_r32"][:, :],
                                   rows[0:1, 512:1024], start=True, stop=True))
            P.cut()
            P.v(lambda v: v.tensor_copy(ib[:, 0:512], psA[:, 1024:1536]))
            P.v(lambda v: v.tensor_copy(ib[:, 512:1024], psB[:, 1024:1536]))
            P.cut()
            P.v(lambda v: v.tensor_tensor(e["nscr"][:, 0:512], psA[:, 0:512], ib[:, 0:512], ALU.mult))
            P.v(lambda v: v.tensor_tensor(rs[:, 0:512], psB[:, 0:512], ib[:, 512:1024], ALU.mult))
            P.v(lambda v, q0=q0: v.tensor_tensor(Yh[:, q0:q0 + 512], e["nscr"][:, 0:512],
                                                 rs[:, 0:512], ALU.subtract))
            P.cut()
        P.v(lambda v: v.tensor_copy(sqs[:, :], Yh[:, :]))
        P.cut()
        for jj in range(NT):
            P.t(lambda t, jj=jj: t.matmul(psA[0:1, jj * 512:(jj + 1) * 512], e["ones_cb"][:, :],
                                          sqs[:, jj * 512:(jj + 1) * 512], start=True, stop=True))
        P.cut()
        P.s(lambda s: s.activation(sqs[:, :], sqs[:, :], AF.Square))
        P.cut()
        for jj in range(NT):
            P.t(lambda t, jj=jj: t.matmul(psB[0:1, jj * 512:(jj + 1) * 512], e["ones_cb"][:, :],
                                          sqs[:, jj * 512:(jj + 1) * 512], start=True, stop=True))
        P.cut()
        P.s(lambda s: s.activation(rows[0:1, 0:T], psA[0:1, 0:T], AF.Copy))
        P.s(lambda s: s.activation(rows[0:1, T:2 * T], psB[0:1, 0:T], AF.Copy))
        P.cut()
        P.v(lambda v: v.tensor_reduce(cl[0:1, 8:9], rows[0:1, 0:T], AX.X, ALU.add))
        P.v(lambda v: v.tensor_reduce(cl[0:1, 9:10], rows[0:1, T:2 * T], AX.X, ALU.add))
        P.cut()
        NE = 128.0 * 2048.0
        P.v(lambda v: v.tensor_scalar(cl[0:1, 10:11], cl[0:1, 8:9], 1.0 / NE, None, ALU.mult))
        P.cut()
        P.v(lambda v: v.scalar_tensor_tensor(cl[0:1, 11:12], cl[0:1, 10:11], -1.0,
                                             cl[0:1, 10:11], ALU.mult, ALU.mult))
        P.cut()
        P.v(lambda v: v.scalar_tensor_tensor(cl[0:1, 11:12], cl[0:1, 9:10], 1.0 / NE,
                                             cl[0:1, 11:12], ALU.mult, ALU.add))
        P.cut()
        P.constf(1e-5)
        P.s(lambda s: s.activation(cl[0:1, 11:12], cl[0:1, 11:12], AF.Sqrt, bias=1e-5))
        P.cut()
        P.v(lambda v: v.reciprocal(cl[0:1, 11:12], cl[0:1, 11:12]))
        P.cut()
        P.v(lambda v: v.tensor_scalar(cl[0:1, 11:12], cl[0:1, 11:12], 0.8, None, ALU.mult))
        P.cut()
        P.v(lambda v: v.scalar_tensor_tensor(cl[0:1, 12:13], cl[0:1, 10:11], -1.0,
                                             cl[0:1, 11:12], ALU.mult, ALU.mult))
        P.cut()
        P.t(lambda t: t.matmul(psA[:, 1024:1026], e["ones_r32"][:, :], cl[0:1, 11:13],
                               start=True, stop=True))
        P.cut()
        P.v(lambda v: v.tensor_copy(gcol[:, 0:2], psA[:, 1024:1026]))
        P.cut()
        Yre = Yh[:, 0:2048].rearrange("p (c a) -> p c a", a=16)
        for g in range(4):
            for aa in range(4):
                a = g * 4 + aa
                P.t(lambda t, a=a, aa=aa, Yre=Yre: t.transpose(
                    psB[:, 1024 + aa * 128:1024 + (aa + 1) * 128],
                    Yre[:, :, a:a + 1], e["ident"][:, :]))
            P.cut()
            for aa in range(4):
                a = g * 4 + aa
                P.v(lambda v, a=a, aa=aa, h=h: v.tensor_scalar(
                    V[:, a * 2048 + h * 128: a * 2048 + h * 128 + 128],
                    psB[:, 1024 + aa * 128:1024 + (aa + 1) * 128],
                    gcol[:, 0:1], gcol[:, 1:2], ALU.mult, ALU.add))
            P.cut()
    P.sfree("l0q")
    P.sfree("l0k")
    P.sfree("l0yh")
    P.sfree("l0pb")
    P.sfree("l0ss")
    P.sfree("l0sq")
    P.sfree("l0ib")
    P.sfree("l0rs2")

    unpark_x(P)
    X = e["X"]
    q1 = P.sbuf("l0q1", 2048, F32)
    q2 = P.sbuf("l0q2", 2048, F32)
    sg = P.sbuf("l0sg", 2048, BF16)
    gtmp = P.sbuf("l0gt", 2048, BF16)
    Vre = V[:, 0:32768].rearrange("p (a dv) -> p a dv", dv=2048)

    def rhs_fn(ic, j):
        return Vre[:, 4 * j:4 * j + 4, ic * 128:(ic + 1) * 128]

    def evp(oc, ps):
        if oc % 2 == 0:
            P.s(lambda s: s.activation(sg[:, :], ps[:, :], AF.Sign))
            P.s(lambda s: s.activation(q1[:, :], ps[:, :], AF.Abs))
            P.cut()
            P.v(lambda v: v.tensor_scalar(q2[:, :], q1[:, :], 1.0, None, ALU.max))
            P.v(lambda v: v.reciprocal(q2[:, :], q2[:, :]))
            P.v(lambda v: v.tensor_tensor(q2[:, :], q1[:, :], q2[:, :], ALU.min))
            P.cut()
            P.s(lambda s: s.activation(q2[:, :], q2[:, :], AF.Arctan))
            P.cut()
            P.v(lambda v: v.tensor_scalar(e["nscr"][:, :], q2[:, :], -2.0,
                                          math.pi / 2.0, ALU.mult, ALU.add))
            P.v(lambda v: v.tensor_scalar(q1[:, :], q1[:, :], 1.0, None, ALU.is_gt))
            P.v(lambda v: v.tensor_tensor(e["nscr"][:, :], e["nscr"][:, :], q1[:, :], ALU.mult))
            P.v(lambda v: v.tensor_tensor(q2[:, :], q2[:, :], e["nscr"][:, :], ALU.add))
            P.v(lambda v: v.tensor_tensor(q2[:, :], q2[:, :], sg[:, :], ALU.mult))
            P.v(lambda v: v.tensor_scalar(gtmp[:, :], q2[:, :], c1col[:, 0:1], 0.5,
                                          ALU.mult, ALU.add))
        else:
            c = oc // 2
            P.v(lambda v, c=c: v.tensor_tensor(e["nscr"][:, :], ps[:, :], gtmp[:, :], ALU.mult))
            P.v(lambda v, c=c: v.tensor_tensor(X[:, c * T:(c + 1) * T], e["nscr"][:, :],
                                               X[:, c * T:(c + 1) * T], ALU.add))
    mm_stream(P, "l0a_wproj", packed["L0a_wproj"], rhs_fn, 0, 0, T, evp, PW=128)
    P.sfree("l0v")
    P.sfree("l0q1")
    P.sfree("l0q2")
    P.sfree("l0sg")
    P.sfree("l0gt")


def ensure_sel(P, packed):
    if "selw" not in P._sb:
        sel = P.sbuf("selw", 2048, F32, top=True)
        P.env["selw"] = sel
        P.dma("sync", sel[0:16, :], P.din("c_sel16_f32", packed["c_sel16_f32"])[:, :], "selc")
        P.all_wait("selc", P.semv["selc"])
    return P.env["selw"]


def emit_l0_ffn(P, packed):
    e = P.env
    psA, psB, rows = e["psA"], e["psB"], e["rows"]
    emit_ln_v2(P)
    park_x(P)
    XH = e["XH"]
    NH = 22
    if "l0rw" not in P._sb:
        rw = P.sbuf("l0rw", 64, BF16, top=True)
        rbr = P.sbuf("l0rb", 8, F32, top=True)
        e["l0rw"], e["l0rb"] = rw, rbr
        P.dma("gpsimd", rw[:, 0:56], P.din("L0f_wr", packed["L0f_wr"]).rearrange("(c p) e -> p c e", p=128), "l0c")
        P.dma("sync", rbr[0:1, 0:7], P.din("L0f_rb", packed["L0f_rb"])[:, :], "l0c")
        P.all_wait("l0c", P.semv["l0c"])
    rw, rbr = e["l0rw"], e["l0rb"]
    sel = ensure_sel(P, packed)

    logit = P.sbuf("l0lg", 128, F32)
    P.t(lambda t: t.matmul(psB[:, 32:39], e["ones_r32"][:, :], rbr[0:1, 0:7], start=True, stop=True))
    P.cut()
    P.v(lambda v: v.tensor_copy(e["bc1"][:, 0:7], psB[:, 32:39]))
    P.cut()
    for tt in range(17):
        if tt < 16:
            ps = psA if tt % 2 == 0 else psB
            for ic in range(NC):
                P.t(lambda t, tt=tt, ic=ic, ps=ps: t.matmul(
                    ps[:, 0:7],
                    XH[:, ic * T + tt * 128: ic * T + tt * 128 + 128],
                    rw[:, ic * 7:(ic + 1) * 7],
                    start=(ic == 0), stop=(ic == NC - 1)))
        if tt > 0:
            psp = psA if (tt - 1) % 2 == 0 else psB
            P.v(lambda v, tt=tt, psp=psp: v.tensor_tensor(
                logit[:, (tt - 1) * 7:tt * 7], psp[:, 0:7], e["bc1"][:, 0:7], ALU.add))
        P.cut()

    pbuf = P.sbuf("l0p", 128, F32)
    mcol = P.sbuf("l0m", 16, F32)
    dsum = P.sbuf("l0d", 16, F32)
    keep = P.sbuf("l0keep", 128, F32)
    for tt in range(16):
        P.v(lambda v, tt=tt: v.tensor_reduce(mcol[:, tt:tt + 1], logit[:, tt * 7:(tt + 1) * 7],
                                             AX.X, ALU.max, negate=True))
    P.cut()
    for tt in range(16):
        P.s(lambda s, tt=tt: s.activation(pbuf[:, tt * 7:(tt + 1) * 7],
                                          logit[:, tt * 7:(tt + 1) * 7],
                                          AF.Exp, bias=mcol[:, tt:tt + 1]))
    P.cut()
    P.v(lambda v: v.tensor_copy(logit[:, :], pbuf[:, :]))
    P.cut()
    for step in range(2):
        for tt in range(16):
            P.v(lambda v, tt=tt: v.tensor_reduce(mcol[:, tt:tt + 1], logit[:, tt * 7:(tt + 1) * 7],
                                                 AX.X, ALU.max))
        P.cut()
        if step == 0:
            P.v(lambda v: v.tensor_copy(dsum[:, :], mcol[:, :]))
            for tt in range(16):
                P.v(lambda v, tt=tt: v.tensor_scalar(keep[:, tt * 7:(tt + 1) * 7],
                                                     logit[:, tt * 7:(tt + 1) * 7],
                                                     mcol[:, tt:tt + 1], None, ALU.is_lt))
            for tt in range(16):
                P.v(lambda v, tt=tt: v.tensor_tensor(logit[:, tt * 7:(tt + 1) * 7],
                                                     logit[:, tt * 7:(tt + 1) * 7],
                                                     keep[:, tt * 7:(tt + 1) * 7], ALU.mult))
        else:
            P.v(lambda v: v.tensor_tensor(dsum[:, :], dsum[:, :], mcol[:, :], ALU.add))
        P.cut()
    for tt in range(16):
        P.v(lambda v, tt=tt: v.tensor_scalar(keep[:, tt * 7:(tt + 1) * 7],
                                             pbuf[:, tt * 7:(tt + 1) * 7],
                                             mcol[:, tt:tt + 1], None, ALU.is_ge))
    P.v(lambda v: v.reciprocal(dsum[:, :], dsum[:, :]))
    P.cut()
    for tt in range(16):
        P.v(lambda v, tt=tt: v.tensor_scalar(pbuf[:, tt * 7:(tt + 1) * 7],
                                             pbuf[:, tt * 7:(tt + 1) * 7],
                                             dsum[:, tt:tt + 1], None, ALU.mult))
    P.cut()
    for tt in range(16):
        P.v(lambda v, tt=tt: v.tensor_tensor(pbuf[:, tt * 7:(tt + 1) * 7],
                                             pbuf[:, tt * 7:(tt + 1) * 7],
                                             keep[:, tt * 7:(tt + 1) * 7], ALU.mult))
    P.cut()
    wrow = P.sbuf("l0wr", 2048, F32)
    for tt in range(16):
        P.t(lambda t, tt=tt: t.transpose(psA[0:7, tt * 128:(tt + 1) * 128],
                                         pbuf[:, tt * 7:(tt + 1) * 7], e["ident"][:, :]))
    P.cut()
    P.v(lambda v: v.tensor_copy(wrow[0:7, :], psA[0:7, 0:2048]))
    P.cut()
    P.sfree("l0lg")
    P.sfree("l0p")
    P.sfree("l0m")
    P.sfree("l0d")
    P.sfree("l0keep")

    He = P.sbuf("l0he", NH * 1024, BF16)
    ACC = P.sbuf("l0acc", 8192, F32)
    TH = 1024
    for half in range(2):
        q0 = half * TH
        for ei in range(8):
            routed = ei > 0
            ex = ei - 1
            if routed:
                P.t(lambda t, ex=ex: t.matmul(psB[:, 1024:1536], sel[0:7, ex * 128:(ex + 1) * 128],
                                              wrow[0:7, q0:q0 + 512], start=True, stop=True))
                P.t(lambda t, ex=ex: t.matmul(psB[:, 1536:2048], sel[0:7, ex * 128:(ex + 1) * 128],
                                              wrow[0:7, q0 + 512:q0 + TH], start=True, stop=True))
                P.cut()
                P.v(lambda v: v.tensor_copy(e["bc0"][:, 0:TH], psB[:, 1024:2048]))
                P.cut()
            gname = "L0f_sg" if not routed else f"L0f_rg_{ex}"
            uname = "L0f_su" if not routed else f"L0f_ru_{ex}"
            dname = "L0f_sd" if not routed else f"L0f_rd_{ex}"

            def evg(oc, ps):
                P.s(lambda s, oc=oc, ps=ps: s.activation(
                    He[:, oc * TH:(oc + 1) * TH], ps[:, 0:TH], AF.Silu))
            mm_stream(P, gname, packed[gname], XH, q0, T, TH, evg, PW=256)

            def evu(oc, ps):
                P.v(lambda v, oc=oc, ps=ps: v.tensor_tensor(
                    He[:, oc * TH:(oc + 1) * TH], ps[:, 0:TH],
                    He[:, oc * TH:(oc + 1) * TH], ALU.mult))
            mm_stream(P, uname, packed[uname], XH, q0, T, TH, evu, PW=256)

            def evd(oc, ps, routed=routed, ei=ei):
                if not routed:
                    P.v(lambda v, oc=oc, ps=ps: v.tensor_copy(
                        ACC[:, oc * TH:(oc + 1) * TH], ps[:, 0:TH]))
                else:
                    P.v(lambda v, oc=oc, ps=ps: v.tensor_tensor(
                        e["nscr"][:, 0:TH], ps[:, 0:TH], e["bc0"][:, 0:TH], ALU.mult))
                    P.v(lambda v, oc=oc: v.tensor_tensor(
                        ACC[:, oc * TH:(oc + 1) * TH], ACC[:, oc * TH:(oc + 1) * TH],
                        e["nscr"][:, 0:TH], ALU.add))
            mm_stream(P, dname, packed[dname], He, 0, TH, TH, evd, PW=256)
        for c in range(NC):
            sl = e["nscr"][:, (c % 2) * 1024:(c % 2) * 1024 + 1024]
            v1 = P.dma("sync", sl, e["xpark"][:, c * T + q0: c * T + q0 + TH], "xq")
            P.wait("vector", "xq", v1)
            P.v(lambda v, c=c, sl=sl: v.scalar_tensor_tensor(
                sl, ACC[:, c * TH:(c + 1) * TH], 0.5, sl, ALU.mult, ALU.add))
            P.cut()
            P.dma("sync", e["xpark"][:, c * T + q0: c * T + q0 + TH], sl, "xqs")
        P.wait("sync", "xqs", P.semv["xqs"])
        P.cut()
    P.sfree("l0he")
    P.sfree("l0acc")
    P.sfree("l0wr")
    unpark_x(P)


def emit_l2_quant(P, packed):
    e = P.env
    wf = P.sbuf("l2wf", NC * 1024, F32)
    wd = P.din("L2w_ffw", packed["L2w_ffw"])
    for ic in range(NC):
        P.dma("sync", wf[:, ic * 1024:(ic + 1) * 1024], wd[ic * 128:(ic + 1) * 128, :], "l2w")
    e["l2bias"] = P.sbuf("l2bias", NC, F32, top=True)
    P.dma("sync", e["l2bias"][:, :], P.din("L2w_ffb", packed["L2w_ffb"])[:, :], "l2w")
    arow = P.sbuf("l2arow", 1, F32, parts=1)
    P.dma("sync", arow[:, :], P.din("L2w_alpha", packed["L2w_alpha"])[:, :], "l2w")
    P.wait("vector", "l2w", P.semv["l2w"])
    P.wait("scalar", "l2w", P.semv["l2w"])
    P.wait("tensor", "l2w", P.semv["l2w"])
    P.cut()
    if P.env.get("_dbg"):
        P.dbg_dump("wf", wf[:, 0:8192], 8192)
    rows = e["rows"]
    acc = P.sbuf("l2acc", 8, F32)
    for ic in range(NC):
        P.v(lambda v, ic=ic: v.tensor_reduce(acc[:, ic:ic + 1], wf[:, ic * 1024:(ic + 1) * 1024],
                                             AX.X, ALU.add, apply_absolute_value=True))
    P.cut()
    if P.env.get("_dbg"):
        P.dbg_dump("acc", acc[:, 0:8], 8)
    P.t(lambda t: t.matmul(e["psA"][0:1, 0:8], e["ones_c32"][:, :], acc[:, 0:8], start=True, stop=True))
    P.cut()
    drow = rows[0:1, 0:2]
    P.v(lambda v: v.tensor_reduce(rows[0:1, 4:5], e["psA"][0:1, 0:8], AX.X, ALU.add))
    P.cut()
    P.s(lambda s: s.activation(drow[:, 0:1], rows[0:1, 4:5], AF.Copy, scale=0.7 / (1024.0 * 1024.0)))
    P.cut()
    P.v(lambda v: v.tensor_scalar(drow[:, 1:2], drow[:, 0:1], -1.0, None, ALU.mult))
    P.cut()
    P.t(lambda t: t.matmul(e["psB"][:, 0:2], e["ones_r32"][:, :], drow[:, 0:2], start=True, stop=True))
    P.t(lambda t: t.matmul(e["psB"][:, 2:3], e["ones_r32"][:, :], arow[:, 0:1], start=True, stop=True))
    P.cut()
    dcols = P.sbuf("l2dcols", 4, F32, top=True)
    e["l2alpha"] = dcols
    P.v(lambda v: v.tensor_copy(dcols[:, 0:3], e["psB"][:, 0:3]))
    P.cut()
    tern = P.sbuf("l2tern", NC * 1024, BF16, top=True)
    e["l2tern"] = tern
    t0 = P.sbuf("l2t0", 1024, BF16)
    t1 = P.sbuf("l2t1", 1024, BF16)
    for ic in range(NC):
        P.v(lambda v, ic=ic: v.tensor_scalar(t0[:, :], wf[:, ic * 1024:(ic + 1) * 1024],
                                             dcols[:, 0:1], None, ALU.is_gt))
        P.v(lambda v, ic=ic: v.tensor_scalar(t1[:, :], wf[:, ic * 1024:(ic + 1) * 1024],
                                             dcols[:, 1:2], None, ALU.is_lt))
        P.cut()
        P.v(lambda v, ic=ic: v.tensor_tensor(tern[:, ic * 1024:(ic + 1) * 1024], t0[:, :], t1[:, :],
                                             ALU.subtract))
        P.cut()
    P.sfree("l2t0")
    P.sfree("l2t1")
    P.sfree("l2acc")
    P.sfree("l2wf")


def emit_l2_ffn(P, packed, first):
    e = P.env
    emit_ln_v2(P)
    if P.env.get("_dbg"):
        P.dbg_dump("XH", e["XH"][:, 0:16384], 16384, dtype=BF16)
    if first:
        emit_l2_quant(P, packed)
    if P.env.get("_dbg"):
        P.dbg_dump("tern", e["l2tern"][:, 0:8192], 8192, dtype=BF16)
        P.dbg_dump("dcols", e["l2alpha"][:, 0:4], 4)
    tern, dcols, bias = e["l2tern"], e["l2alpha"], e["l2bias"]
    X, XH = e["X"], e["XH"]

    def evac(oc, ps):
        P.v(lambda v: v.tensor_scalar(e["nscr"][:, :], ps[:, :],
                                      dcols[:, 2:3], bias[:, oc:oc + 1], ALU.mult, ALU.add))
        P.v(lambda v: v.scalar_tensor_tensor(X[:, oc * T:(oc + 1) * T], e["nscr"][:, :], 0.5,
                                             X[:, oc * T:(oc + 1) * T], ALU.mult, ALU.add))

    class TW:
        def lhsT(self, ic, o0, ow=128):
            return tern[:, ic * 1024 + o0: ic * 1024 + o0 + ow]

    mm_fm(P, TW(), XH, 0, NC, NC, evac)



def emit_l2_attn(P, packed):
    e = P.env
    psA, psB, rows = e["psA"], e["psB"], e["rows"]
    emit_ln_v2(P)
    park_x(P)
    if "selw" in P._sb:
        P.sfree("selw")
        P.env.pop("selw", None)
    XH = e["XH"]
    Q = P.sbuf("l2q", 16384, BF16)
    Kb = P.sbuf("l2k", 16384, BF16)
    V0 = P.sbuf("l2v0", 8192, BF16)
    V1 = P.sbuf("l2v1", 8192, BF16)

    wq = WFull(P, "l2wq", packed["L2w_wq"])

    def evq(oc, ps):
        P.s(lambda s: s.activation(Q[:, oc * T:(oc + 1) * T], ps[:, :], AF.Copy, scale=0.03125))
    mm_fm(P, wq, XH, 0, NC, NC, evq, wait_sem=wq.sem, wait_val=wq.ready)
    wq.free()
    wk = WFull(P, "l2wk", packed["L2w_wk"])

    def evk(oc, ps):
        P.v(lambda v: v.tensor_copy(Kb[:, oc * T:(oc + 1) * T], ps[:, :]))
    mm_fm(P, wk, XH, 0, NC, NC, evk, wait_sem=wk.sem, wait_val=wk.ready)
    wk.free()

    wv = WFull(P, "l2wv", packed["L2w_wv"])
    P.wait("tensor", wv.sem, wv.ready)
    for tt in range(17):
        if tt < 16:
            ps = psA if tt % 2 == 0 else psB
            for ic in range(NC):
                for j in range(2):
                    P.t(lambda t, tt=tt, ic=ic, j=j, ps=ps: t.matmul(
                        ps[:, j * 512:(j + 1) * 512],
                        XH[:, ic * T + tt * 128: ic * T + tt * 128 + 128],
                        wv.sb[:, ic * 1024 + j * 512: ic * 1024 + (j + 1) * 512],
                        start=(ic == 0), stop=(ic == NC - 1)))
        if tt > 0:
            psp = psA if (tt - 1) % 2 == 0 else psB
            P.v(lambda v, tt=tt, psp=psp: v.tensor_copy(V0[:, (tt - 1) * 512:tt * 512], psp[:, 0:512]))
            P.v(lambda v, tt=tt, psp=psp: v.tensor_copy(V1[:, (tt - 1) * 512:tt * 512], psp[:, 512:1024]))
        P.cut()
    wv.free()
    free_xh(P)
    Yatt = Q
    cells = P.sbuf("l2cells", 8, F32)

    sq0 = P.sbuf("l2sq0", 2048, BF16)
    sq1 = P.sbuf("l2sq1", 2048, BF16)
    for src, psrow in ((Q, psA), (Kb, psB)):
        for c in range(NC + 1):
            if c < NC:
                dst = sq0 if c % 2 == 0 else sq1
                P.s(lambda s, c=c, src=src, dst=dst: s.activation(
                    dst[:, :], src[:, c * T:(c + 1) * T], AF.Square))
            if c > 0:
                sb = sq0 if (c - 1) % 2 == 0 else sq1
                for j in range(NT):
                    P.t(lambda t, j=j, sb=sb, psrow=psrow, c=c: t.matmul(
                        psrow[0:1, j * 512:(j + 1) * 512], e["ones_cb"][:, :],
                        sb[:, j * 512:(j + 1) * 512],
                        start=(c - 1 == 0), stop=(c - 1 == NC - 1)))
            P.cut()
    P.sfree("l2sq0")
    P.sfree("l2sq1")
    qn = rows[0:1, 0:T]
    kn2 = rows[0:1, T:2 * T]
    P.s(lambda s: s.activation(qn, psA[0:1, 0:T], AF.Sqrt))
    P.s(lambda s: s.activation(kn2, psB[0:1, 0:T], AF.Copy))
    P.cut()
    P.v(lambda v: v.tensor_reduce(cells[0:1, 0:1], kn2, AX.X, ALU.max))
    P.cut()
    P.s(lambda s: s.activation(cells[0:1, 1:2], cells[0:1, 0:1], AF.Sqrt))
    P.cut()
    P.v(lambda v: v.tensor_scalar(cells[0:1, 2:3], cells[0:1, 1:2], -1.0, None, ALU.mult))
    P.cut()
    P.v(lambda v: v.tensor_scalar(kn2, qn, cells[0:1, 2:3], None, ALU.mult))
    P.cut()
    for j in range(NT):
        P.t(lambda t, j=j: t.matmul(psA[:, j * 512:(j + 1) * 512], e["ones_r32"][:, :],
                                    kn2[:, j * 512:(j + 1) * 512], start=True, stop=True))
    P.cut()
    P.v(lambda v: v.tensor_copy(e["bc0"][:, :], psA[:, :]))
    P.cut()

    P_all = P.sbuf("l2pall", 16 * 256, BF16)
    s_scr = P.sbuf("l2sscr", 512, F32)
    for qt in range(8):
        nkc = 2 * qt + 2
        q0 = qt * 256
        for j in range(nkc + 3):
            if j < nkc:
                sps = psA if j % 2 == 0 else psB
                for ic in range(NC):
                    P.t(lambda t, j=j, ic=ic, sps=sps: t.matmul(
                        sps[:, 512:768],
                        Kb[:, ic * T + j * 128: ic * T + j * 128 + 128],
                        Q[:, ic * T + q0: ic * T + q0 + 256],
                        start=(ic == 0), stop=(ic == NC - 1)))
            if 0 <= j - 1 < nkc:
                spsp = psA if (j - 1) % 2 == 0 else psB
                P.v(lambda v, j=j, spsp=spsp: v.tensor_tensor(
                    s_scr[:, ((j - 1) % 2) * 256:((j - 1) % 2) * 256 + 256],
                    spsp[:, 512:768], e["bc0"][:, q0:q0 + 256], ALU.add))
            if 0 <= j - 2 < nkc:
                kc = j - 2
                P.s(lambda s, kc=kc: s.activation(
                    P_all[:, kc * 256:(kc + 1) * 256],
                    s_scr[:, (kc % 2) * 256:(kc % 2) * 256 + 256], AF.Exp))
            if 0 <= j - 3 < nkc:
                kc = j - 3
                if kc == 2 * qt:
                    P.v(lambda v, kc=kc: v.tensor_tensor(
                        P_all[:, kc * 256:kc * 256 + 128],
                        P_all[:, kc * 256:kc * 256 + 128], e["tril"][:, :], ALU.mult))
                elif kc == 2 * qt + 1:
                    P.v(lambda v, kc=kc: v.memset(P_all[:, kc * 256:kc * 256 + 128], 0.0))
                    P.v(lambda v, kc=kc: v.tensor_tensor(
                        P_all[:, kc * 256 + 128:kc * 256 + 256],
                        P_all[:, kc * 256 + 128:kc * 256 + 256], e["tril"][:, :], ALU.mult))
            P.cut()
        for kc in range(nkc):
            P.t(lambda t, kc=kc: t.matmul(psA[0:1, 1536:1792], e["ones_cb"][:, :],
                                          P_all[:, kc * 256:(kc + 1) * 256],
                                          start=(kc == 0), stop=(kc == nkc - 1)))
        P.cut()
        lrow = rows[0:1, 2048:2304]
        P.s(lambda s: s.activation(lrow, psA[0:1, 1536:1792], AF.Copy))
        P.cut()
        P.v(lambda v: v.reciprocal(lrow, lrow))
        P.cut()
        P.t(lambda t: t.matmul(psB[:, 768:1024], e["ones_r32"][:, :], lrow, start=True, stop=True))
        P.cut()
        P.v(lambda v: v.tensor_copy(e["bc1"][:, 0:256], psB[:, 768:1024]))
        P.cut()
        for z in range(9):
            if z < 8:
                ps = psA if z % 2 == 0 else psB
                for kc in range(nkc):
                    Vh = V0 if z < 4 else V1
                    zz = z if z < 4 else z - 4
                    P.t(lambda t, zz=zz, kc=kc, ps=ps, Vh=Vh: t.matmul(
                        ps[:, 0:256],
                        Vh[:, kc * 512 + zz * 128: kc * 512 + zz * 128 + 128],
                        P_all[:, kc * 256:(kc + 1) * 256],
                        start=(kc == 0), stop=(kc == nkc - 1)))
            if z > 0:
                psp = psA if (z - 1) % 2 == 0 else psB
                P.v(lambda v, z=z, psp=psp: v.tensor_tensor(
                    Yatt[:, (z - 1) * T + q0:(z - 1) * T + q0 + 256],
                    psp[:, 0:256], e["bc1"][:, 0:256], ALU.mult))
            P.cut()
    P.sfree("l2pall")
    P.sfree("l2sscr")
    P.sfree("l2k")
    P.sfree("l2v0")
    P.sfree("l2v1")
    P.sfree("l2cells")

    unpark_x(P)
    X = e["X"]
    wo = WFull(P, "l2wo", packed["L2w_wo"])

    def evo(oc, ps):
        P.v(lambda v, oc=oc: v.tensor_tensor(X[:, oc * T:(oc + 1) * T], ps[:, :],
                                             X[:, oc * T:(oc + 1) * T], ALU.add))
    mm_fm(P, wo, Yatt, 0, NC, NC, evo, wait_sem=wo.sem, wait_val=wo.ready)
    wo.free()
    P.sfree("l2q")


SUBLAYER_SET = ["l0_ffn_a", "l0_attn", "l0_ffn_b", "l1_ffn_a", "l1_attn", "l1_ffn_b",
                "l2_ffn_a", "l2_attn", "l2_ffn_b", "l3_ffn_a", "l3_attn", "l3_ffn_b"]


def emit_sublayer(P, name, packed):
    if name == "l2_ffn_a":
        emit_l2_ffn(P, packed, first=True)
    elif name == "l2_ffn_b":
        emit_l2_ffn(P, packed, first="l2_ffn_a" not in P.env.get("_done", ()))
    elif name == "l2_attn":
        emit_l2_attn(P, packed)
    elif name in ("l1_ffn_a", "l1_ffn_b"):
        emit_l1_ffn(P, packed)
    elif name in ("l3_ffn_a", "l3_ffn_b"):
        emit_l3_ffn(P, packed)
    elif name == "l1_attn":
        emit_l1_attn(P, packed)
    elif name == "l3_attn":
        emit_l3_attn(P, packed)
    elif name in ("l0_ffn_a", "l0_ffn_b"):
        emit_l0_ffn(P, packed)
    elif name == "l0_attn":
        emit_l0_attn(P, packed)
    else:
        raise NotImplementedError(name)


_LAYER_PERSISTENTS = {
    "l0_ffn_b": ["l0rw", "l0rb"],
    "l0_attn": ["l0lam", "l0al", "l0cl", "l0c1c", "l0blk", "l0gn"],
    "l1_ffn_b": ["l1scales", "l1sh2"],
    "l1_attn": ["l1abias"],
    "l2_ffn_b": ["l2tern", "l2bias", "l2dcols"],
    "l3_attn": ["l3abias"],
    "l3_ffn_b": ["l3rw", "l3rb", "l3b1", "l3b2"],
}


def build_program(names, packed, dbg=False):
    P = Prog()
    P.env["_dbg"] = dbg
    setup_env(P, packed)
    done = []
    for nm in names:
        P.env["_done"] = tuple(done)
        emit_sublayer(P, nm, packed)
        done.append(nm)
        later_layers = {x.split("_")[0] for x in names[len(done):]}
        for k, bufs in _LAYER_PERSISTENTS.items():
            if k.split("_")[0] not in later_layers:
                for b in bufs:
                    if b in P._sb:
                        P.sfree(b)
                        P.env.pop(b, None)
    store_x(P)
    P.finish()
    return P


def run_launch(P, xs):
    in_maps = []
    for b in range(8):
        m = dict(P.inmap)
        m["xin"] = xs[b]
        in_maps.append(m)
    res = run_bass_kernel_spmd(P.nc, in_maps, core_ids=list(range(8)))
    return [r["xout"] for r in res.results]


LAUNCH_GROUPS = [list(SUBLAYER_SET)]


def kernel(**inputs):
    packed = pack_all(inputs)
    x = np.asarray(inputs["L0_attn_x"])
    xs = []
    for b in range(8):
        xf = np.ascontiguousarray(x[b].T.astype(np.float32))
        xs.append(np.ascontiguousarray(xf.reshape(NC, 128, T).transpose(1, 0, 2).reshape(128, NC * T)))
    for grp in LAUNCH_GROUPS:
        packed["xin0"] = xs[0]
        P = build_program(grp, packed)
        xs = run_launch(P, xs)
    outs = []
    for b in range(8):
        o = xs[b].reshape(128, NC, T).transpose(2, 1, 0).reshape(T, D)
        outs.append(o)
    return np.stack(outs).astype(np.float32)


# version 04 (final)
# speedup vs baseline: 1.3516x; 1.1074x over previous
import contextlib
import math
from collections import defaultdict

import numpy as np
import ml_dtypes
import concourse.bass as bass
import concourse.mybir as mybir
from concourse.bass_utils import run_bass_kernel_spmd

F32 = mybir.dt.float32
BF16 = mybir.dt.bfloat16
AF = mybir.ActivationFunctionType
ALU = mybir.AluOpType
AX = mybir.AxisListType

PE_LIMIT = 110
ENG_LIMIT = 200
D = 1024
T = 2048
NC = D // 128
NT = T // 512


class Tile:
    __slots__ = ("base", "off", "cols")

    def __init__(self, base, off, cols):
        self.base = base
        self.off = off
        self.cols = cols

    def __getitem__(self, idx):
        ps, cs = idx
        a = cs.start if cs.start is not None else 0
        b = cs.stop if cs.stop is not None else self.cols
        assert 0 <= a <= b <= self.cols, (a, b, self.cols)
        return self.base[ps, self.off + a:self.off + b]


class Arena:
    ALIGN = 128

    def __init__(self, nbytes):
        self.free_list = [(0, nbytes)]
        self.named = {}
        self.hwm = 0

    def alloc(self, name, nbytes, top=False):
        nbytes = -(-nbytes // self.ALIGN) * self.ALIGN
        cands = [i for i, (off, ln) in enumerate(self.free_list) if ln >= nbytes]
        if top:
            cands.sort(key=lambda i: -(self.free_list[i][0] + self.free_list[i][1]))
        else:
            cands.sort(key=lambda i: (self.free_list[i][1], i))
        for i in cands[:1]:
            off, ln = self.free_list[i]
            if True:
                if top:
                    self.named[name] = (off + ln - nbytes, nbytes)
                    self.free_list[i] = (off, ln - nbytes)
                    if ln == nbytes:
                        self.free_list.pop(i)
                    return off + ln - nbytes
                self.named[name] = (off, nbytes)
                if ln == nbytes:
                    self.free_list.pop(i)
                else:
                    self.free_list[i] = (off + nbytes, ln - nbytes)
                self.hwm = max(self.hwm, off + nbytes)
                return off
        raise MemoryError(f"arena OOM allocating {name} ({nbytes}B); "
                          f"free={self.free_list} named={list(self.named)}")

    def free(self, name):
        off, nbytes = self.named.pop(name)
        self.free_list.append((off, nbytes))
        self.free_list.sort()
        out = []
        for off, ln in self.free_list:
            if out and out[-1][0] + out[-1][1] == off:
                out[-1] = (out[-1][0], out[-1][1] + ln)
            else:
                out.append((off, ln))
        self.free_list = out


class ChunkedX:

    def __init__(self, tiles):
        self.tiles = tiles

    def __getitem__(self, idx):
        ps, cs = idx
        a, b = cs.start or 0, cs.stop
        c = a // T
        assert b <= (c + 1) * T, (a, b)
        return self.tiles[c][ps, a - c * T: b - c * T]


class Prog:
    def __init__(self):
        self.nc = bass.Bass("TRN2", target_bir_lowering=False)
        self._blk = None
        self._cnt = defaultdict(int)
        self._sems = {}
        self.semv = defaultdict(int)
        self._sb = {}
        self._ps = {}
        self.dram = {}
        self.inmap = {}
        self.env = {}
        self._consts = {}

    def constf(self, val, dtype=F32):
        key = (dtype, float(val))
        if key not in self._consts:
            nm = "cstv%d" % len(self._consts)
            h = self.sbuf(nm, 1, dtype, top=True)
            self.v(lambda v: v.memset(h[:, :], val))
            self.cut()
            self.nc.const_aps.register(dtype, float(val), h[:, 0:1])
            self._consts[key] = h
        return self._consts[key]

    def din(self, name, arr):
        if name not in self.dram:
            dt = {np.dtype(np.float32): F32, np.dtype(ml_dtypes.bfloat16): BF16}[arr.dtype]
            t = self.nc.dram_tensor(name, list(arr.shape), dt, kind="ExternalInput")
            self.dram[name] = t.ap()
            self.inmap[name] = arr
        return self.dram[name]

    def dout(self, name, shape, dtype=F32):
        t = self.nc.dram_tensor(name, list(shape), dtype, kind="ExternalOutput")
        return t.ap()

    ARENA_BYTES = 206 * 1024

    def _arena(self):
        if not hasattr(self, "_arh"):
            h = self.nc.sbuf_tensor("arena", [128, self.ARENA_BYTES // 4], F32).__enter__()
            self._arh = {F32: h, BF16: h.bitcast(BF16)}
            self._aralloc = Arena(self.ARENA_BYTES)
        return self._aralloc

    def sbuf(self, name, cols, dtype, parts=128, top=False):
        assert name not in self._sb, name
        if parts != 128:
            cm = self.nc.sbuf_tensor(name, [parts, cols], dtype)
            h = cm.__enter__()
            self._sb[name] = "raw"
            return h
        isz = 4 if dtype == F32 else 2
        off = self._arena().alloc(name, cols * isz, top=top)
        self._sb[name] = "arena"
        return Tile(self._arh[dtype], off // isz, cols)

    def sfree(self, name):
        kind = self._sb.pop(name)
        if kind == "arena":
            self._aralloc.free(name)

    def psum(self, name, cols, dtype=F32, parts=128):
        cm = self.nc.psum_tensor(name, [parts, cols], dtype)
        h = cm.__enter__()
        self._ps[name] = (h, cm)
        return h

    def sem(self, name):
        if name not in self._sems:
            cm = self.nc.semaphore(name)
            self._sems[name] = (cm.__enter__(), cm)
        return self._sems[name][0]

    def bump(self, name, k):
        self.semv[name] += k
        return self.semv[name]

    def _ensure(self):
        if self._blk is None:
            self._blk = self.nc.Block()
            self._blk.__enter__()
            self._q = {k: [] for k in ("tensor", "vector", "scalar", "gpsimd", "sync")}

    def cut(self):
        if self._blk is not None:
            for eng, fs in self._q.items():
                if fs:
                    getattr(self._blk, eng)(lambda e, fs=fs: [f(e) for f in fs])
            self._blk.__exit__(None, None, None)
            self._blk = None
            self._cnt.clear()

    def _emit(self, eng, f, n):
        self._ensure()
        self._q[eng].append(f)
        self._cnt[eng] += n
        lim = PE_LIMIT if eng == "tensor" else ENG_LIMIT
        if self._cnt[eng] >= lim:
            self.cut()

    def t(self, f, n=1):
        self._emit("tensor", f, n)

    def v(self, f, n=1):
        self._emit("vector", f, n)

    def s(self, f, n=1):
        self._emit("scalar", f, n)

    def g(self, f, n=1):
        self._emit("gpsimd", f, n)

    def y(self, f, n=1):
        self._emit("sync", f, n)

    def dma(self, q, out, in_, semname, wait=None):
        s = self.sem(semname)
        c = self.bump(semname, 16)
        if wait is not None:
            wsem, wval = wait
            ws = self.sem(wsem)
            self._emit(q, lambda e: e.wait_ge(ws, wval), 1)
        self._emit(q, lambda e: e.dma_start(out=out, in_=in_).then_inc(s, 16), 1)
        return c

    def wait(self, eng, semname, val):
        s = self.sem(semname)
        self._emit(eng, lambda e: e.wait_ge(s, val), 1)

    def all_wait(self, semname, val):
        for eng in ("tensor", "vector", "scalar", "gpsimd", "sync"):
            self.wait(eng, semname, val)
        self.cut()

    def finish(self):
        self.cut()

    def dbg_dump(self, name, ap_f, cols, parts=128, dtype=F32):
        self.cut()
        o = self.dout("dbg_" + name, [parts, cols], dtype)
        self.dma("sync", o[:, :], ap_f, "dbgs")
        self.wait("sync", "dbgs", self.semv["dbgs"])
        self.cut()



def _fm_vec(v):
    v = np.asarray(v, np.float32)
    return np.ascontiguousarray(v.reshape(-1, 128).T)


def _io(W):
    return np.ascontiguousarray(np.asarray(W, np.float32))


def _oi(W):
    return np.ascontiguousarray(np.asarray(W, np.float32).T)


def make_consts():
    c = {}
    c["ident_f32"] = np.eye(128, dtype=np.float32)
    c["tril_f32"] = np.tril(np.ones((128, 128), dtype=np.float32)).T.copy()
    c["ones_col_f32"] = np.ones((128, 1), dtype=np.float32)
    c["ones_col_b16"] = np.ones((128, 1), dtype=ml_dtypes.bfloat16)
    c["ones_row_f32"] = np.ones((1, 128), dtype=np.float32)
    sel = np.zeros((16, 16 * 128), dtype=np.float32)
    for e in range(16):
        sel[e, e * 128:(e + 1) * 128] = 1.0
    c["sel16_f32"] = sel
    blk = np.zeros((128, 4), dtype=ml_dtypes.bfloat16)
    for g in range(4):
        blk[g * 32:(g + 1) * 32, g] = 1.0
    c["blk32_b16"] = blk
    i = np.arange(16, dtype=np.float32)
    inv_freq = 1.0 / 10000.0 ** (i / 256.0)
    t = np.arange(2048, dtype=np.float32)
    freqs = np.outer(t, inv_freq)
    cos = np.cos(freqs).astype(ml_dtypes.bfloat16)
    sin = np.sin(freqs).astype(ml_dtypes.bfloat16)
    p_idx = np.arange(128)
    j = p_idx % 32
    d_idx = j % 16
    sgn = np.where(j < 16, -1.0, 1.0).astype(np.float32)
    CK = cos.astype(np.float32).T[d_idx]
    SK = sin.astype(np.float32).T[d_idx] * sgn[:, None]
    c["rope"] = np.concatenate(
        [(CK * 0.125), (SK * 0.125), CK, SK], axis=1).astype(ml_dtypes.bfloat16)
    return c


def pack_all(inputs):
    A = {k: np.asarray(v) for k, v in inputs.items()}
    p = {}
    p.update({"c_" + k: v for k, v in make_consts().items()})
    p["L2w_wq"] = _io(A["L2_attn_Wq"])
    p["L2w_wk"] = _io(A["L2_attn_Wk"])
    p["L2w_wv"] = _io(A["L2_attn_Wv"])
    p["L2w_wo"] = _io(A["L2_attn_Wo"])
    p["L2w_ffw"] = _oi(A["L2_ffn_weight"])
    p["L2w_ffb"] = _fm_vec(A["L2_ffn_bias"])
    p["L2w_alpha"] = np.asarray(A["L2_ffn_alpha"], np.float32).reshape(1, 1)
    p["L1w_w1"] = _oi(A["L1_ffn_w1_w"])
    p["L1w_w2"] = _oi(A["L1_ffn_w2_w"])
    p["L1w_h1"] = _oi(A["L1_ffn_h1_w"])
    p["L1w_h2"] = _oi(A["L1_ffn_h2_w"])
    p["L1w_w3"] = _oi(A["L1_ffn_w3_w"])
    p["L1w_s1"] = _fm_vec(A["L1_ffn_w1_s"])
    p["L1w_s2"] = _fm_vec(A["L1_ffn_w2_s"])
    p["L1w_sh1"] = _fm_vec(A["L1_ffn_h1_s"])
    p["L1w_sh2"] = _fm_vec(A["L1_ffn_h2_s"])
    p["L1w_s3"] = _fm_vec(A["L1_ffn_w3_s"])
    p["L3w_rw"] = _io(A["L3_ffn_router_w"])
    p["L3w_rb"] = np.asarray(A["L3_ffn_router_b"], np.float32).reshape(1, 16)
    for ex in range(16):
        p[f"L3w_w1_{ex}"] = _io(A["L3_ffn_w1"][ex])
        p[f"L3w_w2_{ex}"] = _io(A["L3_ffn_w2"][ex])
    p["L3w_b1"] = np.concatenate([_fm_vec(A["L3_ffn_b1"][ex]) for ex in range(16)], axis=1)
    p["L3w_b2"] = np.ascontiguousarray(np.asarray(A["L3_ffn_b2"], np.float32))
    p["L1a_wq"] = _io(A["L1_attn_Wq"])
    p["L1a_wl"] = _io(A["L1_attn_Wl"])
    p["L1a_wk"] = _io(A["L1_attn_Wk"])
    p["L1a_wv"] = _io(A["L1_attn_Wv"])
    p["L1a_wo"] = _io(A["L1_attn_Wo"])
    p["L1a_bias"] = np.concatenate(
        [_fm_vec(A["L1_attn_bq"]), _fm_vec(A["L1_attn_bl"]), _fm_vec(A["L1_attn_bk"]),
         _fm_vec(A["L1_attn_bo"]), _fm_vec(A["L1_attn_bv"])], axis=1)
    p["L0f_sg"] = _oi(A["L0_ffn_sg"])
    p["L0f_su"] = _oi(A["L0_ffn_su"])
    p["L0f_sd"] = _oi(A["L0_ffn_sd"])
    for ex in range(7):
        p[f"L0f_rg_{ex}"] = _oi(A["L0_ffn_rg"][ex])
        p[f"L0f_ru_{ex}"] = _oi(A["L0_ffn_ru"][ex])
        p[f"L0f_rd_{ex}"] = _oi(A["L0_ffn_rd"][ex])
    p["L0f_wr"] = _oi(A["L0_ffn_wr"])
    p["L0f_rb"] = np.asarray(A["L0_ffn_rb"], np.float32).reshape(1, 7)
    p["L0a_wq"] = _io(np.asarray(A["L0_attn_Wq"], np.float32).T)
    p["L0a_wk"] = _io(np.asarray(A["L0_attn_Wk"], np.float32).T)
    p["L0a_wv"] = _oi(A["L0_attn_Wv"])
    wpj = _oi(A["L0_attn_Wproj"])
    il = np.empty_like(wpj)
    for k in range(8):
        il[:, (2 * k) * 128:(2 * k + 1) * 128] = wpj[:, k * 128:(k + 1) * 128]
        il[:, (2 * k + 1) * 128:(2 * k + 2) * 128] = wpj[:, (8 + k) * 128:(9 + k) * 128]
    p["L0a_wproj"] = np.ascontiguousarray(il)
    p["L0a_lam"] = np.concatenate(
        [np.asarray(A[f"L0_attn_lambda_{n}"], np.float32) for n in ("q1", "q2", "k1", "k2")]).reshape(1, 256)
    p["L0a_alpha"] = np.asarray(A["L0_attn_alpha"], np.float32).reshape(1, 1)
    wqkv = _oi(A["L3_attn_w_qkv"])
    p["L3a_wqk"] = np.ascontiguousarray(wqkv[:, 0:2048])
    p["L3a_wv"] = np.ascontiguousarray(wqkv[:, 2048:3072])
    p["L3a_wo"] = _oi(A["L3_attn_w_o"])
    bqkv = np.asarray(A["L3_attn_b_qkv"], np.float32)
    p["L3a_bias"] = np.concatenate(
        [_fm_vec(bqkv[0:1024]), _fm_vec(bqkv[1024:2048]), _fm_vec(bqkv[2048:3072]),
         _fm_vec(A["L3_attn_b_o"])], axis=1)
    return p



def setup_env(P, packed):
    kr = P.nc._kernel_sem_range
    base = kr.start + 2
    if P.nc._bir_kernel_barrier_sem is not None:
        base += 1
    base += len(P.nc._monotonic_sems)
    rng = range(base, kr.stop)
    P.g(lambda g: g.dma_reset(rng))
    P.g(lambda g: g.sem_clear(rng))
    P.cut()
    e = P.env
    e["X"] = ChunkedX([P.sbuf(f"X{c}", T, F32) for c in range(NC)])
    e["XH"] = P.sbuf("XH", 16384, BF16)
    e["rows"] = P.sbuf("rows", 4096, F32)
    e["bc0"] = P.sbuf("bc0", 2048, F32)
    e["bc1"] = P.sbuf("bc1", 2048, F32)
    e["nscr"] = P.sbuf("nscr", 2048, F32)
    e["psA"] = P.psum("psA", 2048, F32)
    e["psB"] = P.psum("psB", 2048, F32)
    e["ident"] = P.sbuf("ident", 128, F32)
    e["tril"] = P.sbuf("tril", 128, F32)
    e["ones_c32"] = P.sbuf("ones_c32", 1, F32)
    e["ones_cb"] = P.sbuf("ones_cb", 1, BF16)
    e["ones_r32"] = P.sbuf("ones_r32", 128, F32, parts=1)
    P.dma("sync", e["ident"][:, :], P.din("c_ident_f32", packed["c_ident_f32"])[:, :], "cst")
    P.dma("sync", e["tril"][:, :], P.din("c_tril_f32", packed["c_tril_f32"])[:, :], "cst")
    P.dma("sync", e["ones_c32"][:, :], P.din("c_ones_col_f32", packed["c_ones_col_f32"])[:, :], "cst")
    P.dma("sync", e["ones_cb"][:, :], P.din("c_ones_col_b16", packed["c_ones_col_b16"])[:, :], "cst")
    P.dma("sync", e["ones_r32"][:, :], P.din("c_ones_row_f32", packed["c_ones_row_f32"])[:, :], "cst")
    xin = P.dout("xout", [128, 16384], F32)
    e["xout"] = xin
    xdram = P.din("xin", packed["xin0"])
    for c in range(NC):
        P.dma("sync", e["X"][:, c * T:(c + 1) * T], xdram[:, c * T:(c + 1) * T], "cst")
    P.all_wait("cst", P.semv["cst"])


def store_x(P):
    e = P.env
    P.cut()
    for c in range(NC):
        P.dma("sync", e["xout"][:, c * T:(c + 1) * T], P.env["X"][:, c * T:(c + 1) * T], "xst")
    P.wait("sync", "xst", P.semv["xst"])
    P.cut()


def park_x(P):
    e = P.env
    if "xpark" not in e:
        e["xpark"] = P.nc.dram_tensor("xpark", [128, 16384], F32).ap()
    P.cut()
    for c in range(NC):
        P.dma("sync", e["xpark"][:, c * T:(c + 1) * T], e["X"][:, c * T:(c + 1) * T], "xpk")
    P.all_wait("xpk", P.semv["xpk"])
    for c in range(NC):
        P.sfree(f"X{c}")
    e["X"] = None


def unpark_x(P):
    e = P.env
    e["X"] = ChunkedX([P.sbuf(f"X{c}", T, F32) for c in range(NC)])
    P.cut()
    for c in range(NC):
        P.dma("sync", e["X"][:, c * T:(c + 1) * T], e["xpark"][:, c * T:(c + 1) * T], "xup")
    P.all_wait("xup", P.semv["xup"])


def free_xh(P):
    P.sfree("XH")
    P.env["XH"] = None


def emit_ln_v2(P):
    e = P.env
    if e.get("XH") is None:
        e["XH"] = P.sbuf("XH", 16384, BF16)
    X, XH = e["X"], e["XH"]
    rows, psA, psB = e["rows"], e["psA"], e["psB"]
    XB = P.sbuf("XB", 16384, BF16)
    for c in range(NC):
        P.v(lambda v, c=c: v.tensor_copy(XB[:, c * T:(c + 1) * T], X[:, c * T:(c + 1) * T]))
    P.cut()
    for c in range(NC):
        for j in range(NT):
            P.t(lambda t, c=c, j=j: t.matmul(
                psA[0:1, j * 512:(j + 1) * 512],
                e["ones_cb"][:, :],
                XB[:, c * T + j * 512: c * T + (j + 1) * 512],
                start=(c == 0), stop=(c == NC - 1)))
    P.cut()
    for c in range(NC):
        P.s(lambda s, c=c: s.activation(XB[:, c * T:(c + 1) * T], XB[:, c * T:(c + 1) * T], AF.Square))
    P.cut()
    for c in range(NC):
        for j in range(NT):
            P.t(lambda t, c=c, j=j: t.matmul(
                psB[0:1, j * 512:(j + 1) * 512],
                e["ones_cb"][:, :],
                XB[:, c * T + j * 512: c * T + (j + 1) * 512],
                start=(c == 0), stop=(c == NC - 1)))
    P.cut()
    mu = rows[0:1, 0:T]
    w = rows[0:1, T:2 * T]
    P.s(lambda s: s.activation(mu, psA[0:1, 0:T], AF.Copy, scale=1.0 / 1024.0))
    P.cut()
    P.v(lambda v: v.scalar_tensor_tensor(w, mu, -1.0, mu, ALU.mult, ALU.mult))
    P.cut()
    P.v(lambda v: v.scalar_tensor_tensor(w, psB[0:1, 0:T], 1.0 / 1024.0, w, ALU.mult, ALU.add))
    P.cut()
    P.constf(1e-5)
    P.s(lambda s: s.activation(w, w, AF.Sqrt, bias=1e-5))
    P.cut()
    P.v(lambda v: v.reciprocal(w, w))
    P.cut()
    P.v(lambda v: v.scalar_tensor_tensor(mu, mu, -1.0, w, ALU.mult, ALU.mult))
    P.cut()
    for j in range(NT):
        P.t(lambda t, j=j: t.matmul(psA[:, j * 512:(j + 1) * 512], e["ones_r32"][:, :],
                                    w[:, j * 512:(j + 1) * 512], start=True, stop=True))
        P.t(lambda t, j=j: t.matmul(psB[:, j * 512:(j + 1) * 512], e["ones_r32"][:, :],
                                    mu[:, j * 512:(j + 1) * 512], start=True, stop=True))
    P.cut()
    P.v(lambda v: v.tensor_copy(e["bc0"][:, :], psA[:, :]))
    P.v(lambda v: v.tensor_copy(e["bc1"][:, :], psB[:, :]))
    P.cut()
    for c in range(NC):
        P.v(lambda v, c=c: v.tensor_tensor(e["nscr"][:, :], X[:, c * T:(c + 1) * T], e["bc0"][:, :], ALU.mult))
        P.v(lambda v, c=c: v.tensor_tensor(XH[:, c * T:(c + 1) * T], e["nscr"][:, :], e["bc1"][:, :], ALU.add))
    P.cut()
    P.sfree("XB")


class WFull:

    def __init__(self, P, name, arr, sem=None):
        self.P = P
        self.I, self.O = arr.shape
        self.name = name
        self.sem = sem or ("w_" + name)
        self.dram = P.din("W_" + name, arr)
        self.sb = P.sbuf("wsb_" + name, (self.I // 128) * self.O, BF16)
        P.dma("gpsimd", self.sb[:, :],
              self.dram.rearrange("(c p) o -> p c o", p=128), self.sem)
        self.ready = P.semv[self.sem]

    def lhsT(self, ic, o0, owidth=128):
        return self.sb[:, ic * self.O + o0: ic * self.O + o0 + owidth]

    def free(self):
        self.P.sfree("wsb_" + self.name)


def mm_fm(P, W, rhs_sb, rhs_base, n_i, n_o, evac, tcols=T, wait_sem=None, wait_val=None):
    e = P.env
    nt = tcols // 512
    for oc in range(n_o + 1):
        if oc < n_o:
            ps = e["psA"] if oc % 2 == 0 else e["psB"]
            if oc == 0 and wait_sem is not None:
                P.wait("tensor", wait_sem, wait_val)
            for ic in range(n_i):
                for j in range(nt):
                    P.t(lambda t, oc=oc, ic=ic, j=j, ps=ps: t.matmul(
                        ps[:, j * 512:(j + 1) * 512],
                        W.lhsT(ic, oc * 128),
                        rhs_sb[:, rhs_base + ic * tcols + j * 512: rhs_base + ic * tcols + (j + 1) * 512],
                        start=(ic == 0), stop=(ic == n_i - 1)))
        if oc > 0:
            psp = e["psA"] if (oc - 1) % 2 == 0 else e["psB"]
            evac(oc - 1, psp)
        P.cut()


def mm_stream(P, name, arr, rhs_sb, rhs_off, rhs_stride, tcols, evac, PW=512,
              psum_cols=None):
    e = P.env
    I, O = arr.shape
    assert O % PW == 0, (O, PW)
    nic, nop, opp = I // 128, O // 128, PW // 128
    npan = O // PW
    bname = ["wp_0", "wp_1"]
    bufs = [P.sbuf(bname[s], nic * PW, BF16) for s in (0, 1)]
    sems = ["ws_0", "ws_1"]
    dram = P.din("W_" + name, arr)
    panel_sem = {}
    pe_done = {}
    dram3 = dram.rearrange("(c p) o -> p c o", p=128)

    def issue(p):
        s = p % 2
        if p >= 2:
            P.wait("gpsimd", "pep", pe_done[p - 2])
        panel_sem[p] = P.dma("gpsimd", bufs[s][:, :],
                             dram3[:, :, p * PW:(p + 1) * PW], sems[s])
    issue(0)
    if npan > 1:
        issue(1)
    nt = tcols // 512
    pcols = psum_cols if psum_cols is not None else tcols
    pep = P.sem("pep")
    if pcols <= 1024:
        tiles = [Tile(e["psA"], 0, pcols), Tile(e["psB"], 0, pcols),
                 Tile(e["psA"], 1024, pcols), Tile(e["psB"], 1024, pcols)]
        perblk = 2
    else:
        tiles = [Tile(e["psA"], 0, pcols), Tile(e["psB"], 0, pcols)]
        perblk = 1
    lag = 2 if len(tiles) == 4 else 1
    for oc in range(nop + lag):
        if oc < nop:
            p, s, o0 = oc // opp, (oc // opp) % 2, (oc % opp) * 128
            ps = tiles[oc % len(tiles)]
            if oc % opp == 0:
                P.wait("tensor", sems[s], panel_sem[p])
            for ic in range(nic):
                for j in range(nt):
                    last = (ic == nic - 1 and j == nt - 1 and
                            (oc % opp == opp - 1 or oc == nop - 1))

                    def mk(oc=oc, ic=ic, j=j, ps=ps, s=s, o0=o0, last=last):
                        def f(t):
                            if callable(rhs_sb):
                                rhs = rhs_sb(ic, j)
                            else:
                                rhs = rhs_sb[:, rhs_off + ic * rhs_stride + j * 512:
                                             rhs_off + ic * rhs_stride + min((j + 1) * 512, tcols)]
                            ins = t.matmul(
                                ps[:, j * 512:min((j + 1) * 512, pcols)],
                                bufs[s][:, ic * PW + o0: ic * PW + o0 + 128],
                                rhs,
                                start=(ic == 0), stop=(ic == nic - 1))
                            if last:
                                ins.then_inc(pep, 1)
                        return f
                    P.t(mk())
            if oc % opp == opp - 1 or oc == nop - 1:
                pe_done[p] = P.bump("pep", 1)
                if p + 2 < npan:
                    issue(p + 2)
        if oc - lag >= 0:
            evac(oc - lag, tiles[(oc - lag) % len(tiles)])
        if oc % perblk == perblk - 1 or oc >= nop - 1:
            P.cut()
    P.cut()
    P.sfree(bname[0])
    P.sfree(bname[1])


def emit_l1_ffn(P, packed):
    e = P.env
    emit_ln_v2(P)
    park_x(P)
    XH = e["XH"]
    if "l1scales" not in P._sb:
        sc = P.sbuf("l1scales", 104, F32, top=True)
        sh2 = P.sbuf("l1sh2", 32, F32, top=True)
        e["l1scales"], e["l1sh2"] = sc, sh2
        P.dma("sync", sc[:, 0:32], P.din("L1w_s1", packed["L1w_s1"])[:, :], "l1sc")
        P.dma("sync", sc[:, 32:64], P.din("L1w_s2", packed["L1w_s2"])[:, :], "l1sc")
        P.dma("sync", sc[:, 64:96], P.din("L1w_sh1", packed["L1w_sh1"])[:, :], "l1sc")
        P.dma("sync", sc[:, 96:104], P.din("L1w_s3", packed["L1w_s3"])[:, :], "l1sc")
        P.dma("sync", sh2[:, :], P.din("L1w_sh2", packed["L1w_sh2"])[:, :], "l1sc")
        P.wait("vector", "l1sc", P.semv["l1sc"])
        P.wait("scalar", "l1sc", P.semv["l1sc"])
        P.cut()
    sc, sh2 = e["l1scales"], e["l1sh2"]
    t1 = P.sbuf("l1t1", 16384, BF16)
    t2 = P.sbuf("l1t2", 16384, BF16)
    yv = P.sbuf("l1y", 4096, F32)
    TQ = 512
    for qq in range(4):
        def ev1(oc, ps):
            P.s(lambda s, oc=oc: s.activation(t1[:, oc * TQ:(oc + 1) * TQ], ps[:, 0:TQ],
                                              AF.Silu, scale=sc[:, oc:oc + 1]))
        mm_stream(P, "l1w1", packed["L1w_w1"], XH, qq * TQ, T, TQ, ev1, PW=512)
        def ev2(oc, ps):
            P.v(lambda v, oc=oc: v.tensor_scalar(e["nscr"][:, 0:TQ], ps[:, 0:TQ],
                                                 sc[:, 32 + oc:33 + oc], None, ALU.mult))
            P.v(lambda v, oc=oc: v.tensor_tensor(t1[:, oc * TQ:(oc + 1) * TQ],
                                                 e["nscr"][:, 0:TQ],
                                                 t1[:, oc * TQ:(oc + 1) * TQ], ALU.mult))
        mm_stream(P, "l1w2", packed["L1w_w2"], XH, qq * TQ, T, TQ, ev2, PW=512)
        def ev3(oc, ps):
            P.s(lambda s, oc=oc: s.activation(t2[:, oc * TQ:(oc + 1) * TQ], ps[:, 0:TQ],
                                              AF.Gelu, scale=sc[:, 64 + oc:65 + oc]))
        mm_stream(P, "l1h1", packed["L1w_h1"], t1, 0, TQ, TQ, ev3, PW=128)
        def ev4(oc, ps):
            P.s(lambda s, oc=oc: s.activation(t1[:, oc * TQ:(oc + 1) * TQ], ps[:, 0:TQ],
                                              AF.Gelu, scale=sh2[:, oc:oc + 1]))
        mm_stream(P, "l1h2", packed["L1w_h2"], t2, 0, TQ, TQ, ev4, PW=128)
        def ev5(oc, ps):
            P.v(lambda v, oc=oc: v.tensor_scalar(yv[:, oc * TQ:(oc + 1) * TQ], ps[:, 0:TQ],
                                                 sc[:, 96 + oc:97 + oc], None, ALU.mult))
        mm_stream(P, "l1w3", packed["L1w_w3"], t1, 0, TQ, TQ, ev5, PW=128)
        P.cut()
        for c in range(NC):
            sl = e["nscr"][:, (c % 2) * 512:(c % 2) * 512 + 512]
            v1 = P.dma("sync", sl, e["xpark"][:, c * T + qq * TQ: c * T + qq * TQ + TQ], "xq")
            P.wait("vector", "xq", v1)
            P.v(lambda v, c=c, sl=sl: v.scalar_tensor_tensor(
                sl, yv[:, c * TQ:(c + 1) * TQ], 0.5, sl, ALU.mult, ALU.add))
            P.cut()
            P.dma("sync", e["xpark"][:, c * T + qq * TQ: c * T + qq * TQ + TQ], sl, "xqs")
        P.wait("sync", "xqs", P.semv["xqs"])
        P.cut()
    P.sfree("l1t1")
    P.sfree("l1t2")
    P.sfree("l1y")
    unpark_x(P)


def emit_l3_ffn(P, packed):
    e = P.env
    psA, psB, rows = P.env["psA"], P.env["psB"], P.env["rows"]
    emit_ln_v2(P)
    park_x(P)
    XH = e["XH"]
    if "l3rw" not in P._sb:
        rw = P.sbuf("l3rw", 128, BF16, top=True)
        rbr = P.sbuf("l3rb", 16, F32, top=True)
        b1c = P.sbuf("l3b1", 256, F32, top=True)
        b2c = P.sbuf("l3b2", 1024, BF16, top=True)
        e["l3rw"], e["l3rb"], e["l3b1"], e["l3b2"] = rw, rbr, b1c, b2c
        P.dma("gpsimd", rw[:, :], P.din("L3w_rw", packed["L3w_rw"]).rearrange("(c p) e -> p c e", p=128), "l3c")
        P.dma("sync", rbr[0:1, :], P.din("L3w_rb", packed["L3w_rb"])[:, :], "l3c")
        P.dma("sync", b1c[:, :], P.din("L3w_b1", packed["L3w_b1"])[:, :], "l3c")
        P.dma("gpsimd", b2c[0:16, :], P.din("L3w_b2", packed["L3w_b2"])[:, :], "l3c")
        P.all_wait("l3c", P.semv["l3c"])
    rw, rbr, b1c, b2c = e["l3rw"], e["l3rb"], e["l3b1"], e["l3b2"]
    sel = ensure_sel(P, packed)

    logit = P.sbuf("l3lg", 256, F32)
    P.t(lambda t: t.matmul(psB[:, 32:48], e["ones_r32"][:, :], rbr[0:1, :], start=True, stop=True))
    P.cut()
    P.v(lambda v: v.tensor_copy(e["bc1"][:, 0:16], psB[:, 32:48]))
    P.cut()
    for tt in range(17):
        if tt < 16:
            ps = psA if tt % 2 == 0 else psB
            for ic in range(NC):
                P.t(lambda t, tt=tt, ic=ic, ps=ps: t.matmul(
                    ps[:, 0:16],
                    XH[:, ic * T + tt * 128: ic * T + tt * 128 + 128],
                    rw[:, ic * 16:(ic + 1) * 16],
                    start=(ic == 0), stop=(ic == NC - 1)))
        if tt > 0:
            psp = psA if (tt - 1) % 2 == 0 else psB
            P.v(lambda v, tt=tt, psp=psp: v.tensor_tensor(
                logit[:, (tt - 1) * 16:tt * 16], psp[:, 0:16], e["bc1"][:, 0:16], ALU.add))
        P.cut()

    pbuf = P.sbuf("l3p", 256, F32)
    mcol = P.sbuf("l3m", 16, F32)
    dsum = P.sbuf("l3d", 16, F32)
    keep = P.sbuf("l3keep", 256, F32)
    for tt in range(16):
        P.v(lambda v, tt=tt: v.tensor_reduce(mcol[:, tt:tt + 1], logit[:, tt * 16:(tt + 1) * 16],
                                             AX.X, ALU.max, negate=True))
    P.cut()
    for tt in range(16):
        P.s(lambda s, tt=tt: s.activation(pbuf[:, tt * 16:(tt + 1) * 16],
                                          logit[:, tt * 16:(tt + 1) * 16],
                                          AF.Exp, bias=mcol[:, tt:tt + 1]))
    P.cut()
    P.v(lambda v: v.tensor_copy(logit[:, :], pbuf[:, :]))
    P.cut()
    for step in range(4):
        for tt in range(16):
            P.v(lambda v, tt=tt: v.tensor_reduce(mcol[:, tt:tt + 1], logit[:, tt * 16:(tt + 1) * 16],
                                                 AX.X, ALU.max))
        P.cut()
        if step == 0:
            P.v(lambda v: v.tensor_copy(dsum[:, :], mcol[:, :]))
        else:
            P.v(lambda v: v.tensor_tensor(dsum[:, :], dsum[:, :], mcol[:, :], ALU.add))
        if step < 3:
            for tt in range(16):
                P.v(lambda v, tt=tt: v.tensor_scalar(keep[:, tt * 16:(tt + 1) * 16],
                                                     logit[:, tt * 16:(tt + 1) * 16],
                                                     mcol[:, tt:tt + 1], None, ALU.is_lt))
            for tt in range(16):
                P.v(lambda v, tt=tt: v.tensor_tensor(logit[:, tt * 16:(tt + 1) * 16],
                                                     logit[:, tt * 16:(tt + 1) * 16],
                                                     keep[:, tt * 16:(tt + 1) * 16], ALU.mult))
        P.cut()
    for tt in range(16):
        P.v(lambda v, tt=tt: v.tensor_scalar(keep[:, tt * 16:(tt + 1) * 16],
                                             pbuf[:, tt * 16:(tt + 1) * 16],
                                             mcol[:, tt:tt + 1], None, ALU.is_ge))
    P.v(lambda v: v.reciprocal(dsum[:, :], dsum[:, :]))
    P.cut()
    for tt in range(16):
        P.v(lambda v, tt=tt: v.tensor_scalar(pbuf[:, tt * 16:(tt + 1) * 16],
                                             pbuf[:, tt * 16:(tt + 1) * 16],
                                             dsum[:, tt:tt + 1], None, ALU.mult))
    P.cut()
    for tt in range(16):
        P.v(lambda v, tt=tt: v.tensor_tensor(pbuf[:, tt * 16:(tt + 1) * 16],
                                             pbuf[:, tt * 16:(tt + 1) * 16],
                                             keep[:, tt * 16:(tt + 1) * 16], ALU.mult))
    P.cut()
    wrow = P.sbuf("l3wr", 2048, F32)
    wrowb = P.sbuf("l3wrb", 2048, BF16)
    for tt in range(16):
        P.t(lambda t, tt=tt: t.transpose(psA[0:16, tt * 128:(tt + 1) * 128],
                                         pbuf[:, tt * 16:(tt + 1) * 16], e["ident"][:, :]))
    P.cut()
    P.v(lambda v: v.tensor_copy(wrow[0:16, :], psA[0:16, 0:2048]))
    P.v(lambda v: v.tensor_copy(wrowb[0:16, :], psA[0:16, 0:2048]))
    P.cut()
    P.sfree("l3lg")
    P.sfree("l3p")
    P.sfree("l3m")
    P.sfree("l3d")
    P.sfree("l3keep")

    He = P.sbuf("l3he", 16384, BF16)
    ACC = P.sbuf("l3acc", 8192, F32)
    TH = 1024
    for half in range(2):
        q0 = half * TH
        for ex in range(16):
            P.t(lambda t, ex=ex: t.matmul(psB[:, 1024:1536], sel[0:16, ex * 128:(ex + 1) * 128],
                                          wrow[0:16, q0:q0 + 512], start=True, stop=True))
            P.t(lambda t, ex=ex: t.matmul(psB[:, 1536:2048], sel[0:16, ex * 128:(ex + 1) * 128],
                                          wrow[0:16, q0 + 512:q0 + TH], start=True, stop=True))
            P.cut()
            P.v(lambda v: v.tensor_copy(e["bc0"][:, 0:TH], psB[:, 1024:2048]))
            P.cut()

            def ev1(oc, ps, ex=ex):
                P.s(lambda s, oc=oc, ex=ex, ps=ps: s.activation(
                    He[:, oc * TH:(oc + 1) * TH], ps[:, 0:TH], AF.Gelu,
                    bias=b1c[:, ex * 16 + oc:ex * 16 + oc + 1]))
            mm_stream(P, f"l3w1_{ex}", packed[f"L3w_w1_{ex}"], XH, q0, T, TH, ev1, PW=512)

            def ev2(oc, ps, ex=ex):
                P.v(lambda v, oc=oc, ps=ps: v.tensor_tensor(
                    e["nscr"][:, 0:TH], ps[:, 0:TH], e["bc0"][:, 0:TH], ALU.mult))
                if ex == 0:
                    P.v(lambda v, oc=oc: v.tensor_copy(ACC[:, oc * TH:(oc + 1) * TH],
                                                       e["nscr"][:, 0:TH]))
                else:
                    P.v(lambda v, oc=oc: v.tensor_tensor(ACC[:, oc * TH:(oc + 1) * TH],
                                                         ACC[:, oc * TH:(oc + 1) * TH],
                                                         e["nscr"][:, 0:TH], ALU.add))
            mm_stream(P, f"l3w2_{ex}", packed[f"L3w_w2_{ex}"], He, 0, TH, TH, ev2, PW=512)
        for oc in range(NC):
            ps = psA if oc % 2 == 0 else psB
            for j in range(2):
                P.t(lambda t, oc=oc, j=j, ps=ps: t.matmul(
                    ps[:, j * 512:(j + 1) * 512],
                    b2c[0:16, oc * 128:(oc + 1) * 128],
                    wrowb[0:16, q0 + j * 512:q0 + (j + 1) * 512],
                    start=True, stop=True))
            P.cut()
            P.v(lambda v, oc=oc, ps=ps: v.tensor_tensor(
                ACC[:, oc * TH:(oc + 1) * TH], ACC[:, oc * TH:(oc + 1) * TH],
                ps[:, 0:TH], ALU.add))
            P.cut()
        for c in range(NC):
            sl = e["nscr"][:, (c % 2) * 1024:(c % 2) * 1024 + 1024]
            v1 = P.dma("sync", sl, e["xpark"][:, c * T + q0: c * T + q0 + TH], "xq")
            P.wait("vector", "xq", v1)
            P.v(lambda v, c=c, sl=sl: v.scalar_tensor_tensor(
                sl, ACC[:, c * TH:(c + 1) * TH], 0.5, sl, ALU.mult, ALU.add))
            P.cut()
            P.dma("sync", e["xpark"][:, c * T + q0: c * T + q0 + TH], sl, "xqs")
        P.wait("sync", "xqs", P.semv["xqs"])
        P.cut()
    P.sfree("l3he")
    P.sfree("l3acc")
    P.sfree("l3wr")
    P.sfree("l3wrb")
    unpark_x(P)


def attn_core_64(P, Q, Kb, v_fn, Yatt, bv_fn, sqs, s_scr, Pb, cells):
    e = P.env
    psA, psB, rows = e["psA"], e["psB"], e["rows"]
    for h in range(16):
        hc, hr = h // 2, (h % 2) * 64
        P.s(lambda s, hc=hc, hr=hr: s.activation(sqs[0:64, 0:T], Q[hr:hr + 64, hc * T:(hc + 1) * T], AF.Square))
        P.cut()
        P.t(lambda t: t.matmul(psA[0:1, 0:512], e["ones_cb"][0:64, 0:1], sqs[0:64, 0:512], start=True, stop=True))
        P.t(lambda t: t.matmul(psA[0:1, 512:1024], e["ones_cb"][0:64, 0:1], sqs[0:64, 512:1024], start=True, stop=True))
        P.t(lambda t: t.matmul(psA[0:1, 1024:1536], e["ones_cb"][0:64, 0:1], sqs[0:64, 1024:1536], start=True, stop=True))
        P.t(lambda t: t.matmul(psA[0:1, 1536:2048], e["ones_cb"][0:64, 0:1], sqs[0:64, 1536:2048], start=True, stop=True))
        P.cut()
        P.s(lambda s, hc=hc, hr=hr: s.activation(sqs[0:64, 0:T], Kb[hr:hr + 64, hc * T:(hc + 1) * T], AF.Square))
        P.s(lambda s: s.activation(rows[0:1, 0:T], psA[0:1, 0:T], AF.Sqrt))
        P.cut()
        P.t(lambda t: t.matmul(psB[0:1, 0:512], e["ones_cb"][0:64, 0:1], sqs[0:64, 0:512], start=True, stop=True))
        P.t(lambda t: t.matmul(psB[0:1, 512:1024], e["ones_cb"][0:64, 0:1], sqs[0:64, 512:1024], start=True, stop=True))
        P.t(lambda t: t.matmul(psB[0:1, 1024:1536], e["ones_cb"][0:64, 0:1], sqs[0:64, 1024:1536], start=True, stop=True))
        P.t(lambda t: t.matmul(psB[0:1, 1536:2048], e["ones_cb"][0:64, 0:1], sqs[0:64, 1536:2048], start=True, stop=True))
        P.cut()
        P.s(lambda s: s.activation(rows[0:1, T:2 * T], psB[0:1, 0:T], AF.Copy))
        P.cut()
        P.v(lambda v: v.tensor_reduce(cells[0:1, 0:1], rows[0:1, T:2 * T], AX.X, ALU.max))
        P.cut()
        P.s(lambda s: s.activation(cells[0:1, 1:2], cells[0:1, 0:1], AF.Sqrt))
        P.cut()
        P.v(lambda v: v.tensor_scalar(cells[0:1, 2:3], cells[0:1, 1:2], -1.0, None, ALU.mult))
        P.cut()
        P.v(lambda v: v.tensor_scalar(rows[0:1, T:2 * T], rows[0:1, 0:T],
                                      cells[0:1, 2:3], None, ALU.mult))
        P.cut()
        for j in range(NT):
            P.t(lambda t, j=j: t.matmul(psA[:, j * 512:(j + 1) * 512], e["ones_r32"][:, :],
                                        rows[0:1, T + j * 512:T + (j + 1) * 512], start=True, stop=True))
        P.cut()
        P.v(lambda v: v.tensor_copy(e["bc0"][:, :], psA[:, :]))
        P.cut()
        for qh in range(2):
            q0 = qh * 1024
            fs0, ss0, es0, pv0 = P.semv["afs"], P.semv["ass"], P.semv["aes"], P.semv["apv"]
            afs, aes, apv = P.sem("afs"), P.sem("aes"), P.sem("apv")
            ass = P.sem("ass")

            def emit_pv(kc):
                P.wait("tensor", "aes", es0 + kc + 1)
                P.t(lambda t, kc=kc: t.matmul(
                    psA[0:65, 0:512], v_fn(kc, h),
                    Pb[:, (kc % 2) * 1024:(kc % 2) * 1024 + 512],
                    start=(kc == 0), stop=(kc == 15)))

                def fpv(t, kc=kc):
                    t.matmul(psA[0:65, 512:1024], v_fn(kc, h),
                             Pb[:, (kc % 2) * 1024 + 512:(kc % 2) * 1024 + 1024],
                             start=(kc == 0), stop=(kc == 15)).then_inc(apv, 1)
                P.t(fpv)
                P.bump("apv", 1)

            for j in range(16):
                sps = psA if j % 2 == 0 else psB
                if j >= 2:
                    P.wait("tensor", "ass", ss0 + j - 1)
                P.t(lambda t, j=j, sps=sps, hc=hc, hr=hr: t.matmul(
                    sps[:, 1024:1536],
                    Kb[hr:hr + 64, hc * T + j * 128: hc * T + j * 128 + 128],
                    Q[hr:hr + 64, hc * T + q0: hc * T + q0 + 512],
                    start=True, stop=True))

                def ffill(t, j=j, sps=sps, hc=hc, hr=hr):
                    t.matmul(sps[:, 1536:2048],
                             Kb[hr:hr + 64, hc * T + j * 128: hc * T + j * 128 + 128],
                             Q[hr:hr + 64, hc * T + q0 + 512: hc * T + q0 + 1024],
                             start=True, stop=True).then_inc(afs, 1)
                P.t(ffill)
                P.bump("afs", 1)
                if j >= 2:
                    emit_pv(j - 2)
                P.wait("vector", "afs", fs0 + j + 1)
                if j >= 2:
                    P.wait("vector", "aes", es0 + j - 1)

                def fsub(v, j=j, sps=sps):
                    v.tensor_tensor(
                        s_scr[:, (j % 2) * 1024:(j % 2) * 1024 + 1024],
                        sps[:, 1024:2048], e["bc0"][:, q0:q0 + 1024],
                        ALU.add).then_inc(ass, 1)
                P.v(fsub)
                P.bump("ass", 1)
                P.wait("scalar", "ass", ss0 + j + 1)
                if j >= 2:
                    P.wait("scalar", "apv", pv0 + j - 1)

                def fexp(sc, j=j):
                    sc.activation(Pb[:, (j % 2) * 1024:(j % 2) * 1024 + 1024],
                                  s_scr[:, (j % 2) * 1024:(j % 2) * 1024 + 1024],
                                  AF.Exp).then_inc(aes, 1)
                P.s(fexp)
                P.bump("aes", 1)
            emit_pv(14)
            emit_pv(15)
            P.cut()
            P.s(lambda s: s.activation(rows[0:1, 0:1024], psA[64:65, 0:1024], AF.Copy))
            P.cut()
            P.v(lambda v: v.reciprocal(rows[0:1, 0:1024], rows[0:1, 0:1024]))
            P.cut()
            P.t(lambda t: t.matmul(psB[0:64, 0:512], e["ones_r32"][0:1, 0:64],
                                   rows[0:1, 0:512], start=True, stop=True))
            P.t(lambda t: t.matmul(psB[0:64, 512:1024], e["ones_r32"][0:1, 0:64],
                                   rows[0:1, 512:1024], start=True, stop=True))
            P.cut()
            P.v(lambda v: v.tensor_copy(e["bc1"][0:64, 0:1024], psB[0:64, 0:1024]))
            P.cut()
            P.v(lambda v: v.tensor_tensor(s_scr[0:64, 0:1024], psA[0:64, 0:1024],
                                          e["bc1"][0:64, 0:1024], ALU.mult))
            P.v(lambda v, hc=hc, hr=hr, q0=q0: v.tensor_scalar(
                Yatt[hr:hr + 64, hc * T + q0: hc * T + q0 + 1024],
                s_scr[0:64, 0:1024], bv_fn(hr, hc), None, ALU.add))
            P.cut()


def emit_l1_attn(P, packed):
    e = P.env
    psA, psB, rows = e["psA"], e["psB"], e["rows"]
    emit_ln_v2(P)
    park_x(P)
    XH = e["XH"]
    bias = P.sbuf("l1abias", 48, F32, top=True)
    P.dma("sync", bias[:, 0:34], P.din("L1a_bias", packed["L1a_bias"])[:, :], "l1ab")
    P.wait("vector", "l1ab", P.semv["l1ab"])
    P.cut()
    P.v(lambda v: v.tensor_scalar(bias[:, 40:48], bias[:, 0:8], 0.125, None, ALU.mult))
    P.cut()
    BQ, BL, BK, BO, BV, BQS = 40, 8, 10, 18, 26, 40

    Q = P.sbuf("l1q", 16384, BF16)
    lat = P.sbuf("l1lat", 4096, BF16)
    wq = WFull(P, "l1wq", packed["L1a_wq"])

    def evq(oc, ps):
        P.v(lambda v, oc=oc: v.tensor_scalar(Q[:, oc * T:(oc + 1) * T], ps[:, :],
                                             0.125, bias[:, 40 + oc:41 + oc], ALU.mult, ALU.add))
    mm_fm(P, wq, XH, 0, NC, NC, evq, wait_sem=wq.sem, wait_val=wq.ready)
    wq.free()
    wl = WFull(P, "l1wl", packed["L1a_wl"])

    def evl(oc, ps):
        P.v(lambda v, oc=oc: v.tensor_scalar(lat[:, oc * T:(oc + 1) * T], ps[:, :],
                                             bias[:, 8 + oc:9 + oc], None, ALU.add))
    mm_fm(P, wl, XH, 0, NC, 2, evl, wait_sem=wl.sem, wait_val=wl.ready)
    wl.free()
    free_xh(P)
    Kb = P.sbuf("l1k", 16384, BF16)
    V = P.sbuf("l1v", 16 * 1040, BF16)
    Yatt = Q
    wk = WFull(P, "l1wk", packed["L1a_wk"])

    def evk(oc, ps):
        P.v(lambda v, oc=oc: v.tensor_scalar(Kb[:, oc * T:(oc + 1) * T], ps[:, :],
                                             bias[:, 10 + oc:11 + oc], None, ALU.add))
    mm_fm(P, wk, lat, 0, 2, NC, evk, wait_sem=wk.sem, wait_val=wk.ready)
    wk.free()
    P.v(lambda v: v.memset(V[:, :], 1.0))
    P.cut()
    wv = WFull(P, "l1wv", packed["L1a_wv"])
    P.wait("tensor", wv.sem, wv.ready)
    for tt in range(17):
        if tt < 16:
            ps = psA if tt % 2 == 0 else psB
            for ic in range(2):
                for j in range(2):
                    P.t(lambda t, tt=tt, ic=ic, j=j, ps=ps: t.matmul(
                        ps[:, j * 512:(j + 1) * 512],
                        lat[:, ic * T + tt * 128: ic * T + tt * 128 + 128],
                        wv.sb[:, ic * 1024 + j * 512: ic * 1024 + (j + 1) * 512],
                        start=(ic == 0), stop=(ic == 1)))
        if tt > 0:
            psp = psA if (tt - 1) % 2 == 0 else psB
            dst = V[:, (tt - 1) * 1040:tt * 1040].rearrange("p (h d) -> p h d", d=65)[:, :, 0:64]
            P.v(lambda v, psp=psp, dst=dst: v.tensor_copy(dst, psp[:, 0:1024]))
        P.cut()
    wv.free()
    P.sfree("l1lat")
    sqs = P.sbuf("l1sqs", 2048, BF16)
    s_scr = P.sbuf("l1ss", 2048, F32)
    Pb = P.sbuf("l1pb", 2048, BF16)
    cells = P.sbuf("l1cells", 8, F32)

    attn_core_64(P, Q, Kb, lambda kc, h: V[:, kc * 1040 + h * 65: kc * 1040 + h * 65 + 65],
                 Yatt, lambda hr, hc: bias[hr:hr + 64, 26 + hc:27 + hc],
                 sqs, s_scr, Pb, cells)
    P.sfree("l1k")
    P.sfree("l1v")
    P.sfree("l1sqs")
    P.sfree("l1ss")
    P.sfree("l1pb")
    P.sfree("l1cells")
    unpark_x(P)
    X = e["X"]
    wo = WFull(P, "l1wo", packed["L1a_wo"])

    def evo(oc, ps):
        P.v(lambda v, oc=oc: v.scalar_tensor_tensor(
            X[:, oc * T:(oc + 1) * T], ps[:, :], bias[:, 18 + oc:19 + oc],
            X[:, oc * T:(oc + 1) * T], ALU.add, ALU.add))
    mm_fm(P, wo, Yatt, 0, NC, NC, evo, wait_sem=wo.sem, wait_val=wo.ready)
    wo.free()
    P.sfree("l1q")


def emit_l3_attn(P, packed):
    e = P.env
    psA, psB = e["psA"], e["psB"]
    emit_ln_v2(P)
    park_x(P)
    XH = e["XH"]
    for bn in ("selw", "l3rw", "l3rb", "l3b1", "l3b2"):
        if bn in P._sb:
            P.sfree(bn)
            P.env.pop(bn, None)
    bias = P.sbuf("l3abias", 40, F32, top=True)
    P.dma("sync", bias[:, 0:32], P.din("L3a_bias", packed["L3a_bias"])[:, :], "l3ab")
    P.wait("vector", "l3ab", P.semv["l3ab"])
    P.cut()
    P.v(lambda v: v.tensor_scalar(bias[:, 32:40], bias[:, 0:8], 0.03125, None, ALU.mult))
    P.cut()
    Q = P.sbuf("l3q", 16384, BF16)
    Kb = P.sbuf("l3k", 16384, BF16)
    V0 = P.sbuf("l3v0", 16 * 520, BF16)
    V1 = P.sbuf("l3v1", 16 * 520, BF16)

    def evqk(oc, ps):
        if oc < 8:
            P.v(lambda v, oc=oc: v.tensor_scalar(Q[:, oc * T:(oc + 1) * T], ps[:, :],
                                                 0.03125, bias[:, 32 + oc:33 + oc],
                                                 ALU.mult, ALU.add))
        else:
            c = oc - 8
            P.v(lambda v, c=c: v.tensor_scalar(Kb[:, c * T:(c + 1) * T], ps[:, :],
                                               bias[:, 8 + c:9 + c], None, ALU.add))
    mm_stream(P, "l3a_wqk", packed["L3a_wqk"], XH, 0, T, T, evqk, PW=256)
    P.v(lambda v: v.memset(V0[:, :], 1.0))
    P.v(lambda v: v.memset(V1[:, :], 1.0))
    P.cut()
    wv = WFull(P, "l3awv", packed["L3a_wv"])
    P.wait("tensor", wv.sem, wv.ready)
    for tt in range(17):
        if tt < 16:
            ps = psA if tt % 2 == 0 else psB
            for ic in range(NC):
                for j in range(2):
                    P.t(lambda t, tt=tt, ic=ic, j=j, ps=ps: t.matmul(
                        ps[:, j * 512:(j + 1) * 512],
                        XH[:, ic * T + tt * 128: ic * T + tt * 128 + 128],
                        wv.sb[:, ic * 1024 + j * 512: ic * 1024 + (j + 1) * 512],
                        start=(ic == 0), stop=(ic == NC - 1)))
        if tt > 0:
            psp = psA if (tt - 1) % 2 == 0 else psB
            d0 = V0[:, (tt - 1) * 520:tt * 520].rearrange("p (h d) -> p h d", d=65)[:, :, 0:64]
            d1 = V1[:, (tt - 1) * 520:tt * 520].rearrange("p (h d) -> p h d", d=65)[:, :, 0:64]
            P.v(lambda v, psp=psp, d0=d0: v.tensor_copy(d0, psp[:, 0:512]))
            P.v(lambda v, psp=psp, d1=d1: v.tensor_copy(d1, psp[:, 512:1024]))
        P.cut()
    wv.free()
    sqs = P.sbuf("l3sqs", 2048, BF16)
    s_scr = P.sbuf("l3ss", 2048, F32)
    Pb = P.sbuf("l3pb", 2048, BF16)
    cells = P.sbuf("l3cells", 8, F32)
    def v_fn(kc, h):
        if h < 8:
            return V0[:, kc * 520 + h * 65: kc * 520 + h * 65 + 65]
        return V1[:, kc * 520 + (h - 8) * 65: kc * 520 + (h - 8) * 65 + 65]
    attn_core_64(P, Q, Kb, v_fn, Q, lambda hr, hc: bias[hr:hr + 64, 16 + hc:17 + hc],
                 sqs, s_scr, Pb, cells)
    P.sfree("l3k")
    P.sfree("l3v0")
    P.sfree("l3v1")
    P.sfree("l3sqs")
    P.sfree("l3pb")
    P.sfree("l3cells")
    unpark_x(P)
    X = e["X"]
    wo = WFull(P, "l3awo", packed["L3a_wo"])

    def evo(oc, ps):
        P.v(lambda v, oc=oc: v.scalar_tensor_tensor(
            s_scr[:, 0:T] if False else e["nscr"][:, :], ps[:, :], bias[:, 24 + oc:25 + oc],
            XH[:, oc * T:(oc + 1) * T], ALU.add, ALU.add))
        P.v(lambda v, oc=oc: v.tensor_tensor(X[:, oc * T:(oc + 1) * T], e["nscr"][:, :],
                                             X[:, oc * T:(oc + 1) * T], ALU.add))
    mm_fm(P, wo, Q, 0, NC, NC, evo, wait_sem=wo.sem, wait_val=wo.ready)
    wo.free()
    P.sfree("l3ss")
    P.sfree("l3q")


def emit_l0_attn(P, packed):
    e = P.env
    psA, psB, rows = e["psA"], e["psB"], e["rows"]
    emit_ln_v2(P)
    park_x(P)
    XH = e["XH"]
    if "selw" in P._sb:
        P.sfree("selw")
        P.env.pop("selw", None)
    lamr = P.sbuf("l0lam", 256, F32, top=True)
    alr = P.sbuf("l0al", 2, F32, top=True)
    cl = P.sbuf("l0cl", 16, F32, top=True)
    c1col = P.sbuf("l0c1c", 2, F32, top=True)
    blk = P.sbuf("l0blk", 4, BF16, top=True)
    P.dma("sync", lamr[0:1, :], P.din("L0a_lam", packed["L0a_lam"])[:, :], "l0ac")
    P.dma("sync", alr[0:1, 0:1], P.din("L0a_alpha", packed["L0a_alpha"])[:, :], "l0ac")
    P.dma("sync", blk[:, :], P.din("c_blk32_b16", packed["c_blk32_b16"])[:, :], "l0ac")
    P.all_wait("l0ac", P.semv["l0ac"])
    P.v(lambda v: v.tensor_tensor(rows[0:1, 0:64], lamr[0:1, 0:64], lamr[0:1, 128:192], ALU.mult))
    P.v(lambda v: v.tensor_tensor(rows[0:1, 64:128], lamr[0:1, 64:128], lamr[0:1, 192:256], ALU.mult))
    P.cut()
    P.v(lambda v: v.tensor_reduce(cl[0:1, 0:1], rows[0:1, 0:64], AX.X, ALU.add))
    P.v(lambda v: v.tensor_reduce(cl[0:1, 1:2], rows[0:1, 64:128], AX.X, ALU.add))
    P.cut()
    P.s(lambda s: s.activation(cl[0:1, 4:5], cl[0:1, 0:1], AF.Exp))
    P.s(lambda s: s.activation(cl[0:1, 5:6], cl[0:1, 1:2], AF.Exp))
    P.cut()
    P.v(lambda v: v.tensor_tensor(cl[0:1, 2:3], cl[0:1, 4:5], cl[0:1, 5:6], ALU.subtract))
    P.cut()
    P.v(lambda v: v.tensor_scalar(cl[0:1, 2:3], cl[0:1, 2:3], 0.2, None, ALU.add))
    P.v(lambda v: v.tensor_scalar(cl[0:1, 3:4], alr[0:1, 0:1], 2.0 / math.pi,
                                  1.0 / math.pi, ALU.mult, ALU.add))
    P.cut()
    P.t(lambda t: t.matmul(psA[:, 0:1], e["ones_r32"][:, :], cl[0:1, 3:4], start=True, stop=True))
    P.cut()
    P.v(lambda v: v.tensor_copy(c1col[:, 0:1], psA[:, 0:1]))
    P.cut()

    if "vpark" not in e:
        e["vpark"] = P.nc.dram_tensor("vpark", [128, 32768], BF16).ap()
    V = P.sbuf("l0v", 32768, BF16)
    wv = WFull(P, "l0awv", packed["L0a_wv"])
    P.wait("tensor", wv.sem, wv.ready)
    for tt in range(17):
        if tt < 16:
            ps = psA if tt % 2 == 0 else psB
            for ic in range(NC):
                for jj in range(4):
                    P.t(lambda t, tt=tt, ic=ic, jj=jj, ps=ps: t.matmul(
                        ps[:, jj * 512:(jj + 1) * 512],
                        XH[:, ic * T + tt * 128: ic * T + tt * 128 + 128],
                        wv.sb[:, ic * 2048 + jj * 512: ic * 2048 + (jj + 1) * 512],
                        start=(ic == 0), stop=(ic == NC - 1)))
        if tt > 0:
            psp = psA if (tt - 1) % 2 == 0 else psB
            P.v(lambda v, tt=tt, psp=psp: v.tensor_copy(V[:, (tt - 1) * 2048:tt * 2048], psp[:, :]))
        P.cut()
    wv.free()
    for c in range(16):
        P.dma("sync", e["vpark"][:, c * 2048:(c + 1) * 2048], V[:, c * 2048:(c + 1) * 2048], "vpk")
    P.all_wait("vpk", P.semv["vpk"])
    P.sfree("l0v")

    Q = P.sbuf("l0q", 16384, BF16)
    wq = WFull(P, "l0awq", packed["L0a_wq"])

    def evq(oc, ps):
        P.v(lambda v, oc=oc: v.tensor_copy(Q[:, oc * T:(oc + 1) * T], ps[:, :]))
    mm_fm(P, wq, XH, 0, NC, NC, evq, wait_sem=wq.sem, wait_val=wq.ready)
    wq.free()
    Kb = P.sbuf("l0k", 16384, BF16)
    wk = WFull(P, "l0awk", packed["L0a_wk"])

    def evk(oc, ps):
        P.v(lambda v, oc=oc: v.tensor_copy(Kb[:, oc * T:(oc + 1) * T], ps[:, :]))
    mm_fm(P, wk, XH, 0, NC, NC, evk, wait_sem=wk.sem, wait_val=wk.ready)
    wk.free()
    free_xh(P)

    rope = P.sbuf("l0rope", 8192, BF16)
    P.dma("sync", rope[:, :], P.din("c_rope", packed["c_rope"])[:, :], "l0rp")
    P.wait("vector", "l0rp", P.semv["l0rp"])
    P.cut()
    zsw = P.sbuf("l0zsw", 2048, BF16)
    rs = P.sbuf("l0rs", 2048, F32)
    SHUF = list(range(16, 32)) + list(range(16))
    for buf, co, so in ((Q, 0, 2048), (Kb, 4096, 6144)):
        for c in range(NC):
            P.v(lambda v, buf=buf, c=c: v.stream_shuffle(zsw[:, :], buf[:, c * T:(c + 1) * T], SHUF))
            P.v(lambda v, buf=buf, c=c, co=co: v.tensor_tensor(
                e["nscr"][:, :], buf[:, c * T:(c + 1) * T], rope[:, co:co + 2048], ALU.mult))
            P.v(lambda v, so=so: v.tensor_tensor(rs[:, :], zsw[:, :], rope[:, so:so + 2048], ALU.mult))
            P.v(lambda v, buf=buf, c=c: v.tensor_tensor(
                buf[:, c * T:(c + 1) * T], e["nscr"][:, :], rs[:, :], ALU.add))
    P.cut()
    P.sfree("l0rope")
    P.sfree("l0zsw")
    P.sfree("l0rs")

    V = P.sbuf("l0v", 32768, BF16)
    P.cut()
    for c in range(16):
        P.dma("sync", V[:, c * 2048:(c + 1) * 2048], e["vpark"][:, c * 2048:(c + 1) * 2048], "vup")
    P.all_wait("vup", P.semv["vup"])

    Yh = P.sbuf("l0yh", 2048, F32)
    rs = P.sbuf("l0rs2", 512, F32)
    Pb = P.sbuf("l0pb", 2048, BF16)
    s_scr = P.sbuf("l0ss", 2048, F32)
    sqs = P.sbuf("l0sq", 2048, BF16)
    ib = P.sbuf("l0ib", 1024, F32)
    gcol = P.sbuf("l0gn", 4, F32)
    negb2 = e["bc1"]
    for h in range(16):
        hc4, hr = h // 4, (h % 4) * 32
        for m in (0, 1):
            qch = 4 * m + hc4
            P.s(lambda s, qch=qch: s.activation(sqs[:, :], Q[:, qch * T:(qch + 1) * T], AF.Square))
            P.cut()
            for jj in range(NT):
                P.t(lambda t, jj=jj: t.matmul(psA[0:1, jj * 512:(jj + 1) * 512],
                                              blk[:, h % 4:h % 4 + 1],
                                              sqs[:, jj * 512:(jj + 1) * 512], start=True, stop=True))
            P.cut()
            P.s(lambda s, qch=qch: s.activation(sqs[:, :], Kb[:, qch * T:(qch + 1) * T], AF.Square))
            P.s(lambda s: s.activation(rows[0:1, 0:T], psA[0:1, 0:T], AF.Sqrt))
            P.cut()
            for jj in range(NT):
                P.t(lambda t, jj=jj: t.matmul(psB[0:1, jj * 512:(jj + 1) * 512],
                                              blk[:, h % 4:h % 4 + 1],
                                              sqs[:, jj * 512:(jj + 1) * 512], start=True, stop=True))
            P.cut()
            P.s(lambda s: s.activation(rows[0:1, T:2 * T], psB[0:1, 0:T], AF.Copy))
            P.cut()
            P.v(lambda v: v.tensor_reduce(cl[0:1, 8:9], rows[0:1, T:2 * T], AX.X, ALU.max))
            P.cut()
            P.s(lambda s: s.activation(cl[0:1, 9:10], cl[0:1, 8:9], AF.Sqrt))
            P.cut()
            P.v(lambda v: v.tensor_scalar(cl[0:1, 10:11], cl[0:1, 9:10], -1.0, None, ALU.mult))
            P.cut()
            P.v(lambda v: v.tensor_scalar(rows[0:1, T:2 * T], rows[0:1, 0:T],
                                          cl[0:1, 10:11], None, ALU.mult))
            P.cut()
            for jj in range(NT):
                P.t(lambda t, jj=jj: t.matmul(psA[:, jj * 512:(jj + 1) * 512], e["ones_r32"][:, :],
                                              rows[0:1, T + jj * 512:T + (jj + 1) * 512],
                                              start=True, stop=True))
            P.cut()
            dstbc = e["bc0"] if m == 0 else e["bc1"]
            P.v(lambda v, dstbc=dstbc: v.tensor_copy(dstbc[:, :], psA[:, :]))
            P.cut()
        for qq in range(4):
            q0 = qq * 512
            fs0, ss0, es0, pv0 = P.semv["afs"], P.semv["ass"], P.semv["aes"], P.semv["apv"]
            afs, aes, apv = P.sem("afs"), P.sem("aes"), P.sem("apv")
            ass = P.sem("ass")

            def emit_pv(kc):
                P.wait("tensor", "aes", es0 + kc + 1)
                for m in (0, 1):
                    sp = psA if m == 0 else psB
                    P.t(lambda t, kc=kc, m=m, h=h, sp=sp: t.matmul(
                        sp[:, 0:512],
                        V[:, kc * 2048 + h * 128: kc * 2048 + h * 128 + 128],
                        Pb[:, (m * 2 + (kc % 2)) * 512:(m * 2 + (kc % 2)) * 512 + 512],
                        start=(kc == 0), stop=(kc == 15)))

                    def flmm(t, kc=kc, m=m, sp=sp, last=(m == 1)):
                        ins = t.matmul(
                            sp[0:1, 512:1024], e["ones_cb"][:, :],
                            Pb[:, (m * 2 + (kc % 2)) * 512:(m * 2 + (kc % 2)) * 512 + 512],
                            start=(kc == 0), stop=(kc == 15))
                        if last:
                            ins.then_inc(apv, 1)
                    P.t(flmm)
                P.bump("apv", 1)

            for j in range(16):
                if j >= 2:
                    P.wait("tensor", "ass", ss0 + j - 1)
                for m in (0, 1):
                    qch = 4 * m + hc4
                    sp = psA if m == 0 else psB

                    def ffill(t, j=j, m=m, qch=qch, sp=sp, last=(m == 1)):
                        ins = t.matmul(
                            sp[:, 1024 + (j % 2) * 512:1024 + (j % 2) * 512 + 512],
                            Kb[hr:hr + 32, qch * T + j * 128: qch * T + j * 128 + 128],
                            Q[hr:hr + 32, qch * T + q0: qch * T + q0 + 512],
                            start=True, stop=True, tile_position=(hr, 0))
                        if last:
                            ins.then_inc(afs, 1)
                    P.t(ffill)
                P.bump("afs", 1)
                if j >= 2:
                    emit_pv(j - 2)
                P.wait("vector", "afs", fs0 + j + 1)
                if j >= 2:
                    P.wait("vector", "aes", es0 + j - 1)
                for m in (0, 1):
                    sp = psA if m == 0 else psB
                    bcm = e["bc0"] if m == 0 else e["bc1"]

                    def fsub(v, j=j, m=m, sp=sp, bcm=bcm, last=(m == 1)):
                        ins = v.tensor_tensor(
                            s_scr[:, (m * 2 + (j % 2)) * 512:(m * 2 + (j % 2)) * 512 + 512],
                            sp[:, 1024 + (j % 2) * 512:1024 + (j % 2) * 512 + 512],
                            bcm[:, q0:q0 + 512], ALU.add)
                        if last:
                            ins.then_inc(ass, 1)
                    P.v(fsub)
                P.bump("ass", 1)
                P.wait("scalar", "ass", ss0 + j + 1)
                if j >= 2:
                    P.wait("scalar", "apv", pv0 + j - 1)
                for m in (0, 1):
                    def fexp(sc, j=j, m=m, last=(m == 1)):
                        ins = sc.activation(
                            Pb[:, (m * 2 + (j % 2)) * 512:(m * 2 + (j % 2)) * 512 + 512],
                            s_scr[:, (m * 2 + (j % 2)) * 512:(m * 2 + (j % 2)) * 512 + 512],
                            AF.Exp)
                        if last:
                            ins.then_inc(aes, 1)
                    P.s(fexp)
                P.bump("aes", 1)
            emit_pv(14)
            emit_pv(15)
            P.cut()
            P.s(lambda s: s.activation(rows[0:1, 0:512], psA[0:1, 512:1024], AF.Copy))
            P.s(lambda s: s.activation(rows[0:1, 512:1024], psB[0:1, 512:1024], AF.Copy))
            P.cut()
            P.v(lambda v: v.reciprocal(rows[0:1, 0:512], rows[0:1, 0:512]))
            P.v(lambda v: v.reciprocal(rows[0:1, 512:1024], rows[0:1, 512:1024]))
            P.cut()
            P.v(lambda v: v.tensor_scalar(rows[0:1, 512:1024], rows[0:1, 512:1024],
                                          cl[0:1, 2:3], None, ALU.mult))
            P.cut()
            P.t(lambda t: t.matmul(psA[:, 1024:1536], e["ones_r32"][:, :],
                                   rows[0:1, 0:512], start=True, stop=True))
            P.t(lambda t: t.matmul(psB[:, 1024:1536], e["ones_r32"][:, :],
                                   rows[0:1, 512:1024], start=True, stop=True))
            P.cut()
            P.v(lambda v: v.tensor_copy(ib[:, 0:512], psA[:, 1024:1536]))
            P.v(lambda v: v.tensor_copy(ib[:, 512:1024], psB[:, 1024:1536]))
            P.cut()
            P.v(lambda v: v.tensor_tensor(e["nscr"][:, 0:512], psA[:, 0:512], ib[:, 0:512], ALU.mult))
            P.v(lambda v: v.tensor_tensor(rs[:, 0:512], psB[:, 0:512], ib[:, 512:1024], ALU.mult))
            P.v(lambda v, q0=q0: v.tensor_tensor(Yh[:, q0:q0 + 512], e["nscr"][:, 0:512],
                                                 rs[:, 0:512], ALU.subtract))
            P.cut()
        P.v(lambda v: v.tensor_copy(sqs[:, :], Yh[:, :]))
        P.cut()
        for jj in range(NT):
            P.t(lambda t, jj=jj: t.matmul(psA[0:1, jj * 512:(jj + 1) * 512], e["ones_cb"][:, :],
                                          sqs[:, jj * 512:(jj + 1) * 512], start=True, stop=True))
        P.cut()
        P.s(lambda s: s.activation(sqs[:, :], sqs[:, :], AF.Square))
        P.cut()
        for jj in range(NT):
            P.t(lambda t, jj=jj: t.matmul(psB[0:1, jj * 512:(jj + 1) * 512], e["ones_cb"][:, :],
                                          sqs[:, jj * 512:(jj + 1) * 512], start=True, stop=True))
        P.cut()
        P.s(lambda s: s.activation(rows[0:1, 0:T], psA[0:1, 0:T], AF.Copy))
        P.s(lambda s: s.activation(rows[0:1, T:2 * T], psB[0:1, 0:T], AF.Copy))
        P.cut()
        P.v(lambda v: v.tensor_reduce(cl[0:1, 8:9], rows[0:1, 0:T], AX.X, ALU.add))
        P.v(lambda v: v.tensor_reduce(cl[0:1, 9:10], rows[0:1, T:2 * T], AX.X, ALU.add))
        P.cut()
        NE = 128.0 * 2048.0
        P.v(lambda v: v.tensor_scalar(cl[0:1, 10:11], cl[0:1, 8:9], 1.0 / NE, None, ALU.mult))
        P.cut()
        P.v(lambda v: v.scalar_tensor_tensor(cl[0:1, 11:12], cl[0:1, 10:11], -1.0,
                                             cl[0:1, 10:11], ALU.mult, ALU.mult))
        P.cut()
        P.v(lambda v: v.scalar_tensor_tensor(cl[0:1, 11:12], cl[0:1, 9:10], 1.0 / NE,
                                             cl[0:1, 11:12], ALU.mult, ALU.add))
        P.cut()
        P.constf(1e-5)
        P.s(lambda s: s.activation(cl[0:1, 11:12], cl[0:1, 11:12], AF.Sqrt, bias=1e-5))
        P.cut()
        P.v(lambda v: v.reciprocal(cl[0:1, 11:12], cl[0:1, 11:12]))
        P.cut()
        P.v(lambda v: v.tensor_scalar(cl[0:1, 11:12], cl[0:1, 11:12], 0.8, None, ALU.mult))
        P.cut()
        P.v(lambda v: v.scalar_tensor_tensor(cl[0:1, 12:13], cl[0:1, 10:11], -1.0,
                                             cl[0:1, 11:12], ALU.mult, ALU.mult))
        P.cut()
        P.t(lambda t: t.matmul(psA[:, 1024:1026], e["ones_r32"][:, :], cl[0:1, 11:13],
                               start=True, stop=True))
        P.cut()
        P.v(lambda v: v.tensor_copy(gcol[:, 0:2], psA[:, 1024:1026]))
        P.cut()
        Yre = Yh[:, 0:2048].rearrange("p (c a) -> p c a", a=16)
        for g in range(4):
            for aa in range(4):
                a = g * 4 + aa
                P.t(lambda t, a=a, aa=aa, Yre=Yre: t.transpose(
                    psB[:, 1024 + aa * 128:1024 + (aa + 1) * 128],
                    Yre[:, :, a:a + 1], e["ident"][:, :]))
            P.cut()
            for aa in range(4):
                a = g * 4 + aa
                P.v(lambda v, a=a, aa=aa, h=h: v.tensor_scalar(
                    V[:, a * 2048 + h * 128: a * 2048 + h * 128 + 128],
                    psB[:, 1024 + aa * 128:1024 + (aa + 1) * 128],
                    gcol[:, 0:1], gcol[:, 1:2], ALU.mult, ALU.add))
            P.cut()
    P.sfree("l0q")
    P.sfree("l0k")
    P.sfree("l0yh")
    P.sfree("l0pb")
    P.sfree("l0ss")
    P.sfree("l0sq")
    P.sfree("l0ib")
    P.sfree("l0rs2")

    unpark_x(P)
    X = e["X"]
    q1 = P.sbuf("l0q1", 2048, F32)
    q2 = P.sbuf("l0q2", 2048, F32)
    sg = P.sbuf("l0sg", 2048, BF16)
    gtmp = P.sbuf("l0gt", 2048, BF16)
    Vre = V[:, 0:32768].rearrange("p (a dv) -> p a dv", dv=2048)

    def rhs_fn(ic, j):
        return Vre[:, 4 * j:4 * j + 4, ic * 128:(ic + 1) * 128]

    def evp(oc, ps):
        if oc % 2 == 0:
            P.s(lambda s: s.activation(sg[:, :], ps[:, :], AF.Sign))
            P.s(lambda s: s.activation(q1[:, :], ps[:, :], AF.Abs))
            P.cut()
            P.v(lambda v: v.tensor_scalar(q2[:, :], q1[:, :], 1.0, None, ALU.max))
            P.v(lambda v: v.reciprocal(q2[:, :], q2[:, :]))
            P.v(lambda v: v.tensor_tensor(q2[:, :], q1[:, :], q2[:, :], ALU.min))
            P.cut()
            P.s(lambda s: s.activation(q2[:, :], q2[:, :], AF.Arctan))
            P.cut()
            P.v(lambda v: v.tensor_scalar(e["nscr"][:, :], q2[:, :], -2.0,
                                          math.pi / 2.0, ALU.mult, ALU.add))
            P.v(lambda v: v.tensor_scalar(q1[:, :], q1[:, :], 1.0, None, ALU.is_gt))
            P.v(lambda v: v.tensor_tensor(e["nscr"][:, :], e["nscr"][:, :], q1[:, :], ALU.mult))
            P.v(lambda v: v.tensor_tensor(q2[:, :], q2[:, :], e["nscr"][:, :], ALU.add))
            P.v(lambda v: v.tensor_tensor(q2[:, :], q2[:, :], sg[:, :], ALU.mult))
            P.v(lambda v: v.tensor_scalar(gtmp[:, :], q2[:, :], c1col[:, 0:1], 0.5,
                                          ALU.mult, ALU.add))
        else:
            c = oc // 2
            P.v(lambda v, c=c: v.tensor_tensor(e["nscr"][:, :], ps[:, :], gtmp[:, :], ALU.mult))
            P.v(lambda v, c=c: v.tensor_tensor(X[:, c * T:(c + 1) * T], e["nscr"][:, :],
                                               X[:, c * T:(c + 1) * T], ALU.add))
    mm_stream(P, "l0a_wproj", packed["L0a_wproj"], rhs_fn, 0, 0, T, evp, PW=128)
    P.sfree("l0v")
    P.sfree("l0q1")
    P.sfree("l0q2")
    P.sfree("l0sg")
    P.sfree("l0gt")


def ensure_sel(P, packed):
    if "selw" not in P._sb:
        sel = P.sbuf("selw", 2048, F32, top=True)
        P.env["selw"] = sel
        P.dma("sync", sel[0:16, :], P.din("c_sel16_f32", packed["c_sel16_f32"])[:, :], "selc")
        P.all_wait("selc", P.semv["selc"])
    return P.env["selw"]


def emit_l0_ffn(P, packed):
    e = P.env
    psA, psB, rows = e["psA"], e["psB"], e["rows"]
    emit_ln_v2(P)
    park_x(P)
    XH = e["XH"]
    NH = 22
    if "l0rw" not in P._sb:
        rw = P.sbuf("l0rw", 64, BF16, top=True)
        rbr = P.sbuf("l0rb", 8, F32, top=True)
        e["l0rw"], e["l0rb"] = rw, rbr
        P.dma("gpsimd", rw[:, 0:56], P.din("L0f_wr", packed["L0f_wr"]).rearrange("(c p) e -> p c e", p=128), "l0c")
        P.dma("sync", rbr[0:1, 0:7], P.din("L0f_rb", packed["L0f_rb"])[:, :], "l0c")
        P.all_wait("l0c", P.semv["l0c"])
    rw, rbr = e["l0rw"], e["l0rb"]
    sel = ensure_sel(P, packed)

    logit = P.sbuf("l0lg", 128, F32)
    P.t(lambda t: t.matmul(psB[:, 32:39], e["ones_r32"][:, :], rbr[0:1, 0:7], start=True, stop=True))
    P.cut()
    P.v(lambda v: v.tensor_copy(e["bc1"][:, 0:7], psB[:, 32:39]))
    P.cut()
    for tt in range(17):
        if tt < 16:
            ps = psA if tt % 2 == 0 else psB
            for ic in range(NC):
                P.t(lambda t, tt=tt, ic=ic, ps=ps: t.matmul(
                    ps[:, 0:7],
                    XH[:, ic * T + tt * 128: ic * T + tt * 128 + 128],
                    rw[:, ic * 7:(ic + 1) * 7],
                    start=(ic == 0), stop=(ic == NC - 1)))
        if tt > 0:
            psp = psA if (tt - 1) % 2 == 0 else psB
            P.v(lambda v, tt=tt, psp=psp: v.tensor_tensor(
                logit[:, (tt - 1) * 7:tt * 7], psp[:, 0:7], e["bc1"][:, 0:7], ALU.add))
        P.cut()

    pbuf = P.sbuf("l0p", 128, F32)
    mcol = P.sbuf("l0m", 16, F32)
    dsum = P.sbuf("l0d", 16, F32)
    keep = P.sbuf("l0keep", 128, F32)
    for tt in range(16):
        P.v(lambda v, tt=tt: v.tensor_reduce(mcol[:, tt:tt + 1], logit[:, tt * 7:(tt + 1) * 7],
                                             AX.X, ALU.max, negate=True))
    P.cut()
    for tt in range(16):
        P.s(lambda s, tt=tt: s.activation(pbuf[:, tt * 7:(tt + 1) * 7],
                                          logit[:, tt * 7:(tt + 1) * 7],
                                          AF.Exp, bias=mcol[:, tt:tt + 1]))
    P.cut()
    P.v(lambda v: v.tensor_copy(logit[:, :], pbuf[:, :]))
    P.cut()
    for step in range(2):
        for tt in range(16):
            P.v(lambda v, tt=tt: v.tensor_reduce(mcol[:, tt:tt + 1], logit[:, tt * 7:(tt + 1) * 7],
                                                 AX.X, ALU.max))
        P.cut()
        if step == 0:
            P.v(lambda v: v.tensor_copy(dsum[:, :], mcol[:, :]))
            for tt in range(16):
                P.v(lambda v, tt=tt: v.tensor_scalar(keep[:, tt * 7:(tt + 1) * 7],
                                                     logit[:, tt * 7:(tt + 1) * 7],
                                                     mcol[:, tt:tt + 1], None, ALU.is_lt))
            for tt in range(16):
                P.v(lambda v, tt=tt: v.tensor_tensor(logit[:, tt * 7:(tt + 1) * 7],
                                                     logit[:, tt * 7:(tt + 1) * 7],
                                                     keep[:, tt * 7:(tt + 1) * 7], ALU.mult))
        else:
            P.v(lambda v: v.tensor_tensor(dsum[:, :], dsum[:, :], mcol[:, :], ALU.add))
        P.cut()
    for tt in range(16):
        P.v(lambda v, tt=tt: v.tensor_scalar(keep[:, tt * 7:(tt + 1) * 7],
                                             pbuf[:, tt * 7:(tt + 1) * 7],
                                             mcol[:, tt:tt + 1], None, ALU.is_ge))
    P.v(lambda v: v.reciprocal(dsum[:, :], dsum[:, :]))
    P.cut()
    for tt in range(16):
        P.v(lambda v, tt=tt: v.tensor_scalar(pbuf[:, tt * 7:(tt + 1) * 7],
                                             pbuf[:, tt * 7:(tt + 1) * 7],
                                             dsum[:, tt:tt + 1], None, ALU.mult))
    P.cut()
    for tt in range(16):
        P.v(lambda v, tt=tt: v.tensor_tensor(pbuf[:, tt * 7:(tt + 1) * 7],
                                             pbuf[:, tt * 7:(tt + 1) * 7],
                                             keep[:, tt * 7:(tt + 1) * 7], ALU.mult))
    P.cut()
    wrow = P.sbuf("l0wr", 2048, F32)
    for tt in range(16):
        P.t(lambda t, tt=tt: t.transpose(psA[0:7, tt * 128:(tt + 1) * 128],
                                         pbuf[:, tt * 7:(tt + 1) * 7], e["ident"][:, :]))
    P.cut()
    P.v(lambda v: v.tensor_copy(wrow[0:7, :], psA[0:7, 0:2048]))
    P.cut()
    P.sfree("l0lg")
    P.sfree("l0p")
    P.sfree("l0m")
    P.sfree("l0d")
    P.sfree("l0keep")

    He = P.sbuf("l0he", NH * 1024, BF16)
    ACC = P.sbuf("l0acc", 8192, F32)
    TH = 1024
    for half in range(2):
        q0 = half * TH
        for ei in range(8):
            routed = ei > 0
            ex = ei - 1
            if routed:
                P.t(lambda t, ex=ex: t.matmul(psB[:, 1024:1536], sel[0:7, ex * 128:(ex + 1) * 128],
                                              wrow[0:7, q0:q0 + 512], start=True, stop=True))
                P.t(lambda t, ex=ex: t.matmul(psB[:, 1536:2048], sel[0:7, ex * 128:(ex + 1) * 128],
                                              wrow[0:7, q0 + 512:q0 + TH], start=True, stop=True))
                P.cut()
                P.v(lambda v: v.tensor_copy(e["bc0"][:, 0:TH], psB[:, 1024:2048]))
                P.cut()
            gname = "L0f_sg" if not routed else f"L0f_rg_{ex}"
            uname = "L0f_su" if not routed else f"L0f_ru_{ex}"
            dname = "L0f_sd" if not routed else f"L0f_rd_{ex}"

            def evg(oc, ps):
                P.s(lambda s, oc=oc, ps=ps: s.activation(
                    He[:, oc * TH:(oc + 1) * TH], ps[:, 0:TH], AF.Silu))
            mm_stream(P, gname, packed[gname], XH, q0, T, TH, evg, PW=256)

            def evu(oc, ps):
                P.v(lambda v, oc=oc, ps=ps: v.tensor_tensor(
                    He[:, oc * TH:(oc + 1) * TH], ps[:, 0:TH],
                    He[:, oc * TH:(oc + 1) * TH], ALU.mult))
            mm_stream(P, uname, packed[uname], XH, q0, T, TH, evu, PW=256)

            def evd(oc, ps, routed=routed, ei=ei):
                if not routed:
                    P.v(lambda v, oc=oc, ps=ps: v.tensor_copy(
                        ACC[:, oc * TH:(oc + 1) * TH], ps[:, 0:TH]))
                else:
                    P.v(lambda v, oc=oc, ps=ps: v.tensor_tensor(
                        e["nscr"][:, 0:TH], ps[:, 0:TH], e["bc0"][:, 0:TH], ALU.mult))
                    P.v(lambda v, oc=oc: v.tensor_tensor(
                        ACC[:, oc * TH:(oc + 1) * TH], ACC[:, oc * TH:(oc + 1) * TH],
                        e["nscr"][:, 0:TH], ALU.add))
            mm_stream(P, dname, packed[dname], He, 0, TH, TH, evd, PW=256)
        for c in range(NC):
            sl = e["nscr"][:, (c % 2) * 1024:(c % 2) * 1024 + 1024]
            v1 = P.dma("sync", sl, e["xpark"][:, c * T + q0: c * T + q0 + TH], "xq")
            P.wait("vector", "xq", v1)
            P.v(lambda v, c=c, sl=sl: v.scalar_tensor_tensor(
                sl, ACC[:, c * TH:(c + 1) * TH], 0.5, sl, ALU.mult, ALU.add))
            P.cut()
            P.dma("sync", e["xpark"][:, c * T + q0: c * T + q0 + TH], sl, "xqs")
        P.wait("sync", "xqs", P.semv["xqs"])
        P.cut()
    P.sfree("l0he")
    P.sfree("l0acc")
    P.sfree("l0wr")
    unpark_x(P)


def emit_l2_quant(P, packed):
    e = P.env
    wf = P.sbuf("l2wf", NC * 1024, F32)
    wd = P.din("L2w_ffw", packed["L2w_ffw"])
    for ic in range(NC):
        P.dma("sync", wf[:, ic * 1024:(ic + 1) * 1024], wd[ic * 128:(ic + 1) * 128, :], "l2w")
    e["l2bias"] = P.sbuf("l2bias", NC, F32, top=True)
    P.dma("sync", e["l2bias"][:, :], P.din("L2w_ffb", packed["L2w_ffb"])[:, :], "l2w")
    arow = P.sbuf("l2arow", 1, F32, parts=1)
    P.dma("sync", arow[:, :], P.din("L2w_alpha", packed["L2w_alpha"])[:, :], "l2w")
    P.wait("vector", "l2w", P.semv["l2w"])
    P.wait("scalar", "l2w", P.semv["l2w"])
    P.wait("tensor", "l2w", P.semv["l2w"])
    P.cut()
    if P.env.get("_dbg"):
        P.dbg_dump("wf", wf[:, 0:8192], 8192)
    rows = e["rows"]
    acc = P.sbuf("l2acc", 8, F32)
    for ic in range(NC):
        P.v(lambda v, ic=ic: v.tensor_reduce(acc[:, ic:ic + 1], wf[:, ic * 1024:(ic + 1) * 1024],
                                             AX.X, ALU.add, apply_absolute_value=True))
    P.cut()
    if P.env.get("_dbg"):
        P.dbg_dump("acc", acc[:, 0:8], 8)
    P.t(lambda t: t.matmul(e["psA"][0:1, 0:8], e["ones_c32"][:, :], acc[:, 0:8], start=True, stop=True))
    P.cut()
    drow = rows[0:1, 0:2]
    P.v(lambda v: v.tensor_reduce(rows[0:1, 4:5], e["psA"][0:1, 0:8], AX.X, ALU.add))
    P.cut()
    P.s(lambda s: s.activation(drow[:, 0:1], rows[0:1, 4:5], AF.Copy, scale=0.7 / (1024.0 * 1024.0)))
    P.cut()
    P.v(lambda v: v.tensor_scalar(drow[:, 1:2], drow[:, 0:1], -1.0, None, ALU.mult))
    P.cut()
    P.t(lambda t: t.matmul(e["psB"][:, 0:2], e["ones_r32"][:, :], drow[:, 0:2], start=True, stop=True))
    P.t(lambda t: t.matmul(e["psB"][:, 2:3], e["ones_r32"][:, :], arow[:, 0:1], start=True, stop=True))
    P.cut()
    dcols = P.sbuf("l2dcols", 4, F32, top=True)
    e["l2alpha"] = dcols
    P.v(lambda v: v.tensor_copy(dcols[:, 0:3], e["psB"][:, 0:3]))
    P.cut()
    tern = P.sbuf("l2tern", NC * 1024, BF16, top=True)
    e["l2tern"] = tern
    t0 = P.sbuf("l2t0", 1024, BF16)
    t1 = P.sbuf("l2t1", 1024, BF16)
    for ic in range(NC):
        P.v(lambda v, ic=ic: v.tensor_scalar(t0[:, :], wf[:, ic * 1024:(ic + 1) * 1024],
                                             dcols[:, 0:1], None, ALU.is_gt))
        P.v(lambda v, ic=ic: v.tensor_scalar(t1[:, :], wf[:, ic * 1024:(ic + 1) * 1024],
                                             dcols[:, 1:2], None, ALU.is_lt))
        P.cut()
        P.v(lambda v, ic=ic: v.tensor_tensor(tern[:, ic * 1024:(ic + 1) * 1024], t0[:, :], t1[:, :],
                                             ALU.subtract))
        P.cut()
    P.sfree("l2t0")
    P.sfree("l2t1")
    P.sfree("l2acc")
    P.sfree("l2wf")


def emit_l2_ffn(P, packed, first):
    e = P.env
    emit_ln_v2(P)
    if P.env.get("_dbg"):
        P.dbg_dump("XH", e["XH"][:, 0:16384], 16384, dtype=BF16)
    if first:
        emit_l2_quant(P, packed)
    if P.env.get("_dbg"):
        P.dbg_dump("tern", e["l2tern"][:, 0:8192], 8192, dtype=BF16)
        P.dbg_dump("dcols", e["l2alpha"][:, 0:4], 4)
    tern, dcols, bias = e["l2tern"], e["l2alpha"], e["l2bias"]
    X, XH = e["X"], e["XH"]

    def evac(oc, ps):
        P.v(lambda v: v.tensor_scalar(e["nscr"][:, :], ps[:, :],
                                      dcols[:, 2:3], bias[:, oc:oc + 1], ALU.mult, ALU.add))
        P.v(lambda v: v.scalar_tensor_tensor(X[:, oc * T:(oc + 1) * T], e["nscr"][:, :], 0.5,
                                             X[:, oc * T:(oc + 1) * T], ALU.mult, ALU.add))

    class TW:
        def lhsT(self, ic, o0, ow=128):
            return tern[:, ic * 1024 + o0: ic * 1024 + o0 + ow]

    mm_fm(P, TW(), XH, 0, NC, NC, evac)



def emit_l2_attn(P, packed):
    e = P.env
    psA, psB, rows = e["psA"], e["psB"], e["rows"]
    emit_ln_v2(P)
    park_x(P)
    if "selw" in P._sb:
        P.sfree("selw")
        P.env.pop("selw", None)
    XH = e["XH"]
    Q = P.sbuf("l2q", 16384, BF16)
    Kb = P.sbuf("l2k", 16384, BF16)
    V0 = P.sbuf("l2v0", 8192, BF16)
    V1 = P.sbuf("l2v1", 8192, BF16)

    wq = WFull(P, "l2wq", packed["L2w_wq"])

    def evq(oc, ps):
        P.s(lambda s: s.activation(Q[:, oc * T:(oc + 1) * T], ps[:, :], AF.Copy, scale=0.03125))
    mm_fm(P, wq, XH, 0, NC, NC, evq, wait_sem=wq.sem, wait_val=wq.ready)
    wq.free()
    wk = WFull(P, "l2wk", packed["L2w_wk"])

    def evk(oc, ps):
        P.v(lambda v: v.tensor_copy(Kb[:, oc * T:(oc + 1) * T], ps[:, :]))
    mm_fm(P, wk, XH, 0, NC, NC, evk, wait_sem=wk.sem, wait_val=wk.ready)
    wk.free()

    wv = WFull(P, "l2wv", packed["L2w_wv"])
    P.wait("tensor", wv.sem, wv.ready)
    for tt in range(17):
        if tt < 16:
            ps = psA if tt % 2 == 0 else psB
            for ic in range(NC):
                for j in range(2):
                    P.t(lambda t, tt=tt, ic=ic, j=j, ps=ps: t.matmul(
                        ps[:, j * 512:(j + 1) * 512],
                        XH[:, ic * T + tt * 128: ic * T + tt * 128 + 128],
                        wv.sb[:, ic * 1024 + j * 512: ic * 1024 + (j + 1) * 512],
                        start=(ic == 0), stop=(ic == NC - 1)))
        if tt > 0:
            psp = psA if (tt - 1) % 2 == 0 else psB
            P.v(lambda v, tt=tt, psp=psp: v.tensor_copy(V0[:, (tt - 1) * 512:tt * 512], psp[:, 0:512]))
            P.v(lambda v, tt=tt, psp=psp: v.tensor_copy(V1[:, (tt - 1) * 512:tt * 512], psp[:, 512:1024]))
        P.cut()
    wv.free()
    free_xh(P)
    Yatt = Q
    cells = P.sbuf("l2cells", 8, F32)

    sq0 = P.sbuf("l2sq0", 2048, BF16)
    sq1 = P.sbuf("l2sq1", 2048, BF16)
    for src, psrow in ((Q, psA), (Kb, psB)):
        for c in range(NC + 1):
            if c < NC:
                dst = sq0 if c % 2 == 0 else sq1
                P.s(lambda s, c=c, src=src, dst=dst: s.activation(
                    dst[:, :], src[:, c * T:(c + 1) * T], AF.Square))
            if c > 0:
                sb = sq0 if (c - 1) % 2 == 0 else sq1
                for j in range(NT):
                    P.t(lambda t, j=j, sb=sb, psrow=psrow, c=c: t.matmul(
                        psrow[0:1, j * 512:(j + 1) * 512], e["ones_cb"][:, :],
                        sb[:, j * 512:(j + 1) * 512],
                        start=(c - 1 == 0), stop=(c - 1 == NC - 1)))
            P.cut()
    P.sfree("l2sq0")
    P.sfree("l2sq1")
    qn = rows[0:1, 0:T]
    kn2 = rows[0:1, T:2 * T]
    P.s(lambda s: s.activation(qn, psA[0:1, 0:T], AF.Sqrt))
    P.s(lambda s: s.activation(kn2, psB[0:1, 0:T], AF.Copy))
    P.cut()
    P.v(lambda v: v.tensor_reduce(cells[0:1, 0:1], kn2, AX.X, ALU.max))
    P.cut()
    P.s(lambda s: s.activation(cells[0:1, 1:2], cells[0:1, 0:1], AF.Sqrt))
    P.cut()
    P.v(lambda v: v.tensor_scalar(cells[0:1, 2:3], cells[0:1, 1:2], -1.0, None, ALU.mult))
    P.cut()
    P.v(lambda v: v.tensor_scalar(kn2, qn, cells[0:1, 2:3], None, ALU.mult))
    P.cut()
    for j in range(NT):
        P.t(lambda t, j=j: t.matmul(psA[:, j * 512:(j + 1) * 512], e["ones_r32"][:, :],
                                    kn2[:, j * 512:(j + 1) * 512], start=True, stop=True))
    P.cut()
    P.v(lambda v: v.tensor_copy(e["bc0"][:, :], psA[:, :]))
    P.cut()

    P_all = P.sbuf("l2pall", 16 * 256, BF16)
    s_scr = P.sbuf("l2sscr", 512, F32)
    for qt in range(8):
        nkc = 2 * qt + 2
        q0 = qt * 256
        for j in range(nkc + 3):
            if j < nkc:
                sps = psA if j % 2 == 0 else psB
                for ic in range(NC):
                    P.t(lambda t, j=j, ic=ic, sps=sps: t.matmul(
                        sps[:, 512:768],
                        Kb[:, ic * T + j * 128: ic * T + j * 128 + 128],
                        Q[:, ic * T + q0: ic * T + q0 + 256],
                        start=(ic == 0), stop=(ic == NC - 1)))
            if 0 <= j - 1 < nkc:
                spsp = psA if (j - 1) % 2 == 0 else psB
                P.v(lambda v, j=j, spsp=spsp: v.tensor_tensor(
                    s_scr[:, ((j - 1) % 2) * 256:((j - 1) % 2) * 256 + 256],
                    spsp[:, 512:768], e["bc0"][:, q0:q0 + 256], ALU.add))
            if 0 <= j - 2 < nkc:
                kc = j - 2
                P.s(lambda s, kc=kc: s.activation(
                    P_all[:, kc * 256:(kc + 1) * 256],
                    s_scr[:, (kc % 2) * 256:(kc % 2) * 256 + 256], AF.Exp))
            if 0 <= j - 3 < nkc:
                kc = j - 3
                if kc == 2 * qt:
                    P.v(lambda v, kc=kc: v.tensor_tensor(
                        P_all[:, kc * 256:kc * 256 + 128],
                        P_all[:, kc * 256:kc * 256 + 128], e["tril"][:, :], ALU.mult))
                elif kc == 2 * qt + 1:
                    P.v(lambda v, kc=kc: v.memset(P_all[:, kc * 256:kc * 256 + 128], 0.0))
                    P.v(lambda v, kc=kc: v.tensor_tensor(
                        P_all[:, kc * 256 + 128:kc * 256 + 256],
                        P_all[:, kc * 256 + 128:kc * 256 + 256], e["tril"][:, :], ALU.mult))
            P.cut()
        for kc in range(nkc):
            P.t(lambda t, kc=kc: t.matmul(psA[0:1, 1536:1792], e["ones_cb"][:, :],
                                          P_all[:, kc * 256:(kc + 1) * 256],
                                          start=(kc == 0), stop=(kc == nkc - 1)))
        P.cut()
        lrow = rows[0:1, 2048:2304]
        P.s(lambda s: s.activation(lrow, psA[0:1, 1536:1792], AF.Copy))
        P.cut()
        P.v(lambda v: v.reciprocal(lrow, lrow))
        P.cut()
        P.t(lambda t: t.matmul(psB[:, 768:1024], e["ones_r32"][:, :], lrow, start=True, stop=True))
        P.cut()
        P.v(lambda v: v.tensor_copy(e["bc1"][:, 0:256], psB[:, 768:1024]))
        P.cut()
        for z in range(9):
            if z < 8:
                ps = psA if z % 2 == 0 else psB
                for kc in range(nkc):
                    Vh = V0 if z < 4 else V1
                    zz = z if z < 4 else z - 4
                    P.t(lambda t, zz=zz, kc=kc, ps=ps, Vh=Vh: t.matmul(
                        ps[:, 0:256],
                        Vh[:, kc * 512 + zz * 128: kc * 512 + zz * 128 + 128],
                        P_all[:, kc * 256:(kc + 1) * 256],
                        start=(kc == 0), stop=(kc == nkc - 1)))
            if z > 0:
                psp = psA if (z - 1) % 2 == 0 else psB
                P.v(lambda v, z=z, psp=psp: v.tensor_tensor(
                    Yatt[:, (z - 1) * T + q0:(z - 1) * T + q0 + 256],
                    psp[:, 0:256], e["bc1"][:, 0:256], ALU.mult))
            P.cut()
    P.sfree("l2pall")
    P.sfree("l2sscr")
    P.sfree("l2k")
    P.sfree("l2v0")
    P.sfree("l2v1")
    P.sfree("l2cells")

    unpark_x(P)
    X = e["X"]
    wo = WFull(P, "l2wo", packed["L2w_wo"])

    def evo(oc, ps):
        P.v(lambda v, oc=oc: v.tensor_tensor(X[:, oc * T:(oc + 1) * T], ps[:, :],
                                             X[:, oc * T:(oc + 1) * T], ALU.add))
    mm_fm(P, wo, Yatt, 0, NC, NC, evo, wait_sem=wo.sem, wait_val=wo.ready)
    wo.free()
    P.sfree("l2q")


SUBLAYER_SET = ["l0_ffn_a", "l0_attn", "l0_ffn_b", "l1_ffn_a", "l1_attn", "l1_ffn_b",
                "l2_ffn_a", "l2_attn", "l2_ffn_b", "l3_ffn_a", "l3_attn", "l3_ffn_b"]


def emit_sublayer(P, name, packed):
    if name == "l2_ffn_a":
        emit_l2_ffn(P, packed, first=True)
    elif name == "l2_ffn_b":
        emit_l2_ffn(P, packed, first="l2_ffn_a" not in P.env.get("_done", ()))
    elif name == "l2_attn":
        emit_l2_attn(P, packed)
    elif name in ("l1_ffn_a", "l1_ffn_b"):
        emit_l1_ffn(P, packed)
    elif name in ("l3_ffn_a", "l3_ffn_b"):
        emit_l3_ffn(P, packed)
    elif name == "l1_attn":
        emit_l1_attn(P, packed)
    elif name == "l3_attn":
        emit_l3_attn(P, packed)
    elif name in ("l0_ffn_a", "l0_ffn_b"):
        emit_l0_ffn(P, packed)
    elif name == "l0_attn":
        emit_l0_attn(P, packed)
    else:
        raise NotImplementedError(name)


_LAYER_PERSISTENTS = {
    "l0_ffn_b": ["l0rw", "l0rb"],
    "l0_attn": ["l0lam", "l0al", "l0cl", "l0c1c", "l0blk", "l0gn"],
    "l1_ffn_b": ["l1scales", "l1sh2"],
    "l1_attn": ["l1abias"],
    "l2_ffn_b": ["l2tern", "l2bias", "l2dcols"],
    "l3_attn": ["l3abias"],
    "l3_ffn_b": ["l3rw", "l3rb", "l3b1", "l3b2"],
}


def build_program(names, packed, dbg=False):
    P = Prog()
    P.env["_dbg"] = dbg
    setup_env(P, packed)
    done = []
    for nm in names:
        P.env["_done"] = tuple(done)
        emit_sublayer(P, nm, packed)
        done.append(nm)
        later_layers = {x.split("_")[0] for x in names[len(done):]}
        for k, bufs in _LAYER_PERSISTENTS.items():
            if k.split("_")[0] not in later_layers:
                for b in bufs:
                    if b in P._sb:
                        P.sfree(b)
                        P.env.pop(b, None)
    store_x(P)
    P.finish()
    return P


def run_launch(P, xs):
    in_maps = []
    for b in range(8):
        m = dict(P.inmap)
        m["xin"] = xs[b]
        in_maps.append(m)
    res = run_bass_kernel_spmd(P.nc, in_maps, core_ids=list(range(8)))
    return [r["xout"] for r in res.results]


LAUNCH_GROUPS = [list(SUBLAYER_SET)]


def kernel(**inputs):
    packed = pack_all(inputs)
    x = np.asarray(inputs["L0_attn_x"])
    xs = []
    for b in range(8):
        xf = np.ascontiguousarray(x[b].T.astype(np.float32))
        xs.append(np.ascontiguousarray(xf.reshape(NC, 128, T).transpose(1, 0, 2).reshape(128, NC * T)))
    for grp in LAUNCH_GROUPS:
        packed["xin0"] = xs[0]
        P = build_program(grp, packed)
        xs = run_launch(P, xs)
    outs = []
    for b in range(8):
        o = xs[b].reshape(128, NC, T).transpose(2, 1, 0).reshape(T, D)
        outs.append(o)
    return np.stack(outs).astype(np.float32)
